# version 12
# baseline (speedup 1.0000x reference)
import math
import contextlib
import os
import numpy as np
import ml_dtypes
import concourse.bass as bass
import concourse.mybir as mybir
from concourse.bass_utils import run_bass_kernel_spmd

F32 = mybir.dt.float32
BF16 = mybir.dt.bfloat16
I32 = mybir.dt.int32
U8 = mybir.dt.uint8
ACT = mybir.ActivationFunctionType
ALU = mybir.AluOpType
AX = mybir.AxisListType

B = 2
S = 4096
D = 2048
T = 1024
NT = 8
KC = 16
HID = 5632
HC = 44
DL = 512
NTS = 16
INV_SQRT_DH = 1.0 / math.sqrt(128.0)
C_BOUND = 40.0 * math.sqrt(128.0)


def build_nc(stop_after="all"):
    nc = bass.Bass("TRN2", target_bir_lowering=False, debug=False)

    def din(name, shape, dt=F32):
        return nc.dram_tensor(name, shape, dt, kind="ExternalInput").ap()

    def dint(name, shape, dt=BF16):
        return nc.dram_tensor(name, shape, dt, kind="Internal").ap()

    x_in = din("x", [T, D])
    wqkv_my = din("wqkv_my", [D, 768])
    wout0_my = din("wout0_my", [256, D])
    gain2 = din("gain2", [128, 2])
    c0tm = din("c0tm", [128, 2048])
    s0tm = din("s0tm", [128, 2048])
    ident_in = din("ident", [128, 128])
    onesM_in = din("onesM", [128, 128])
    onescol_in = din("onescol", [128, 64])
    epsv_in = din("epsv", [128, 4])
    maskT_in = din("maskT", [128, 4 * 512])
    wgT_in = din("wgT", [D, HID], U8)
    wuT_in = din("wuT", [D, HID], U8)
    wdT_in = din("wdT", [HID, D], U8)
    sg_in = din("sg_pp", [128, HC])
    su_in = din("su_pp", [128, HC])
    sd_in = din("sd_pp", [128, KC])
    wq1_my = din("wq1_my", [D, 256])
    wqbtm_in = din("wqbtm", [128, 1024])
    wkvaT_in = din("wkvaT", [D, DL])
    normwtm_in = din("normwtm", [128, 512])
    wkvb_my = din("wkvb_my", [DL, 512])
    wout1_my = din("wout1_my", [256, D])
    outb_in = din("outb_bc", [128, D])
    c2tm_in = din("c2tm", [128, 256])
    s2tm_in = din("s2tm", [128, 256])
    wgA_in = din("wgA", [D, 520])
    bwT_in = din("bwT", [512, D])
    out_t = nc.dram_tensor("out", [T, D], F32, kind="ExternalOutput").ap()

    ag_in_d = [dint("ag0_in", [D, T]), dint("ag1_in", [D, T])]
    ag_out_d = [dint("ag0_out", [8 * D, T]), dint("ag1_out", [8 * D, T])]
    agl_in = dint("agl_in", [DL, T])
    agl_out = dint("agl_out", [8 * DL, T])
    rs_in_d = [dint("rs0_in", [8 * T, D]), dint("rs1_in", [8 * T, D])]
    rs_out_d = [dint("rs0_out", [T, D]), dint("rs1_out", [T, D])]
    sc_q = [dint("scq0", [256, 8 * T]), dint("scq1", [256, 8 * T])]
    sc_k = [dint("sck0", [256, 8 * T]), dint("sck1", [256, 8 * T])]
    sc_v = [dint("scv0", [2 * 8 * T, 128]), dint("scv1", [2 * 8 * T, 128])]

    es = contextlib.ExitStack()
    with es:
        sb = lambda n, s, d: es.enter_context(nc.sbuf_tensor(n, s, d))
        x_sb = sb("x_sb", [128, NT * D], F32)
        ident_sb = sb("ident_sb", [128, 128], BF16)
        onesM = sb("onesM_sb", [128, 128], BF16)
        gain_sb = sb("gain_sb", [128, 2], F32)
        nag_sb = sb("nag_sb", [128, 2], F32)
        eps_sb = sb("eps_sb", [128, 4], F32)
        sg_sb = sb("sg_sb", [128, HC], F32)
        su_sb = sb("su_sb", [128, HC], F32)
        sd_sb = sb("sd_sb", [128, KC], F32)

        class Ctr:
            def __init__(self, h):
                self.h = h
                self.n = 0

        sem = {k: Ctr(es.enter_context(nc.semaphore(k)))
               for k in ("sW", "sG", "sC", "sT", "sS", "sV",
                         "bW", "bG", "bT", "bS", "bV",
                         "pS0", "pS1", "sX0", "sX1", "sQ0", "sQ1",
                         "sU0", "sU1", "sL0", "sL1", "sP0", "sP1",
                         "sF0", "sF1", "sFu")}
        sW, sG, sC = sem["sW"], sem["sG"], sem["sC"]
        sT, sS, sV = sem["sT"], sem["sS"], sem["sV"]
        pS = [sem["pS0"], sem["pS1"]]
        sX = [sem["sX0"], sem["sX1"]]
        sQ = [sem["sQ0"], sem["sQ1"]]
        sU = [sem["sU0"], sem["sU1"]]
        sL = [sem["sL0"], sem["sL1"]]
        sP = [sem["sP0"], sem["sP1"]]
        sF = [sem["sF0"], sem["sF1"]]
        sFu = sem["sFu"]
        SYNC_DMA_SEMS = [sW] + pS + sX + sQ + sU + sL + sP
        GP_DMA_SEMS = [sG] + sF + [sFu]
        BARS = [sem[k] for k in ("bW", "bG", "bT", "bS", "bV")]
        ENG = ("sync", "gpsimd", "tensor", "scalar", "vector")

        block = es.enter_context(nc.Block())

        _DATA_OPS = frozenset((
            "matmul", "transpose", "tensor_tensor", "tensor_copy",
            "tensor_scalar", "activation", "reciprocal", "reduce_sum",
            "reduce_max", "scalar_tensor_tensor"))

        class _IW:
            def __init__(self, i, o):
                self.i = i
                self.o = o

            def then_inc(self, sem, value=1):
                self.i.then_inc(sem, value)
                self.o._has_inc = True
                return self

            def __getattr__(self, k):
                return getattr(self.i, k)

        class _EP:
            def __init__(self, e, elide_solo=False):
                self._e = e
                self._last = None
                self._has_inc = False
                self._pend = False
                self._elide = elide_solo

            def _flush(self):
                if self._pend:
                    if not self._elide:
                        self._e.drain()
                    self._pend = False

            def drain(self, *a, **k):
                self._pend = True
                return None

            def nop(self, *a, **k):
                if (self._pend and self._last is not None
                        and not self._has_inc):
                    self._pend = False
                    return _IW(self._last, self)
                self._flush()
                i = self._e.nop()
                self._last = i
                self._has_inc = False
                return _IW(i, self)

            def sem_inc(self, sem, val=1):
                self._pend = False
                self._e.drain()
                self._last = None
                return self._e.sem_inc(sem, val)

            def wait_ge(self, *a, **kw):
                self._flush()
                return self._e.wait_ge(*a, **kw)

            def __getattr__(self, k):
                f = getattr(self._e, k)
                if k in _DATA_OPS:
                    def g(*a, _f=f, **kw):
                        self._flush()
                        i = _f(*a, **kw)
                        self._last = i
                        self._has_inc = False
                        return _IW(i, self)
                    return g
                if callable(f):
                    def h(*a, _f=f, **kw):
                        self._flush()
                        return _f(*a, **kw)
                    return h
                return f

        _proxies = {}

        def sec(name, fn):
            if name in ("tensor", "scalar", "vector"):
                def wrapped(e, fn=fn, name=name):
                    p = _proxies.get(name)
                    if p is None:
                        p = _EP(e, elide_solo=name == "scalar")
                        _proxies[name] = p
                    elif p._e is not e:
                        p._e = e
                    fn(p)
                getattr(block, name)(wrapped)
            else:
                getattr(block, name)(fn)

        def barrier():
            def _w(e):
                for s_ in SYNC_DMA_SEMS:
                    e.wait_ge(s_.h, s_.n)
                e.sem_inc(BARS[0].h, 1)
            def _g(e):
                for s_ in GP_DMA_SEMS:
                    e.wait_ge(s_.h, s_.n)
                e.wait_ge(sC.h, sC.n)
                e.sem_inc(BARS[1].h, 1)
            sec("sync", _w)
            sec("gpsimd", _g)
            sec("tensor", lambda e: e.sem_inc(BARS[2].h, 1))
            sec("scalar", lambda e: e.sem_inc(BARS[3].h, 1))
            sec("vector", lambda e: e.sem_inc(BARS[4].h, 1))
            for b_ in BARS:
                b_.n += 1
            tg = [(b_.h, b_.n) for b_ in BARS]
            for nm in ENG:
                def _wt(e, tg=tg):
                    for h, v in tg:
                        e.wait_ge(h, v)
                sec(nm, _wt)

        def _p0w(e):
            e.dma_start(out=x_sb[:].rearrange("p (t d) -> p t d", d=D),
                        in_=x_in.rearrange("(t p) d -> p t d", p=128)
                        ).then_inc(sW.h, 16)
            sW.n += 16
            for dst, src in ((gain_sb, gain2), (eps_sb, epsv_in),
                             (sg_sb, sg_in), (su_sb, su_in),
                             (sd_sb, sd_in)):
                e.dma_start(out=dst[:], in_=src).then_inc(sW.h, 16)
                sW.n += 16
        sec("sync", _p0w)
        w_small = sW.n

        def _p0g(e):
            for dst, src in ((ident_sb, ident_in), (onesM, onesM_in)):
                e.dma_start(out=dst[:], in_=src).then_inc(sG.h, 16)
                sG.n += 16
        sec("gpsimd", _p0g)

        def _p0s(e):
            e.wait_ge(sW.h, w_small)
            e.activation(out=nag_sb[:], in_=gain_sb[:], func=ACT.Abs
                         ).then_inc(sS.h, 1)
            sS.n += 1
        sec("scalar", _p0s)
        s_abs = sS.n

        def _p0v(e):
            e.wait_ge(sS.h, s_abs)
            e.tensor_scalar(out=nag_sb[:], in0=nag_sb[:], scalar1=-1.0,
                            scalar2=None, op0=ALU.mult)
            e.drain()
        sec("vector", _p0v)
        barrier()

        def emit_ln(tag, ag_ap, xT_sb):
            ph = contextlib.ExitStack()
            with ph:
                lsb = lambda n, s, d: ph.enter_context(
                    nc.sbuf_tensor(tag + n, s, d))
                xh = lsb("xh", [128, NT * D], BF16)
                sqx = lsb("sqx", [128, 2 * D], BF16)
                st8 = lsb("st8", [128, 8], F32)
                sq8 = lsb("sq8", [128, 8], F32)
                m8 = lsb("m8", [128, 8], F32)
                v8 = lsb("v8", [128, 8], F32)
                r8 = lsb("r8", [128, 8], F32)
                r8i = lsb("r8i", [128, 8], F32)
                stg = lsb("stg", [128, 2 * T], BF16) if ag_ap is not None else None
                ptr = ph.enter_context(
                    nc.psum_tensor(tag + "ptr", [128, 2 * T], BF16))

                def _s(e):
                    for t in range(NT):
                        e.activation(out=sqx[:, (t % 2) * D:(t % 2) * D + D],
                                     in_=x_sb[:, t * D:(t + 1) * D],
                                     func=ACT.Square,
                                     accum_out=sq8[:, t:t + 1])
                        e.drain()
                    e.nop().then_inc(sS.h, 1)
                    sS.n += 1
                sec("scalar", _s)
                s_sq = sS.n

                def _v(e):
                    for t in range(NT):
                        e.reduce_sum(out=st8[:, t:t + 1],
                                     in_=x_sb[:, t * D:(t + 1) * D], axis=AX.X)
                    e.drain()
                    e.tensor_scalar(out=m8[:], in0=st8[:], scalar1=1.0 / D,
                                    scalar2=None, op0=ALU.mult)
                    e.drain()
                    e.tensor_tensor(out=v8[:], in0=m8[:], in1=m8[:],
                                    op=ALU.mult)
                    e.drain()
                    e.wait_ge(sS.h, s_sq)
                    e.scalar_tensor_tensor(out=v8[:], in0=sq8[:],
                                           scalar=1.0 / D, in1=v8[:],
                                           op0=ALU.mult, op1=ALU.subtract)
                    e.drain()
                    e.nop().then_inc(sV.h, 1)
                    sV.n += 1
                sec("vector", _v)
                v_var = sV.n

                def _s2(e):
                    e.wait_ge(sV.h, v_var)
                    e.activation(out=r8[:], in_=v8[:], func=ACT.Sqrt,
                                 bias=eps_sb[:, 0:1]).then_inc(sS.h, 1)
                    sS.n += 1
                sec("scalar", _s2)
                s_r = sS.n

                def _v2(e):
                    e.wait_ge(sS.h, s_r)
                    e.reciprocal(out=r8i[:], in_=r8[:])
                    e.drain()
                    for t in range(NT):
                        e.tensor_scalar(out=xh[:, t * D:(t + 1) * D],
                                        in0=x_sb[:, t * D:(t + 1) * D],
                                        scalar1=m8[:, t:t + 1],
                                        scalar2=r8i[:, t:t + 1],
                                        op0=ALU.subtract, op1=ALU.mult)
                    e.drain()
                    e.nop().then_inc(sV.h, 1)
                    sV.n += 1
                sec("vector", _v2)
                v_xh = sV.n

                evac_done = [0, 0]
                dma_done = [0, 0]
                for c in range(KC):
                    half = (c % 2) * T

                    def _t(e, c=c, half=half):
                        if c == 0:
                            e.wait_ge(sV.h, v_xh)
                        if c >= 2:
                            e.wait_ge(sV.h, evac_done[c % 2])
                        for t in range(NT):
                            e.transpose(
                                ptr[:, half + t * 128: half + t * 128 + 128],
                                xh[:, t * D + c * 128: t * D + c * 128 + 128],
                                ident_sb[:])
                        e.drain()
                        e.nop().then_inc(sT.h, 1)
                        sT.n += 1
                    sec("tensor", _t)
                    t_tr = sT.n

                    def _ve(e, c=c, half=half, t_tr=t_tr):
                        e.wait_ge(sT.h, t_tr)
                        if ag_ap is not None and c >= 2:
                            e.wait_ge(pS[c % 2].h, dma_done[c % 2])
                        if xT_sb is not None:
                            e.tensor_copy(out=xT_sb[:, c * T:(c + 1) * T],
                                          in_=ptr[:, half:half + T])
                        if ag_ap is not None:
                            e.tensor_copy(out=stg[:, half:half + T],
                                          in_=ptr[:, half:half + T])
                        e.drain()
                        e.nop().then_inc(sV.h, 1)
                        sV.n += 1
                        evac_done[c % 2] = sV.n
                    sec("vector", _ve)
                    v_e = sV.n

                    if ag_ap is not None:
                        def _w(e, c=c, half=half, v_e=v_e):
                            e.wait_ge(sV.h, v_e)
                            e.dma_start(out=ag_ap[c * 128:(c + 1) * 128, :],
                                        in_=stg[:, half:half + T]
                                        ).then_inc(pS[c % 2].h, 16)
                            pS[c % 2].n += 16
                            dma_done[c % 2] = pS[c % 2].n
                        sec("sync", _w)

        def emit_collective(kind, op, in_ap, out_ap):
            wvs = [(s_.h, s_.n) for s_ in SYNC_DMA_SEMS]

            def _g(e):
                for h_, v_ in wvs:
                    e.wait_ge(h_, v_)
                e.collective_compute(kind, op,
                                     replica_groups=[[0, 1, 2, 3, 4, 5, 6, 7]],
                                     ins=[in_ap], outs=[out_ap]
                                     ).then_inc(sC.h, 1)
            sec("gpsimd", _g)
            sC.n += 1

        def emit_qkv(layer):
            P = f"q{layer}"
            ph = contextlib.ExitStack()
            with ph:
                lsb = lambda n, s, d: ph.enter_context(
                    nc.sbuf_tensor(P + n, s, d))
                psb = lambda n, s, d: ph.enter_context(
                    nc.psum_tensor(P + n, s, d))
                if layer == 0:
                    W = lsb("W", [128, KC * 768], BF16)
                    c0_sb = lsb("c0", [128, 2048], F32)
                    s0_sb = lsb("s0", [128, 2048], F32)
                else:
                    W = lsb("W", [128, KC * 256], BF16)
                    Wkv = lsb("Wkv", [128, 4 * 512], BF16)
                    lg = lsb("lg", [128, 2 * 4 * 512], BF16)
                    wqbtm = lsb("wqb", [128, 1024], F32)
                    c2_sb = lsb("c2", [128, 256], F32)
                    s2_sb = lsb("s2", [128, 256], F32)
                xg = lsb("xg", [128, 2 * KC * 512], BF16)
                raw = lsb("raw", [128, 512], BF16)
                sq = lsb("sq", [128, 512], BF16)
                ss4 = lsb("ss4", [128, 4], F32)
                sd4 = lsb("sd4", [128, 4], F32)
                ri4 = lsb("ri4", [128, 4], F32)
                qn = lsb("qn", [128, 512], BF16)
                ro = lsb("ro", [128, 512], BF16)
                tA = lsb("tA", [128, 256], F32)
                tB = lsb("tB", [128, 256], F32)
                qo = lsb("qo", [128, 2 * 4 * 512], BF16)
                vf = lsb("vf", [128, 2 * 2 * 512], BF16)
                vt = lsb("vt", [128, 2 * 2 * 512], BF16)
                pq = [psb(f"pq{i}", [128, 512], F32) for i in range(2)]
                ptm = psb("ptm", [128, 512], BF16)
                pfm = psb("pfm", [128, 512], BF16)
                pvt = [psb(f"pvt{i}", [128, 512], BF16) for i in range(2)]

                def _g(e):
                    if layer == 0:
                        e.dma_start(
                            out=W[:].rearrange("p (k o) -> p k o", o=768),
                            in_=wqkv_my.rearrange("(k p) o -> p k o", p=128)
                        ).then_inc(sG.h, 16)
                        sG.n += 16
                    else:
                        e.dma_start(
                            out=W[:].rearrange("p (k o) -> p k o", o=256),
                            in_=wq1_my.rearrange("(k p) o -> p k o", p=128)
                        ).then_inc(sG.h, 16)
                        e.dma_start(
                            out=Wkv[:].rearrange("p (k o) -> p k o", o=512),
                            in_=wkvb_my.rearrange("(k p) o -> p k o", p=128)
                        ).then_inc(sG.h, 16)
                        sG.n += 32
                sec("gpsimd", _g)
                g_w = sG.n

                def _w0(e):
                    if layer == 0:
                        for dst, src in ((c0_sb, c0tm), (s0_sb, s0tm)):
                            e.dma_start(out=dst[:], in_=src).then_inc(sW.h, 16)
                            sW.n += 16
                    else:
                        for dst, src in ((wqbtm, wqbtm_in), (c2_sb, c2tm_in),
                                         (s2_sb, s2tm_in)):
                            e.dma_start(out=dst[:], in_=src).then_inc(sW.h, 16)
                            sW.n += 16
                sec("sync", _w0)
                w_tabs = sW.n

                ag_out = ag_out_d[layer]
                scq, sck, scv = sc_q[layer], sc_k[layer], sc_v[layer]
                ocw = 768 if layer == 0 else 256

                xg_ready = [0, 0]
                lg_ready = [0, 0]
                xg_free = [0, 0]
                qo_free = [0, 0]
                vt_free = [0, 0]
                pvt_free = [0, 0]
                raw_free = [0]
                pfm_free = [0]
                evacq = [0, 0]
                mm_done = [0] * 6

                for ts in range(NTS):
                    par = ts % 2
                    xoff = par * KC * 512

                    def _w(e, ts=ts, par=par, xoff=xoff):
                        e.wait_ge(sT.h, xg_free[par])
                        for kc in range(KC):
                            e.dma_start(
                                out=xg[:, xoff + kc * 512:
                                       xoff + kc * 512 + 512],
                                in_=ag_out[2048 * (ts // 2) + kc * 128:
                                           2048 * (ts // 2) + kc * 128 + 128,
                                           512 * (ts % 2):
                                           512 * (ts % 2) + 512]
                            ).then_inc(sX[par].h, 16)
                            sX[par].n += 16
                        if layer == 1:
                            for lc in range(4):
                                e.dma_start(
                                    out=lg[:, par * 2048 + lc * 512:
                                           par * 2048 + lc * 512 + 512],
                                    in_=agl_out[512 * (ts // 2) + lc * 128:
                                                512 * (ts // 2) + lc * 128 + 128,
                                                512 * (ts % 2):
                                                512 * (ts % 2) + 512]
                                ).then_inc(sX[par].h, 16)
                                sX[par].n += 16
                        xg_ready[par] = sX[par].n
                        lg_ready[par] = sX[par].n
                    sec("sync", _w)

                    if os.environ.get("QSKIP") == "5":
                        def _t5(e, par=par):
                            e.wait_ge(sW.h, xg_ready[par])
                            e.drain()
                            e.nop().then_inc(sT.h, 1)
                            sT.n += 1
                        sec("tensor", _t5)
                        xg_free[par] = sT.n
                        continue
                    for oc in range(6):
                        def _t(e, ts=ts, oc=oc, par=par, xoff=xoff):
                            e.wait_ge(sG.h, g_w)
                            e.wait_ge(sW.h, w_tabs)
                            e.wait_ge(sX[par].h, xg_ready[par])
                            e.wait_ge(sV.h, evacq[oc % 2])
                            if layer == 0 or oc < 2:
                                for kc in range(KC):
                                    e.matmul(
                                        pq[oc % 2][:],
                                        W[:, kc * ocw + oc * 128:
                                          kc * ocw + oc * 128 + 128],
                                        xg[:, xoff + kc * 512:
                                           xoff + kc * 512 + 512],
                                        start=(kc == 0), stop=(kc == KC - 1))
                            else:
                                kvoc = oc - 2
                                for lc in range(4):
                                    e.matmul(
                                        pq[oc % 2][:],
                                        Wkv[:, lc * 512 + kvoc * 128:
                                            lc * 512 + kvoc * 128 + 128],
                                        lg[:, par * 2048 + lc * 512:
                                           par * 2048 + lc * 512 + 512],
                                        start=(lc == 0), stop=(lc == 3))
                            e.drain()
                            e.nop().then_inc(sT.h, 1)
                            sT.n += 1
                        sec("tensor", _t)
                        mm_done[oc] = sT.n
                        if oc == 5:
                            xg_free[par] = sT.n
                        if os.environ.get("QSKIP") == "4":
                            def _v4(e, oc=oc):
                                e.wait_ge(sT.h, mm_done[oc])
                                e.drain()
                                e.nop().then_inc(sV.h, 1)
                                sV.n += 1
                                evacq[oc % 2] = sV.n
                            sec("vector", _v4)
                            continue

                        if oc < 4:
                            hh = oc % 2
                            is_q = oc < 2
                            do_rope = (layer == 1) or (hh == 0)
                            if os.environ.get("QSKIP") in ("1", "3", "6"):
                                qoff = par * 2048 + (oc % 4) * 512

                                def _vz(e, oc=oc, par=par, qoff=qoff):
                                    e.wait_ge(sT.h, mm_done[oc])
                                    e.wait_ge(sQ[par].h, qo_free[par])
                                    e.tensor_copy(out=qo[:, qoff:qoff + 512],
                                                  in_=pq[oc % 2][:])
                                    e.drain()
                                    e.nop().then_inc(sV.h, 1)
                                    sV.n += 1
                                    evacq[oc % 2] = sV.n
                                sec("vector", _vz)
                                v_qo = sV.n
                                dst = scq if is_q else sck
                                if os.environ.get("QSKIP") == "6":
                                    qo_free[par] = 0
                                    continue

                                def _wz(e, ts=ts, oc=oc, hh=hh, dst=dst,
                                        par=par, qoff=qoff, v_qo=v_qo):
                                    e.wait_ge(sV.h, v_qo)
                                    e.dma_start(
                                        out=dst[hh * 128:(hh + 1) * 128,
                                                ts * 512: ts * 512 + 512],
                                        in_=qo[:, qoff:qoff + 512]
                                    ).then_inc(sQ[par].h, 16)
                                    sQ[par].n += 16
                                    if oc == 3:
                                        qo_free[par] = sQ[par].n
                                sec("sync", _wz)
                                continue

                            def _v(e, oc=oc):
                                e.wait_ge(sT.h, mm_done[oc])
                                e.wait_ge(sT.h, raw_free[0])
                                e.tensor_copy(out=raw[:], in_=pq[oc % 2][:])
                                e.drain()
                                e.nop().then_inc(sV.h, 1)
                                sV.n += 1
                                evacq[oc % 2] = sV.n
                            sec("vector", _v)
                            v_raw = sV.n

                            def _t2(e, v_raw=v_raw):
                                e.wait_ge(sV.h, v_raw)
                                for q in range(4):
                                    e.transpose(
                                        ptm[:, q * 128: q * 128 + 128],
                                        raw[:, q * 128: q * 128 + 128],
                                        ident_sb[:])
                                e.drain()
                                e.nop().then_inc(sT.h, 1)
                                sT.n += 1
                            sec("tensor", _t2)
                            t_tm = sT.n
                            raw_free[0] = t_tm

                            if layer == 0:
                                def _vn(e, t_tm=t_tm):
                                    e.wait_ge(sT.h, t_tm)
                                    e.wait_ge(sS.h, sS.n)
                                    e.tensor_copy(out=qn[:], in_=ptm[:])
                                    e.drain()
                                    e.tensor_tensor(out=sq[:], in0=qn[:],
                                                    in1=qn[:], op=ALU.mult)
                                    e.drain()
                                    e.reduce_sum(
                                        out=ss4[:],
                                        in_=sq[:].rearrange(
                                            "p (q d) -> p q d", d=128),
                                        axis=AX.X)
                                    e.drain()
                                    e.nop().then_inc(sV.h, 1)
                                    sV.n += 1
                                sec("vector", _vn)
                                v_ss = sV.n

                                def _sn(e, v_ss=v_ss):
                                    e.wait_ge(sV.h, v_ss)
                                    e.activation(out=sd4[:], in_=ss4[:],
                                                 func=ACT.Sqrt,
                                                 bias=eps_sb[:, 1:2]
                                                 ).then_inc(sS.h, 1)
                                    e.drain()
                                    sS.n += 1
                                sec("scalar", _sn)
                                s_n = sS.n

                                def _vap(e, ts=ts, do_rope=do_rope, s_n=s_n):
                                    e.wait_ge(sS.h, s_n)
                                    e.reciprocal(out=ri4[:], in_=sd4[:])
                                    e.drain()
                                    for q in range(4):
                                        e.tensor_scalar(
                                            out=qn[:, q * 128: q * 128 + 128],
                                            in0=qn[:, q * 128: q * 128 + 128],
                                            scalar1=ri4[:, q:q + 1],
                                            scalar2=None, op0=ALU.mult)
                                    e.drain()
                                    if do_rope:
                                        po_ = (ts % 8) * 256
                                        cv = c0_sb[:, po_: po_ + 256] \
                                            .rearrange("p (q d) -> p q d", d=64)
                                        sv = s0_sb[:, po_: po_ + 256] \
                                            .rearrange("p (q d) -> p q d", d=64)
                                        qv = qn[:].rearrange(
                                            "p (q h d) -> p q h d", h=2, d=64)
                                        rv = ro[:].rearrange(
                                            "p (q h d) -> p q h d", h=2, d=64)
                                        tAv = tA[:].rearrange(
                                            "p (q d) -> p q d", d=64)
                                        tBv = tB[:].rearrange(
                                            "p (q d) -> p q d", d=64)
                                        a = qv[:, :, 0, :]
                                        bq = qv[:, :, 1, :]
                                        e.tensor_tensor(out=tAv, in0=a,
                                                        in1=cv, op=ALU.mult)
                                        e.tensor_tensor(out=tBv, in0=bq,
                                                        in1=sv, op=ALU.mult)
                                        e.drain()
                                        e.tensor_tensor(out=rv[:, :, 0, :],
                                                        in0=tAv, in1=tBv,
                                                        op=ALU.subtract)
                                        e.drain()
                                        e.tensor_tensor(out=tAv, in0=a,
                                                        in1=sv, op=ALU.mult)
                                        e.tensor_tensor(out=tBv, in0=bq,
                                                        in1=cv, op=ALU.mult)
                                        e.drain()
                                        e.tensor_tensor(out=rv[:, :, 1, :],
                                                        in0=tAv, in1=tBv,
                                                        op=ALU.add)
                                        e.drain()
                                    e.nop().then_inc(sV.h, 1)
                                    sV.n += 1
                                sec("vector", _vap)
                                v_ro = sV.n
                                src_tile = ro if do_rope else qn
                            else:
                                def _vap(e, hh=hh, is_q=is_q, t_tm=t_tm):
                                    e.wait_ge(sT.h, t_tm)
                                    if is_q:
                                        e.tensor_tensor(
                                            out=qn[:], in0=ptm[:],
                                            in1=wqbtm[:, hh * 512:
                                                      hh * 512 + 512],
                                            op=ALU.add)
                                    else:
                                        e.tensor_copy(out=qn[:], in_=ptm[:])
                                    e.drain()
                                    qlo = qn[:].rearrange(
                                        "p (q h d) -> p q h d", h=2, d=64)
                                    rlo = ro[:].rearrange(
                                        "p (q h d) -> p q h d", h=2, d=64)
                                    e.tensor_copy(out=rlo[:, :, 0, :],
                                                  in_=qlo[:, :, 0, :])
                                    e.drain()
                                    qv = qn[:].rearrange(
                                        "p (q h j t) -> p q h j t",
                                        h=2, j=32, t=2)
                                    rv = ro[:].rearrange(
                                        "p (q h j t) -> p q h j t",
                                        h=2, j=32, t=2)
                                    a = qv[:, :, 1, :, 0]
                                    bq = qv[:, :, 1, :, 1]
                                    cv = c2_sb[:, hh * 128: hh * 128 + 128] \
                                        .rearrange("p (q j) -> p q j", j=32)
                                    sv = s2_sb[:, hh * 128: hh * 128 + 128] \
                                        .rearrange("p (q j) -> p q j", j=32)
                                    tAv = tA[:, 0:128].rearrange(
                                        "p (q j) -> p q j", j=32)
                                    tBv = tB[:, 0:128].rearrange(
                                        "p (q j) -> p q j", j=32)
                                    e.tensor_tensor(out=tAv, in0=a, in1=cv,
                                                    op=ALU.mult)
                                    e.tensor_tensor(out=tBv, in0=bq, in1=sv,
                                                    op=ALU.mult)
                                    e.drain()
                                    e.tensor_tensor(out=rv[:, :, 1, :, 0],
                                                    in0=tAv, in1=tBv,
                                                    op=ALU.subtract)
                                    e.drain()
                                    e.tensor_tensor(out=tAv, in0=a, in1=sv,
                                                    op=ALU.mult)
                                    e.tensor_tensor(out=tBv, in0=bq, in1=cv,
                                                    op=ALU.mult)
                                    e.drain()
                                    e.tensor_tensor(out=rv[:, :, 1, :, 1],
                                                    in0=tAv, in1=tBv,
                                                    op=ALU.add)
                                    e.drain()
                                    e.nop().then_inc(sV.h, 1)
                                    sV.n += 1
                                sec("vector", _vap)
                                v_ro = sV.n
                                src_tile = ro

                            def _t3(e, v_ro=v_ro, src_tile=src_tile):
                                e.wait_ge(sV.h, v_ro)
                                e.wait_ge(sV.h, pfm_free[0])
                                for q in range(4):
                                    e.transpose(
                                        pfm[:, q * 128: q * 128 + 128],
                                        src_tile[:, q * 128: q * 128 + 128],
                                        ident_sb[:])
                                e.drain()
                                e.nop().then_inc(sT.h, 1)
                                sT.n += 1
                            sec("tensor", _t3)
                            t_fm = sT.n

                            qoff = par * 2048 + (oc % 4) * 512

                            def _vq(e, par=par, qoff=qoff, t_fm=t_fm):
                                e.wait_ge(sT.h, t_fm)
                                e.wait_ge(sQ[par].h, qo_free[par])
                                e.tensor_copy(out=qo[:, qoff:qoff + 512],
                                              in_=pfm[:])
                                e.drain()
                                e.nop().then_inc(sV.h, 1)
                                sV.n += 1
                                pfm_free[0] = sV.n
                            sec("vector", _vq)
                            v_qo = sV.n

                            dst = scq if is_q else sck

                            def _wq(e, ts=ts, oc=oc, hh=hh, dst=dst, par=par,
                                    qoff=qoff, v_qo=v_qo):
                                e.wait_ge(sV.h, v_qo)
                                e.dma_start(
                                    out=dst[hh * 128:(hh + 1) * 128,
                                            ts * 512: ts * 512 + 512],
                                    in_=qo[:, qoff:qoff + 512]
                                ).then_inc(sQ[par].h, 16)
                                sQ[par].n += 16
                                if oc == 3:
                                    qo_free[par] = sQ[par].n
                            sec("sync", _wq)
                        else:
                            j = oc - 4
                            if os.environ.get("QSKIP") in ("2", "3", "6"):
                                def _vz2(e, oc=oc, j=j, par=par):
                                    e.wait_ge(sT.h, mm_done[oc])
                                    e.tensor_copy(
                                        out=vf[:, par * 1024 + j * 512:
                                                par * 1024 + j * 512 + 512],
                                        in_=pq[oc % 2][:])
                                    e.drain()
                                    e.nop().then_inc(sV.h, 1)
                                    sV.n += 1
                                    evacq[oc % 2] = sV.n
                                sec("vector", _vz2)
                                continue

                            def _v(e, oc=oc, j=j, par=par):
                                e.wait_ge(sT.h, mm_done[oc])
                                e.wait_ge(sV.h, pvt_free[j])
                                e.tensor_copy(
                                    out=vf[:, par * 1024 + j * 512:
                                            par * 1024 + j * 512 + 512],
                                    in_=pq[oc % 2][:])
                                e.drain()
                                e.nop().then_inc(sV.h, 1)
                                sV.n += 1
                                evacq[oc % 2] = sV.n
                            sec("vector", _v)
                            v_vf = sV.n

                            def _t2(e, j=j, par=par, v_vf=v_vf):
                                e.wait_ge(sV.h, v_vf)
                                for q in range(4):
                                    e.transpose(
                                        pvt[j][:, q * 128: q * 128 + 128],
                                        vf[:, par * 1024 + j * 512 + q * 128:
                                           par * 1024 + j * 512 + q * 128 + 128],
                                        ident_sb[:])
                                e.drain()
                                e.nop().then_inc(sT.h, 1)
                                sT.n += 1
                            sec("tensor", _t2)
                            t_vt = sT.n

                            def _v2(e, j=j, par=par, t_vt=t_vt):
                                e.wait_ge(sT.h, t_vt)
                                e.wait_ge(sU[par].h, vt_free[par])
                                e.tensor_copy(
                                    out=vt[:, par * 1024 + j * 512:
                                            par * 1024 + j * 512 + 512],
                                    in_=pvt[j][:])
                                e.drain()
                                e.nop().then_inc(sV.h, 1)
                                sV.n += 1
                                pvt_free[j] = sV.n
                            sec("vector", _v2)
                            v_vt = sV.n

                            def _w2(e, ts=ts, j=j, par=par, v_vt=v_vt):
                                e.wait_ge(sV.h, v_vt)
                                e.dma_start(
                                    out=scv[j * 8192 + ts * 512:
                                            j * 8192 + ts * 512 + 512, :]
                                    .rearrange("(q p) d -> p q d", p=128),
                                    in_=vt[:, par * 1024 + j * 512:
                                           par * 1024 + j * 512 + 512]
                                    .rearrange("p (q d) -> p q d", d=128)
                                ).then_inc(sU[par].h, 16)
                                sU[par].n += 16
                                if j == 1:
                                    vt_free[par] = sU[par].n
                            sec("sync", _w2)

        def emit_attn(layer):
            P = f"a{layer}"
            pho = contextlib.ExitStack()
            with pho:
                y_sb = pho.enter_context(
                    nc.sbuf_tensor(P + "y", [128, 64 * 256], BF16))
                scq, sck, scv = sc_q[layer], sc_k[layer], sc_v[layer]
                rs_in = rs_in_d[layer]

                ph = contextlib.ExitStack()
                with ph:
                    lsb = lambda n, s, d: ph.enter_context(
                        nc.sbuf_tensor(P + n, s, d))
                    psb = lambda n, s, d: ph.enter_context(
                        nc.psum_tensor(P + n, s, d))
                    qs = lsb("qs", [128, 2 * S], BF16)
                    ks = lsb("ks", [128, 2 * S], BF16)
                    vs = lsb("vs", [128, 2 * 32 * 129], BF16)
                    mask_sb = lsb("mask", [128, 4 * 512], BF16)
                    pr = [lsb(f"pr{i}", [128, 512], BF16) for i in range(2)]
                    lp = lsb("lp", [128, 1], F32)
                    rp = lsb("rp", [128, 1], F32)
                    ps = [psb(f"ps{i}", [128, 512], F32) for i in range(2)]
                    py = [psb(f"py{i}", [128, 129], F32) for i in range(4)]
                    if layer == 1:
                        pcb = psb("pcb", [128, 512], F32)
                        q2f = lsb("q2f", [128, 512], BF16)
                        kkm = lsb("kkm", [128, 1], F32)
                        skk = lsb("skk", [128, 1], F32)
                        sqf = lsb("sqf", [128, S], BF16)
                        cbc = lsb("cbc", [128, 2 * S], BF16)

                    def _gm(e):
                        e.dma_start(out=mask_sb[:], in_=maskT_in
                                    ).then_inc(sG.h, 16)
                        for pp_ in range(2):
                            e.dma_start(
                                out=vs[:, pp_ * 32 * 129:
                                       (pp_ + 1) * 32 * 129]
                                .rearrange("p (j c) -> p j c", c=129)
                                [:, :, 128:129],
                                in_=onescol_in[:, 0:32]
                                .rearrange("p (j o) -> p j o", o=1),
                                allow_slow_non_contiguous=True
                            ).then_inc(sG.h, 16)
                        sG.n += 48
                    sec("gpsimd", _gm)
                    g_mask = sG.n

                    seq_free = [0, 0]
                    py_free = [0]
                    s_cbc = 0
                    for si in range(4):
                        b, hh = si // 2, si % 2
                        par = si % 2

                        def _w(e, b=b, hh=hh, par=par, si=si):
                            e.wait_ge(sT.h, seq_free[par])
                            if si == 0:
                                e.wait_ge(sG.h, g_mask)
                            e.dma_start(out=qs[:, par * S: par * S + S],
                                        in_=scq[hh * 128:(hh + 1) * 128,
                                                b * S:(b + 1) * S]
                                        ).then_inc(sL[par].h, 16)
                            e.dma_start(out=ks[:, par * S: par * S + S],
                                        in_=sck[hh * 128:(hh + 1) * 128,
                                                b * S:(b + 1) * S]
                                        ).then_inc(sL[par].h, 16)
                            e.dma_start(
                                out=vs[:, par * 32 * 129:
                                       (par + 1) * 32 * 129]
                                .rearrange("p (j c) -> p j c", c=129)
                                [:, :, 0:128],
                                in_=scv[hh * 8192 + b * S:
                                        hh * 8192 + (b + 1) * S, :]
                                .rearrange("(j p) c -> p j c", p=128)
                            ).then_inc(sL[par].h, 16)
                            sL[par].n += 48
                        sec("sync", _w)
                        w_seq = sL[par].n
                        w_par = par

                        if layer == 1:
                            for i in range(8):
                                def _vk(e, i=i, par=par, w_seq=w_seq):
                                    if i == 0:
                                        e.wait_ge(sL[par].h, w_seq)
                                    e.wait_ge(sT.h, sT.n)
                                    e.tensor_tensor(
                                        out=q2f[:],
                                        in0=ks[:, par * S + i * 512:
                                               par * S + i * 512 + 512],
                                        in1=ks[:, par * S + i * 512:
                                               par * S + i * 512 + 512],
                                        op=ALU.mult)
                                    e.drain()
                                    e.nop().then_inc(sV.h, 1)
                                    sV.n += 1
                                sec("vector", _vk)
                                v_sq = sV.n

                                def _tk(e, v_sq=v_sq):
                                    e.wait_ge(sV.h, v_sq)
                                    e.matmul(pcb[:], onesM[:], q2f[:],
                                             start=True, stop=True)
                                    e.drain()
                                    e.nop().then_inc(sT.h, 1)
                                    sT.n += 1
                                sec("tensor", _tk)
                                t_i = sT.n

                                def _vk2(e, i=i, t_i=t_i):
                                    e.wait_ge(sT.h, t_i)
                                    if i == 0:
                                        e.reduce_max(out=kkm[:], in_=pcb[:],
                                                     axis=AX.X)
                                        e.drain()
                                    else:
                                        e.reduce_max(out=skk[:], in_=pcb[:],
                                                     axis=AX.X)
                                        e.drain()
                                        e.tensor_tensor(out=kkm[:],
                                                        in0=kkm[:],
                                                        in1=skk[:],
                                                        op=ALU.max)
                                        e.drain()
                                    e.nop().then_inc(sV.h, 1)
                                    sV.n += 1
                                sec("vector", _vk2)
                            for i in range(8):
                                def _vq(e, i=i, par=par):
                                    e.wait_ge(sT.h, sT.n)
                                    e.tensor_tensor(
                                        out=q2f[:],
                                        in0=qs[:, par * S + i * 512:
                                               par * S + i * 512 + 512],
                                        in1=qs[:, par * S + i * 512:
                                               par * S + i * 512 + 512],
                                        op=ALU.mult)
                                    e.drain()
                                    e.nop().then_inc(sV.h, 1)
                                    sV.n += 1
                                sec("vector", _vq)
                                v_sq = sV.n

                                def _tq(e, v_sq=v_sq):
                                    e.wait_ge(sV.h, v_sq)
                                    e.wait_ge(sS.h, sS.n)
                                    e.matmul(pcb[:], onesM[:], q2f[:],
                                             start=True, stop=True)
                                    e.drain()
                                    e.nop().then_inc(sT.h, 1)
                                    sT.n += 1
                                sec("tensor", _tq)
                                t_i = sT.n

                                def _sq(e, i=i, t_i=t_i):
                                    e.wait_ge(sT.h, t_i)
                                    e.activation(out=sqf[:, i * 512:
                                                        i * 512 + 512],
                                                 in_=pcb[:], func=ACT.Sqrt
                                                 ).then_inc(sS.h, 1)
                                    e.drain()
                                    sS.n += 1
                                sec("scalar", _sq)
                            v_kkm = sV.n
                            s_sqq = sS.n

                            def _sc(e, v_kkm=v_kkm, s_sqq=s_sqq,
                                    w_par=w_par):
                                e.wait_ge(sV.h, v_kkm)
                                e.wait_ge(sS.h, s_sqq)
                                e.activation(out=skk[:], in_=kkm[:],
                                             func=ACT.Sqrt)
                                e.drain()
                                e.nop().then_inc(sS.h, 1)
                                sS.n += 1
                                e.activation(out=cbc[:, w_par * S:
                                                    w_par * S + S],
                                             in_=sqf[:],
                                             func=ACT.Relu,
                                             scale=skk[:, 0:1],
                                             bias=eps_sb[:, 2:3]
                                             ).then_inc(sS.h, 1)
                                sS.n += 1
                            sec("scalar", _sc)
                        s_cbc = sS.n

                        for qc in range(8):
                            nkc = 4 * qc + 4
                            T0 = sT.n
                            S0 = sS.n
                            V0 = sV.n
                            sub_idx = {}
                            mask_idx = {}
                            vidx = 0
                            for kc in range(nkc):
                                if layer == 1:
                                    vidx += 1
                                    sub_idx[kc] = vidx
                                if kc >= 4 * qc:
                                    vidx += 1
                                    mask_idx[kc] = vidx
                            nvec = vidx

                            def _t(e, qc=qc, nkc=nkc, par=par, T0=T0,
                                   S0=S0, V0=V0, mask_idx=mask_idx,
                                   w_seq=w_seq):
                                e.wait_ge(sL[par].h, w_seq)
                                e.wait_ge(sV.h, V0)
                                e.wait_ge(sV.h, py_free[0])
                                for kc in range(nkc):
                                    if kc >= 2:
                                        e.wait_ge(sS.h, S0 + kc - 1)
                                    e.matmul(ps[kc % 2][:],
                                             ks[:, par * S + kc * 128:
                                                par * S + kc * 128 + 128],
                                             qs[:, par * S + qc * 512:
                                                par * S + qc * 512 + 512],
                                             start=True, stop=True)
                                    e.drain()
                                    e.nop().then_inc(sT.h, 1)
                                    e.wait_ge(sS.h, S0 + kc + 1)
                                    if kc >= 4 * qc:
                                        e.wait_ge(sV.h, V0 + mask_idx[kc])
                                    for qt in range(4):
                                        e.matmul(
                                            py[qt][:],
                                            pr[kc % 2][:, qt * 128:
                                                       qt * 128 + 128],
                                            vs[:, par * 32 * 129 + kc * 129:
                                               par * 32 * 129 + kc * 129 + 129],
                                            start=(kc == 0),
                                            stop=(kc == nkc - 1))
                                    e.drain()
                                    e.nop().then_inc(sT.h, 1)
                                sT.n += 2 * nkc
                            sec("tensor", _t)
                            t_qc_end = sT.n

                            def _vsm(e, qc=qc, nkc=nkc, par=par, T0=T0,
                                     S0=S0, s_cbc=(s_cbc if layer == 1
                                                   else 0)):
                                if layer == 1:
                                    e.wait_ge(sS.h, s_cbc)
                                for kc in range(nkc):
                                    if layer == 1:
                                        e.wait_ge(sT.h, T0 + 2 * kc + 1)
                                        e.tensor_tensor(
                                            out=ps[kc % 2][:],
                                            in0=ps[kc % 2][:],
                                            in1=cbc[:, par * S + qc * 512:
                                                    par * S + qc * 512 + 512],
                                            op=ALU.subtract)
                                        e.drain()
                                        e.nop().then_inc(sV.h, 1)
                                    if kc >= 4 * qc:
                                        m = kc - 4 * qc
                                        e.wait_ge(sS.h, S0 + kc + 1)
                                        e.tensor_tensor(
                                            out=pr[kc % 2][:],
                                            in0=pr[kc % 2][:],
                                            in1=mask_sb[:, m * 512:
                                                        m * 512 + 512],
                                            op=ALU.mult)
                                        e.drain()
                                        e.nop().then_inc(sV.h, 1)
                                sV.n += (nkc if layer == 1 else 0) + 4
                            sec("vector", _vsm)

                            def _s(e, nkc=nkc, T0=T0, V0=V0, hh=hh,
                                   sub_idx=sub_idx):
                                for kc in range(nkc):
                                    if layer == 1:
                                        e.wait_ge(sV.h, V0 + sub_idx[kc])
                                    else:
                                        e.wait_ge(sT.h, T0 + 2 * kc + 1)
                                    if kc >= 2:
                                        e.wait_ge(sT.h, T0 + 2 * (kc - 2) + 2)
                                    if layer == 0:
                                        e.activation(
                                            out=pr[kc % 2][:],
                                            in_=ps[kc % 2][:], func=ACT.Exp,
                                            scale=gain_sb[:, hh:hh + 1],
                                            bias=nag_sb[:, hh:hh + 1]
                                        ).then_inc(sS.h, 1)
                                        e.drain()
                                    else:
                                        e.activation(
                                            out=pr[kc % 2][:],
                                            in_=ps[kc % 2][:], func=ACT.Exp,
                                            scale=INV_SQRT_DH
                                        ).then_inc(sS.h, 1)
                                        e.drain()
                                sS.n += nkc
                            sec("scalar", _s)

                            def _ve(e, qc=qc, b=b, hh=hh,
                                    t_qc_end=t_qc_end):
                                e.wait_ge(sT.h, t_qc_end)
                                for qt in range(4):
                                    g = 32 * b + 4 * qc + qt
                                    e.tensor_scalar(out=lp[:],
                                                    in0=py[qt][:, 128:129],
                                                    scalar1=1e-30,
                                                    scalar2=None, op0=ALU.add)
                                    e.drain()
                                    e.reciprocal(out=rp[:], in_=lp[:])
                                    e.drain()
                                    e.tensor_scalar(
                                        out=y_sb[:, g * 256 + hh * 128:
                                                 g * 256 + hh * 128 + 128],
                                        in0=py[qt][:, 0:128],
                                        scalar1=rp[:, 0:1], scalar2=None,
                                        op0=ALU.mult)
                                    e.drain()
                                e.nop().then_inc(sV.h, 1)
                                sV.n += 1
                            sec("vector", _ve)
                            py_free[0] = sV.n
                        seq_free[par] = t_qc_end
                barrier()

                ph2 = contextlib.ExitStack()
                with ph2:
                    lsb2 = lambda n, s, d: ph2.enter_context(
                        nc.sbuf_tensor(P + n, s, d))
                    psb2 = lambda n, s, d: ph2.enter_context(
                        nc.psum_tensor(P + n, s, d))
                    yT = lsb2("yT", [128, 2 * 8192], BF16)
                    po = lsb2("po", [128, 2 * D], BF16)
                    Wo = lsb2("Wo", [128, 2 * D], BF16)
                    ptp = [psb2(f"ptp{i}", [128, 512], BF16)
                           for i in range(2)]
                    pso = [psb2(f"pso{i}", [128, 512], F32)
                           for i in range(2)]
                    wo_in = wout0_my if layer == 0 else wout1_my

                    def _g(e):
                        e.dma_start(
                            out=Wo[:].rearrange("p (k o) -> p k o", o=D),
                            in_=wo_in.rearrange("(k p) o -> p k o", p=128)
                        ).then_inc(sG.h, 16)
                        sG.n += 16
                    sec("gpsimd", _g)
                    g_wo = sG.n

                    tp_free = [0, 0]
                    for hh2 in range(2):
                        for gg in range(16):
                            pbuf = gg % 2

                            def _t(e, hh2=hh2, gg=gg, pbuf=pbuf):
                                e.wait_ge(sV.h, tp_free[pbuf])
                                for k in range(4):
                                    g4 = gg * 4 + k
                                    e.transpose(
                                        ptp[pbuf][:, k * 128: k * 128 + 128],
                                        y_sb[:, g4 * 256 + hh2 * 128:
                                             g4 * 256 + hh2 * 128 + 128],
                                        ident_sb[:])
                                e.drain()
                                e.nop().then_inc(sT.h, 1)
                                sT.n += 1
                            sec("tensor", _t)
                            t_tp = sT.n

                            def _v(e, hh2=hh2, gg=gg, pbuf=pbuf, t_tp=t_tp):
                                e.wait_ge(sT.h, t_tp)
                                e.tensor_copy(
                                    out=yT[:, hh2 * 8192 + gg * 512:
                                           hh2 * 8192 + gg * 512 + 512],
                                    in_=ptp[pbuf][:])
                                e.drain()
                                e.nop().then_inc(sV.h, 1)
                                sV.n += 1
                                tp_free[pbuf] = sV.n
                            sec("vector", _v)
                    v_yT = sV.n

                    po_free = [0, 0]
                    for tt in range(64):
                        par2 = tt % 2
                        vbase = sV.n

                        def _t(e, tt=tt, par2=par2, vbase=vbase):
                            if tt == 0:
                                e.wait_ge(sV.h, v_yT)
                                e.wait_ge(sG.h, g_wo)
                            e.wait_ge(sV.h, vbase)
                            e.wait_ge(sP[par2].h, po_free[par2])
                            for grp in range(4):
                                if grp >= 2:
                                    e.wait_ge(sV.h, vbase + grp - 1)
                                for hh2 in range(2):
                                    e.matmul(
                                        pso[grp % 2][:],
                                        yT[:, hh2 * 8192 + tt * 128:
                                           hh2 * 8192 + tt * 128 + 128],
                                        Wo[:, hh2 * D + grp * 512:
                                           hh2 * D + grp * 512 + 512],
                                        start=(hh2 == 0), stop=(hh2 == 1))
                                e.drain()
                                e.nop().then_inc(sT.h, 1)
                            sT.n += 4
                        sec("tensor", _t)
                        t_mm = sT.n

                        def _v(e, par2=par2, t_mm=t_mm):
                            for grp in range(4):
                                e.wait_ge(sT.h, t_mm - 4 + grp + 1)
                                e.tensor_copy(
                                    out=po[:, par2 * D + grp * 512:
                                           par2 * D + grp * 512 + 512],
                                    in_=pso[grp % 2][:])
                                e.drain()
                                e.nop().then_inc(sV.h, 1)
                            sV.n += 4
                        sec("vector", _v)
                        v_po = sV.n

                        def _w(e, tt=tt, par2=par2, v_po=v_po):
                            e.wait_ge(sV.h, v_po)
                            e.dma_start(
                                out=rs_in[tt * 128:(tt + 1) * 128, :],
                                in_=po[:, par2 * D: par2 * D + D]
                            ).then_inc(sP[par2].h, 16)
                            sP[par2].n += 16
                            po_free[par2] = sP[par2].n
                        sec("sync", _w)

                    emit_collective("ReduceScatter", ALU.add, rs_in,
                                    rs_out_d[layer])

        def emit_residual(layer, add_bias):
            ph = contextlib.ExitStack()
            with ph:
                ro2 = ph.enter_context(
                    nc.sbuf_tensor(f"rr{layer}", [128, NT * D], BF16))
                outb_sb = ph.enter_context(
                    nc.sbuf_tensor(f"ob{layer}", [128, D], F32))
                cwait = sC.n

                def _w(e):
                    e.wait_ge(sC.h, cwait)
                    e.dma_start(
                        out=ro2[:].rearrange("p (t d) -> p t d", d=D),
                        in_=rs_out_d[layer].rearrange("(t p) d -> p t d",
                                                      p=128)
                    ).then_inc(sW.h, 16)
                    sW.n += 16
                    if add_bias:
                        e.dma_start(out=outb_sb[:], in_=outb_in
                                    ).then_inc(sW.h, 16)
                        sW.n += 16
                sec("sync", _w)
                w_ro = sW.n

                def _v(e):
                    e.wait_ge(sW.h, w_ro)
                    for t in range(NT):
                        e.tensor_tensor(out=x_sb[:, t * D:(t + 1) * D],
                                        in0=x_sb[:, t * D:(t + 1) * D],
                                        in1=ro2[:, t * D:(t + 1) * D],
                                        op=ALU.add)
                        if add_bias:
                            e.drain()
                            e.tensor_tensor(out=x_sb[:, t * D:(t + 1) * D],
                                            in0=x_sb[:, t * D:(t + 1) * D],
                                            in1=outb_sb[:], op=ALU.add)
                    e.drain()
                    e.nop().then_inc(sV.h, 1)
                    sV.n += 1
                sec("vector", _v)

        def emit_ffn():
            ph = contextlib.ExitStack()
            with ph:
                lsb = lambda n, s, d: ph.enter_context(
                    nc.sbuf_tensor("f" + n, s, d))
                psb = lambda n, s, d: ph.enter_context(
                    nc.psum_tensor("f" + n, s, d))
                pbc = psb("pbc", [128, 512], F32)
                phg = contextlib.ExitStack()
                gsbf = lambda n, s, d: phg.enter_context(
                    nc.sbuf_tensor("fg" + n, s, d, side="right"))
                xT = gsbf("xT", [128, KC * T], BF16)
                emit_ln("L2", None, xT)
                barrier()
                hT = lsb("hT", [128, HC * T], BF16)
                S128 = gsbf("S128", [128, T], BF16)
                wg = gsbf("wg", [128, 2 * 2048], BF16)
                wu = gsbf("wu", [128, 2048], BF16)
                tgs = gsbf("tgs", [128, 512], F32)
                gsb = gsbf("gsb", [128, 512], BF16)
                pg = phg.enter_context(nc.psum_tensor("fpg", [128, 512], F32))
                pu = phg.enter_context(nc.psum_tensor("fpu", [128, 512], F32))

                for half in range(2):
                    def _t(e, half=half):
                        if half == 1:
                            e.wait_ge(sV.h, sV.n)
                        for kc in range(KC):
                            e.matmul(pbc[:], onesM[:],
                                     xT[:, kc * T + half * 512:
                                        kc * T + half * 512 + 512],
                                     start=(kc == 0), stop=(kc == KC - 1))
                        e.drain()
                        e.nop().then_inc(sT.h, 1)
                        sT.n += 1
                    sec("tensor", _t)
                    t_sx = sT.n

                    def _v(e, half=half, t_sx=t_sx):
                        e.wait_ge(sT.h, t_sx)
                        e.tensor_scalar(out=S128[:, half * 512:
                                                half * 512 + 512],
                                        in0=pbc[:], scalar1=-128.0,
                                        scalar2=None, op0=ALU.mult)
                        e.drain()
                        e.nop().then_inc(sV.h, 1)
                        sV.n += 1
                    sec("vector", _v)
                v_s128 = sV.n

                w_free = [0, 0]
                wu_free = [0]
                gd = [0, 0]
                gdu = [0]
                for hc in range(HC):
                    par = hc % 2

                    def _g(e, hc=hc, par=par):
                        e.wait_ge(sT.h, w_free[par])
                        e.dma_start(
                            out=wg[:, par * 2048: par * 2048 + 2048
                                   ].rearrange("p (kc c) -> p kc c", c=128),
                            in_=wgT_in[:, hc * 128:(hc + 1) * 128
                                       ].rearrange("(kc p) c -> p kc c",
                                                   p=128)
                        ).then_inc(sF[par].h, 16)
                        sF[par].n += 16
                        gd[par] = sF[par].n
                        e.wait_ge(sT.h, wu_free[0])
                        e.dma_start(
                            out=wu[:].rearrange("p (kc c) -> p kc c", c=128),
                            in_=wuT_in[:, hc * 128:(hc + 1) * 128
                                       ].rearrange("(kc p) c -> p kc c",
                                                   p=128)
                        ).then_inc(sFu.h, 16)
                        sFu.n += 16
                        gdu[0] = sFu.n
                    sec("gpsimd", _g)

                    for half in range(2):
                        def _t(e, hc=hc, par=par, half=half):
                            e.wait_ge(sF[par].h, gd[par])
                            e.wait_ge(sV.h, v_s128 if (hc == 0 and half == 0)
                                      else sV.n)
                            for kc in range(KC):
                                e.matmul(pg[:],
                                         wg[:, par * 2048 + kc * 128:
                                            par * 2048 + kc * 128 + 128],
                                         xT[:, kc * T + half * 512:
                                            kc * T + half * 512 + 512],
                                         start=(kc == 0), stop=(kc == KC - 1))
                            e.drain()
                            e.nop().then_inc(sT.h, 1)
                            e.wait_ge(sFu.h, gdu[0])
                            for kc in range(KC):
                                e.matmul(pu[:],
                                         wu[:, kc * 128: kc * 128 + 128],
                                         xT[:, kc * T + half * 512:
                                            kc * T + half * 512 + 512],
                                         start=(kc == 0), stop=(kc == KC - 1))
                            e.drain()
                            e.nop().then_inc(sT.h, 1)
                            sT.n += 2
                            if half == 1:
                                w_free[par] = sT.n
                                wu_free[0] = sT.n
                        sec("tensor", _t)
                        t_gu = sT.n

                        def _v(e, half=half, t_gu=t_gu):
                            e.wait_ge(sT.h, t_gu - 1)
                            e.tensor_tensor(out=tgs[:], in0=pg[:],
                                            in1=S128[:, half * 512:
                                                     half * 512 + 512],
                                            op=ALU.add)
                            e.drain()
                            e.nop().then_inc(sV.h, 1)
                            sV.n += 1
                        sec("vector", _v)
                        v_tg = sV.n

                        def _s(e, hc=hc, v_tg=v_tg):
                            e.wait_ge(sV.h, v_tg)
                            e.activation(out=gsb[:], in_=tgs[:],
                                         func=ACT.Silu,
                                         scale=sg_sb[:, hc:hc + 1]
                                         ).then_inc(sS.h, 1)
                            e.drain()
                            sS.n += 1
                        sec("scalar", _s)
                        s_g = sS.n

                        def _v2(e, hc=hc, half=half, s_g=s_g, t_gu=t_gu):
                            e.wait_ge(sS.h, s_g)
                            e.wait_ge(sT.h, t_gu)
                            e.tensor_tensor(out=tgs[:], in0=pu[:],
                                            in1=S128[:, half * 512:
                                                     half * 512 + 512],
                                            op=ALU.add)
                            e.drain()
                            e.scalar_tensor_tensor(
                                out=hT[:, hc * T + half * 512:
                                       hc * T + half * 512 + 512],
                                in0=tgs[:], scalar=su_sb[:, hc:hc + 1],
                                in1=gsb[:], op0=ALU.mult, op1=ALU.mult)
                            e.drain()
                            e.nop().then_inc(sV.h, 1)
                            sV.n += 1
                        sec("vector", _v2)
                barrier()
                phg.close()

                mshb = lsb("mshb", [128, T], BF16)
                for half in range(2):
                    def _t(e, half=half):
                        if half == 1:
                            e.wait_ge(sV.h, sV.n)
                        for hc in range(HC):
                            e.matmul(pbc[:], onesM[:],
                                     hT[:, hc * T + half * 512:
                                        hc * T + half * 512 + 512],
                                     start=(hc == 0), stop=(hc == HC - 1))
                        e.drain()
                        e.nop().then_inc(sT.h, 1)
                        sT.n += 1
                    sec("tensor", _t)
                    t_sh = sT.n

                    def _v(e, half=half, t_sh=t_sh):
                        e.wait_ge(sT.h, t_sh)
                        e.tensor_scalar(out=mshb[:, half * 512:
                                                half * 512 + 512],
                                        in0=pbc[:], scalar1=-128.0,
                                        scalar2=None, op0=ALU.mult)
                        e.drain()
                        e.nop().then_inc(sV.h, 1)
                        sV.n += 1
                    sec("vector", _v)
                v_mshb = sV.n

                wd = lsb("wd", [128, 2 * HC * 128], BF16)
                tds = lsb("tds", [128, 512], F32)
                dfm = lsb("dfm", [128, 512], BF16)
                pd = [psb(f"pd{i}", [128, 512], F32) for i in range(2)]
                ptd = psb("ptd", [128, 512], BF16)
                wd_free = [0, 0]
                gdd = [0, 0]
                for oc in range(KC):
                    par = oc % 2

                    def _g(e, oc=oc, par=par):
                        e.wait_ge(sT.h, wd_free[par])
                        e.dma_start(
                            out=wd[:, par * HC * 128:
                                   par * HC * 128 + HC * 128
                                   ].rearrange("p (hc c) -> p hc c", c=128),
                            in_=wdT_in[:, oc * 128:(oc + 1) * 128
                                       ].rearrange("(hc p) c -> p hc c",
                                                   p=128)
                        ).then_inc(sF[par].h, 16)
                        sF[par].n += 16
                        gdd[par] = sF[par].n
                    sec("gpsimd", _g)

                    for half in range(2):
                        def _t(e, oc=oc, par=par, half=half):
                            e.wait_ge(sF[par].h, gdd[par])
                            e.wait_ge(sV.h, v_mshb if (oc == 0 and half == 0)
                                      else sV.n)
                            for hc in range(HC):
                                e.matmul(pd[half][:],
                                         wd[:, par * HC * 128 + hc * 128:
                                            par * HC * 128 + hc * 128 + 128],
                                         hT[:, hc * T + half * 512:
                                            hc * T + half * 512 + 512],
                                         start=(hc == 0), stop=(hc == HC - 1))
                            e.drain()
                            e.nop().then_inc(sT.h, 1)
                            sT.n += 1
                            if half == 1:
                                wd_free[par] = sT.n
                        sec("tensor", _t)
                        t_d = sT.n

                        def _v(e, oc=oc, half=half, t_d=t_d):
                            e.wait_ge(sT.h, t_d)
                            e.tensor_tensor(out=tds[:], in0=pd[half][:],
                                            in1=mshb[:, half * 512:
                                                     half * 512 + 512],
                                            op=ALU.add)
                            e.drain()
                            e.tensor_scalar(out=dfm[:], in0=tds[:],
                                            scalar1=sd_sb[:, oc:oc + 1],
                                            scalar2=None, op0=ALU.mult)
                            e.drain()
                            e.nop().then_inc(sV.h, 1)
                            sV.n += 1
                        sec("vector", _v)
                        v_d = sV.n

                        def _t2(e, v_d=v_d):
                            e.wait_ge(sV.h, v_d)
                            for j in range(4):
                                e.transpose(ptd[:, j * 128: j * 128 + 128],
                                            dfm[:, j * 128: j * 128 + 128],
                                            ident_sb[:])
                            e.drain()
                            e.nop().then_inc(sT.h, 1)
                            sT.n += 1
                        sec("tensor", _t2)
                        t_tr = sT.n

                        def _v2(e, oc=oc, half=half, t_tr=t_tr):
                            e.wait_ge(sT.h, t_tr)
                            for j in range(4):
                                lt = half * 4 + j
                                e.tensor_tensor(
                                    out=x_sb[:, lt * D + oc * 128:
                                             lt * D + oc * 128 + 128],
                                    in0=x_sb[:, lt * D + oc * 128:
                                             lt * D + oc * 128 + 128],
                                    in1=ptd[:, j * 128: j * 128 + 128],
                                    op=ALU.add)
                            e.drain()
                            e.nop().then_inc(sV.h, 1)
                            sV.n += 1
                        sec("vector", _v2)

        def emit_lat(xT):
            ph = contextlib.ExitStack()
            with ph:
                lsb = lambda n, s, d: ph.enter_context(
                    nc.sbuf_tensor("l" + n, s, d))
                psb = lambda n, s, d: ph.enter_context(
                    nc.psum_tensor("l" + n, s, d))
                Wa = lsb("Wa", [128, KC * DL], BF16)
                lat = lsb("lat", [128, 4 * T], BF16)
                nwtm = lsb("nwtm", [128, 512], F32)
                ssl = lsb("ssl", [128, 1], F32)
                sdl = lsb("sdl", [128, 1], F32)
                ril = lsb("ril", [128, 1], F32)
                sql = lsb("sql", [128, 512], BF16)
                ltc = lsb("ltc", [128, 512], BF16)
                lnt = lsb("lnt", [128, 512], BF16)
                latn = lsb("latn", [128, 4 * T], BF16)
                pl = [psb(f"pl{i}", [128, 512], F32) for i in range(2)]
                ptl = psb("ptl", [128, 512], BF16)
                plf = psb("plf", [128, 512], BF16)
                v_entry = sV.n

                def _g(e):
                    e.dma_start(
                        out=Wa[:].rearrange("p (k o) -> p k o", o=DL),
                        in_=wkvaT_in.rearrange("(k p) o -> p k o", p=128)
                    ).then_inc(sG.h, 16)
                    sG.n += 16
                sec("gpsimd", _g)
                g_wa = sG.n

                def _w0(e):
                    e.dma_start(out=nwtm[:], in_=normwtm_in).then_inc(sW.h, 16)
                    sW.n += 16
                sec("sync", _w0)
                w_nw = sW.n

                for lc in range(4):
                    for half in range(2):
                        def _t(e, lc=lc, half=half):
                            e.wait_ge(sG.h, g_wa)
                            e.wait_ge(sV.h, v_entry if (lc == 0 and half == 0)
                                      else sV.n)
                            for kc in range(KC):
                                e.matmul(pl[half][:],
                                         Wa[:, kc * DL + lc * 128:
                                            kc * DL + lc * 128 + 128],
                                         xT[:, kc * T + half * 512:
                                            kc * T + half * 512 + 512],
                                         start=(kc == 0), stop=(kc == KC - 1))
                            e.drain()
                            e.nop().then_inc(sT.h, 1)
                            sT.n += 1
                        sec("tensor", _t)
                        t_l = sT.n

                        def _v(e, lc=lc, half=half, t_l=t_l):
                            e.wait_ge(sT.h, t_l)
                            e.tensor_copy(out=lat[:, lc * T + half * 512:
                                                  lc * T + half * 512 + 512],
                                          in_=pl[half][:])
                            e.drain()
                            e.nop().then_inc(sV.h, 1)
                            sV.n += 1
                        sec("vector", _v)
                v_lat = sV.n

                for t in range(NT):
                    def _t(e, t=t):
                        e.wait_ge(sV.h, v_lat if t == 0 else sV.n)
                        for lc in range(4):
                            e.transpose(ptl[:, lc * 128: lc * 128 + 128],
                                        lat[:, lc * T + t * 128:
                                            lc * T + t * 128 + 128],
                                        ident_sb[:])
                        e.drain()
                        e.nop().then_inc(sT.h, 1)
                        sT.n += 1
                    sec("tensor", _t)
                    t_tm = sT.n

                    def _v(e, t=t, t_tm=t_tm):
                        e.wait_ge(sT.h, t_tm)
                        e.wait_ge(sW.h, w_nw)
                        e.tensor_copy(out=ltc[:], in_=ptl[:])
                        e.drain()
                        e.tensor_tensor(out=sql[:], in0=ltc[:], in1=ltc[:],
                                        op=ALU.mult)
                        e.drain()
                        e.reduce_sum(out=ssl[:], in_=sql[:], axis=AX.X)
                        e.drain()
                        e.nop().then_inc(sV.h, 1)
                        sV.n += 1
                    sec("vector", _v)
                    v_ss = sV.n

                    def _s(e, v_ss=v_ss):
                        e.wait_ge(sV.h, v_ss)
                        e.activation(out=sdl[:], in_=ssl[:], func=ACT.Sqrt,
                                     scale=1.0 / DL, bias=eps_sb[:, 1:2]
                                     ).then_inc(sS.h, 1)
                        e.drain()
                        sS.n += 1
                    sec("scalar", _s)
                    s_d = sS.n

                    def _v2(e, s_d=s_d):
                        e.wait_ge(sS.h, s_d)
                        e.reciprocal(out=ril[:], in_=sdl[:])
                        e.drain()
                        e.tensor_scalar(out=lnt[:], in0=ptl[:],
                                        scalar1=ril[:, 0:1], scalar2=None,
                                        op0=ALU.mult)
                        e.drain()
                        e.tensor_tensor(out=lnt[:], in0=lnt[:], in1=nwtm[:],
                                        op=ALU.mult)
                        e.drain()
                        e.nop().then_inc(sV.h, 1)
                        sV.n += 1
                    sec("vector", _v2)
                    v_ln = sV.n

                    def _t2(e, v_ln=v_ln):
                        e.wait_ge(sV.h, v_ln)
                        for lc in range(4):
                            e.transpose(plf[:, lc * 128: lc * 128 + 128],
                                        lnt[:, lc * 128: lc * 128 + 128],
                                        ident_sb[:])
                        e.drain()
                        e.nop().then_inc(sT.h, 1)
                        sT.n += 1
                    sec("tensor", _t2)
                    t_fm = sT.n

                    def _v3(e, t=t, t_fm=t_fm):
                        e.wait_ge(sT.h, t_fm)
                        for lc in range(4):
                            e.tensor_copy(
                                out=latn[:, lc * T + t * 128:
                                         lc * T + t * 128 + 128],
                                in_=plf[:, lc * 128: lc * 128 + 128])
                        e.drain()
                        e.nop().then_inc(sV.h, 1)
                        sV.n += 1
                    sec("vector", _v3)
                v_latn = sV.n

                def _w(e, v_latn=v_latn):
                    e.wait_ge(sV.h, v_latn)
                    for lc in range(4):
                        e.dma_start(out=agl_in[lc * 128:(lc + 1) * 128, :],
                                    in_=latn[:, lc * T:(lc + 1) * T]
                                    ).then_inc(sW.h, 16)
                        sW.n += 16
                sec("sync", _w)

        def emit_moe():
            ph = contextlib.ExitStack()
            with ph:
                lsb = lambda n, s, d: ph.enter_context(
                    nc.sbuf_tensor("m" + n, s, d))
                psb = lambda n, s, d: ph.enter_context(
                    nc.psum_tensor("m" + n, s, d))
                xT = lsb("xT", [128, KC * T], BF16)
                emit_ln("L4", None, xT)
                barrier()
                WA = lsb("WA", [128, KC * 520], BF16)
                BW = lsb("BW", [128, 4 * D], BF16)
                lgs = lsb("lgs", [128, 8], F32)
                hsf = lsb("hsf", [128, 512], F32)
                nmx = lsb("nmx", [128, 1], F32)
                mxp = lsb("mxp", [128, 1], F32)
                expv = lsb("expv", [128, 8], F32)
                sme = lsb("sme", [128, 1], F32)
                rcp = lsb("rcp", [128, 1], F32)
                ps4 = lsb("ps4", [128, 1], F32)
                msk = lsb("msk", [128, 8], F32)
                hh2 = lsb("hh2", [128, 512], BF16)
                hmT = lsb("hmT", [128, 4 * T], BF16)
                plh = [psb(f"plh{i}", [128, 520], F32) for i in range(2)]
                pht = psb("pht", [128, 512], BF16)
                pyy = [psb(f"pyy{i}", [128, 512], F32) for i in range(2)]

                def _g(e):
                    e.dma_start(
                        out=WA[:].rearrange("p (k o) -> p k o", o=520),
                        in_=wgA_in.rearrange("(k p) o -> p k o", p=128)
                    ).then_inc(sG.h, 16)
                    e.dma_start(
                        out=BW[:].rearrange("p (k o) -> p k o", o=D),
                        in_=bwT_in.rearrange("(k p) o -> p k o", p=128)
                    ).then_inc(sG.h, 16)
                    sG.n += 32
                sec("gpsimd", _g)
                g_w = sG.n

                evd = [0, 0]
                for t in range(NT):
                    par = t % 2

                    def _t(e, t=t, par=par):
                        if t == 0:
                            e.wait_ge(sG.h, g_w)
                        e.wait_ge(sV.h, evd[par])
                        for kc in range(KC):
                            e.matmul(plh[par][:, 0:512],
                                     xT[:, kc * T + t * 128:
                                        kc * T + t * 128 + 128],
                                     WA[:, kc * 520: kc * 520 + 512],
                                     start=(kc == 0), stop=(kc == KC - 1))
                        for kc in range(KC):
                            e.matmul(plh[par][:, 512:520],
                                     xT[:, kc * T + t * 128:
                                        kc * T + t * 128 + 128],
                                     WA[:, kc * 520 + 512: kc * 520 + 520],
                                     start=(kc == 0), stop=(kc == KC - 1))
                        e.drain()
                        e.nop().then_inc(sT.h, 1)
                        sT.n += 1
                    sec("tensor", _t)
                    t_lh = sT.n

                    def _v(e, par=par, t_lh=t_lh):
                        e.wait_ge(sT.h, t_lh)
                        e.tensor_copy(out=lgs[:], in_=plh[par][:, 512:520])
                        e.tensor_copy(out=hsf[:], in_=plh[par][:, 0:512])
                        e.drain()
                        e.reduce_max(out=nmx[:], in_=lgs[:], axis=AX.X,
                                     negate=True)
                        e.drain()
                        e.tensor_scalar(out=mxp[:], in0=nmx[:], scalar1=-1.0,
                                        scalar2=None, op0=ALU.mult)
                        e.drain()
                        e.tensor_scalar(out=msk[:], in0=lgs[:],
                                        scalar1=mxp[:, 0:1], scalar2=None,
                                        op0=ALU.is_ge)
                        e.drain()
                        e.nop().then_inc(sV.h, 1)
                        sV.n += 1
                    sec("vector", _v)
                    v_sm = sV.n

                    def _s(e, v_sm=v_sm):
                        e.wait_ge(sV.h, v_sm)
                        e.activation(out=expv[:], in_=lgs[:], func=ACT.Exp,
                                     bias=nmx[:, 0:1],
                                     accum_out=sme[:, 0:1]).then_inc(sS.h, 1)
                        e.drain()
                        sS.n += 1
                    sec("scalar", _s)
                    s_e = sS.n

                    def _v2(e, par=par, s_e=s_e):
                        e.wait_ge(sS.h, s_e)
                        e.reciprocal(out=rcp[:], in_=sme[:])
                        e.drain()
                        e.tensor_scalar(out=ps4[:], in0=rcp[:], scalar1=4.0,
                                        scalar2=None, op0=ALU.mult)
                        e.drain()
                        for ex in range(8):
                            e.tensor_scalar(out=hh2[:, ex * 64: ex * 64 + 64],
                                            in0=hsf[:, ex * 64: ex * 64 + 64],
                                            scalar1=msk[:, ex:ex + 1],
                                            scalar2=ps4[:, 0:1],
                                            op0=ALU.mult, op1=ALU.mult)
                        e.drain()
                        e.nop().then_inc(sV.h, 1)
                        sV.n += 1
                        evd[par] = sV.n
                    sec("vector", _v2)
                    v_h = sV.n

                    def _t2(e, v_h=v_h):
                        e.wait_ge(sV.h, v_h)
                        for j in range(4):
                            e.transpose(pht[:, j * 128: j * 128 + 128],
                                        hh2[:, j * 128: j * 128 + 128],
                                        ident_sb[:])
                        e.drain()
                        e.nop().then_inc(sT.h, 1)
                        sT.n += 1
                    sec("tensor", _t2)
                    t_ht = sT.n

                    def _v3(e, t=t, t_ht=t_ht):
                        e.wait_ge(sT.h, t_ht)
                        for j in range(4):
                            e.tensor_copy(out=hmT[:, j * T + t * 128:
                                                  j * T + t * 128 + 128],
                                          in_=pht[:, j * 128: j * 128 + 128])
                        e.drain()
                        e.nop().then_inc(sV.h, 1)
                        sV.n += 1
                    sec("vector", _v3)
                v_hmT = sV.n

                for t in range(NT):
                    for grp in range(4):
                        gpar = grp % 2

                        def _t(e, t=t, grp=grp, gpar=gpar):
                            if t == 0 and grp == 0:
                                e.wait_ge(sV.h, v_hmT)
                            if not (t == 0 and grp < 2):
                                e.wait_ge(sV.h, sV.n)
                            for rc in range(4):
                                e.matmul(pyy[gpar][:],
                                         hmT[:, rc * T + t * 128:
                                             rc * T + t * 128 + 128],
                                         BW[:, rc * D + grp * 512:
                                            rc * D + grp * 512 + 512],
                                         start=(rc == 0), stop=(rc == 3))
                            e.drain()
                            e.nop().then_inc(sT.h, 1)
                            sT.n += 1
                        sec("tensor", _t)
                        t_y = sT.n

                        def _v(e, t=t, grp=grp, gpar=gpar, t_y=t_y):
                            e.wait_ge(sT.h, t_y)
                            e.tensor_tensor(
                                out=x_sb[:, t * D + grp * 512:
                                         t * D + grp * 512 + 512],
                                in0=x_sb[:, t * D + grp * 512:
                                         t * D + grp * 512 + 512],
                                in1=pyy[gpar][:], op=ALU.add)
                            e.drain()
                            e.nop().then_inc(sV.h, 1)
                            sV.n += 1
                        sec("vector", _v)

        _ORDER = ["init", "ln1", "qkv0", "attn0", "res0", "ffn",
                  "ln3", "qkv1", "attn1", "res1", "moe", "all"]

        def want(name):
            return _ORDER.index(name) <= _ORDER.index(stop_after)

        if want("ln1"):
            emit_ln("L1", ag_in_d[0], None)
            emit_collective("AllGather", ALU.bypass, ag_in_d[0], ag_out_d[0])
            barrier()
        if want("qkv0"):
            emit_qkv(0)
            barrier()
        if want("attn0"):
            emit_attn(0)
            barrier()
        if want("res0"):
            emit_residual(0, add_bias=False)
            barrier()
        if want("ffn"):
            emit_ffn()
            barrier()

        ph_l1 = contextlib.ExitStack()
        with ph_l1:
            if want("ln3"):
                xT3 = ph_l1.enter_context(
                    nc.sbuf_tensor("xT3", [128, KC * T], BF16))
                emit_ln("L3", ag_in_d[1], xT3)
                emit_collective("AllGather", ALU.bypass, ag_in_d[1],
                                ag_out_d[1])
                emit_lat(xT3)
                emit_collective("AllGather", ALU.bypass, agl_in, agl_out)
                barrier()
            if want("qkv1"):
                emit_qkv(1)
                barrier()
        if want("attn1"):
            emit_attn(1)
            barrier()
        if want("res1"):
            emit_residual(1, add_bias=True)
            barrier()
        if want("moe"):
            emit_moe()
            barrier()

        def _wout(e):
            e.dma_start(out=out_t.rearrange("(t p) d -> p t d", p=128),
                        in_=x_sb[:].rearrange("p (t d) -> p t d", d=D)
                        ).then_inc(sW.h, 16)
            sW.n += 16
            e.wait_ge(sW.h, sW.n)
        sec("sync", _wout)

    return nc


def _host_constants():
    pos = np.arange(S, dtype=np.float32)
    j64 = np.arange(64, dtype=np.float32)
    inv = (np.float32(10000.0) ** (-j64 / np.float32(64.0))).astype(np.float32)
    theta = (pos[None, :] * inv[:, None]).astype(np.float32)
    cos = np.cos(theta).astype(np.float32)
    sin = np.sin(theta).astype(np.float32)
    c0tmh = np.zeros((128, 2048), dtype=np.float32)
    s0tmh = np.zeros((128, 2048), dtype=np.float32)
    for pb in range(32):
        c0tmh[:, pb * 64:(pb + 1) * 64] = cos[:, pb * 128:(pb + 1) * 128].T
        s0tmh[:, pb * 64:(pb + 1) * 64] = sin[:, pb * 128:(pb + 1) * 128].T
    ident = np.eye(128, dtype=np.float32)
    onesM = np.ones((128, 128), dtype=np.float32)
    onescol = np.ones((128, 64), dtype=np.float32)
    epsv = np.broadcast_to(
        np.array([1e-5, 1e-6, -C_BOUND, 0.0], np.float32)[None, :],
        (128, 4)).copy()
    maskT = np.zeros((128, 4 * 512), dtype=np.float32)
    qpos = np.arange(512)
    for m in range(4):
        maskT[:, m * 512:(m + 1) * 512] = (
            qpos[None, :] >= (m * 128 + np.arange(128))[:, None]
        ).astype(np.float32)
    return c0tmh, s0tmh, ident, onesM, onescol, epsv, maskT


def kernel(L0_attn_x, L0_attn_w_qkv, L0_attn_w_out, L0_attn_q_gain,
           L0_ffn_wq_gate, L0_ffn_s_gate, L0_ffn_wq_up, L0_ffn_s_up,
           L0_ffn_wq_down, L0_ffn_s_down, L1_attn_freqs, L1_attn_wq_w,
           L1_attn_wq_b, L1_attn_wkv_a_w, L1_attn_wkv_b_w, L1_attn_norm_w,
           L1_attn_out_w, L1_attn_out_b, L1_ffn_A, L1_ffn_Bw, L1_ffn_Wg):
    def hnp(a, dt):
        a = np.asarray(a)
        if a.dtype != dt:
            a = a.astype(dt)
        return a

    c0, s0, ident, onesM, onescol, epsv, maskT = _host_constants()
    x_full = np.ascontiguousarray(
        hnp(L0_attn_x, np.float32).reshape(8 * T, D))
    wqkv = hnp(L0_attn_w_qkv, np.float32)
    wout0 = hnp(L0_attn_w_out, np.float32)
    qgain = hnp(L0_attn_q_gain, np.float32)
    wgTh = np.ascontiguousarray(hnp(L0_ffn_wq_gate, np.int32).T.astype(np.uint8))
    wuTh = np.ascontiguousarray(hnp(L0_ffn_wq_up, np.int32).T.astype(np.uint8))
    wdTh = np.ascontiguousarray(hnp(L0_ffn_wq_down, np.int32).T.astype(np.uint8))
    sg = hnp(L0_ffn_s_gate, np.float32)
    su = hnp(L0_ffn_s_up, np.float32)
    sd = hnp(L0_ffn_s_down, np.float32)
    freqs = hnp(L1_attn_freqs, np.float32)
    wq1 = hnp(L1_attn_wq_w, np.float32)
    wqb = hnp(L1_attn_wq_b, np.float32)
    wkva = hnp(L1_attn_wkv_a_w, np.float32)
    wkvb = hnp(L1_attn_wkv_b_w, np.float32)
    normw = hnp(L1_attn_norm_w, np.float32)
    wout1 = hnp(L1_attn_out_w, np.float32)
    outb = hnp(L1_attn_out_b, np.float32)
    A = hnp(L1_ffn_A, np.float32)
    Bw = hnp(L1_ffn_Bw, np.float32)
    Wg = hnp(L1_ffn_Wg, np.float32)

    sg_pp = np.ascontiguousarray(sg.reshape(HC, 128).T)
    su_pp = np.ascontiguousarray(su.reshape(HC, 128).T)
    sd_pp = np.ascontiguousarray(sd.reshape(KC, 128).T)
    normwtm = np.ascontiguousarray(
        np.broadcast_to(normw[None, :], (128, 512)))
    wkvaT = np.ascontiguousarray(wkva.T)
    wgA = np.ascontiguousarray(
        np.concatenate([A.reshape(512, D).T, Wg.T], axis=1))
    bwT = np.ascontiguousarray(Bw.transpose(0, 2, 1).reshape(512, D))
    outb_bc = np.ascontiguousarray(np.broadcast_to(outb[None, :], (128, D)))

    in_maps = []
    for c in range(8):
        h0 = 2 * c
        cols = []
        for t3 in range(3):
            for hh in range(2):
                base = t3 * D + (h0 + hh) * 128
                cols.append(wqkv[:, base:base + 128])
        wqkv_myc = np.ascontiguousarray(np.concatenate(cols, axis=1))
        wout0_myc = np.ascontiguousarray(wout0[256 * c:256 * c + 256, :])
        gain2 = np.ascontiguousarray(
            np.broadcast_to(qgain[h0:h0 + 2][None, :], (128, 2)))
        wq1_myc = np.ascontiguousarray(wq1[256 * c:256 * c + 256, :].T)
        wqbtm = np.zeros((128, 1024), dtype=np.float32)
        for hh in range(2):
            bvec = wqb[(h0 + hh) * 128:(h0 + hh + 1) * 128]
            wqbtm[:, hh * 512: hh * 512 + 512] = np.broadcast_to(
                np.tile(bvec, 4)[None, :], (128, 512))
        krows = wkvb[h0 * 128:(h0 + 2) * 128, :]
        vrows = wkvb[2048 + h0 * 128: 2048 + (h0 + 2) * 128, :]
        wkvb_myc = np.ascontiguousarray(
            np.concatenate([krows, vrows], axis=0).T)
        wout1_myc = np.ascontiguousarray(wout1[:, 256 * c:256 * c + 256].T)
        c2tm = np.zeros((128, 256), dtype=np.float32)
        s2tm = np.zeros((128, 256), dtype=np.float32)
        for hh in range(2):
            h = h0 + hh
            crow = freqs[h, :, 0]
            srow = freqs[h, :, 1]
            c2tm[:, hh * 128: hh * 128 + 128] = np.broadcast_to(
                np.tile(crow, 4)[None, :], (128, 128))
            s2tm[:, hh * 128: hh * 128 + 128] = np.broadcast_to(
                np.tile(srow, 4)[None, :], (128, 128))
        in_maps.append(dict(
            x=np.ascontiguousarray(x_full[c * T:(c + 1) * T]),
            wqkv_my=wqkv_myc, wout0_my=wout0_myc, gain2=gain2,
            c0tm=c0, s0tm=s0, ident=ident, onesM=onesM, onescol=onescol,
            epsv=epsv, maskT=maskT, wgT=wgTh, wuT=wuTh, wdT=wdTh,
            sg_pp=sg_pp, su_pp=su_pp, sd_pp=sd_pp,
            wq1_my=wq1_myc, wqbtm=wqbtm, wkvaT=wkvaT, normwtm=normwtm,
            wkvb_my=wkvb_myc, wout1_my=wout1_myc, outb_bc=outb_bc,
            c2tm=c2tm, s2tm=s2tm, wgA=wgA, bwT=bwT))

    nc = build_nc()
    res = run_bass_kernel_spmd(nc, in_maps, core_ids=list(range(8)))
    out = np.concatenate([np.asarray(res.results[c]["out"])
                          for c in range(8)], axis=0)
    return out.reshape(B, S, D).astype(np.float32)


if __name__ == "__main__":
    build_nc()
    print("build ok")


# version 13
# speedup vs baseline: 1.1893x; 1.1893x over previous
import math
import contextlib
import os
import numpy as np
import ml_dtypes
import concourse.bass as bass
import concourse.mybir as mybir
from concourse.bass_utils import run_bass_kernel_spmd

F32 = mybir.dt.float32
BF16 = mybir.dt.bfloat16
I32 = mybir.dt.int32
U8 = mybir.dt.uint8
ACT = mybir.ActivationFunctionType
ALU = mybir.AluOpType
AX = mybir.AxisListType

B = 2
S = 4096
D = 2048
T = 1024
NT = 8
KC = 16
HID = 5632
HC = 44
DL = 512
NTS = 16
INV_SQRT_DH = 1.0 / math.sqrt(128.0)
C_BOUND = 40.0 * math.sqrt(128.0)


def build_nc(stop_after="all"):
    nc = bass.Bass("TRN2", target_bir_lowering=False, debug=False)

    def din(name, shape, dt=F32):
        return nc.dram_tensor(name, shape, dt, kind="ExternalInput").ap()

    def dint(name, shape, dt=BF16):
        return nc.dram_tensor(name, shape, dt, kind="Internal").ap()

    x_in = din("x", [T, D])
    wqkv_my = din("wqkv_my", [D, 768])
    wout0_my = din("wout0_my", [256, D])
    gain2 = din("gain2", [128, 2])
    c0tm = din("c0tm", [128, 2048])
    s0tm = din("s0tm", [128, 2048])
    ident_in = din("ident", [128, 128])
    onesM_in = din("onesM", [128, 128])
    onescol_in = din("onescol", [128, 64])
    epsv_in = din("epsv", [128, 4])
    maskT_in = din("maskT", [128, 4 * 512])
    wgT_in = din("wgT", [D, HID], U8)
    wuT_in = din("wuT", [D, HID], U8)
    wdT_in = din("wdT", [HID, D], U8)
    sg_in = din("sg_pp", [128, HC])
    su_in = din("su_pp", [128, HC])
    sd_in = din("sd_pp", [128, KC])
    wq1_my = din("wq1_my", [D, 256])
    wqbtm_in = din("wqbtm", [128, 1024])
    wkvaT_in = din("wkvaT", [D, DL])
    normwtm_in = din("normwtm", [128, 512])
    wkvb_my = din("wkvb_my", [DL, 512])
    wout1_my = din("wout1_my", [256, D])
    outb_in = din("outb_bc", [128, D])
    c2tm_in = din("c2tm", [128, 256])
    s2tm_in = din("s2tm", [128, 256])
    wgA_in = din("wgA", [D, 520])
    bwT_in = din("bwT", [512, D])
    out_t = nc.dram_tensor("out", [T, D], F32, kind="ExternalOutput").ap()

    ag_in_d = [dint("ag0_in", [D, T]), dint("ag1_in", [D, T])]
    ag_out_d = [dint("ag0_out", [8 * D, T]), dint("ag1_out", [8 * D, T])]
    agl_in = dint("agl_in", [DL, T])
    agl_out = dint("agl_out", [8 * DL, T])
    rs_in_d = [dint("rs0_in", [8 * T, D]), dint("rs1_in", [8 * T, D])]
    rs_out_d = [dint("rs0_out", [T, D]), dint("rs1_out", [T, D])]
    sc_q = [dint("scq0", [256, 8 * T]), dint("scq1", [256, 8 * T])]
    sc_k = [dint("sck0", [256, 8 * T]), dint("sck1", [256, 8 * T])]
    sc_v = [dint("scv0", [2 * 8 * T, 128]), dint("scv1", [2 * 8 * T, 128])]

    es = contextlib.ExitStack()
    with es:
        sb = lambda n, s, d: es.enter_context(nc.sbuf_tensor(n, s, d))
        x_sb = sb("x_sb", [128, NT * D], F32)
        ident_sb = sb("ident_sb", [128, 128], BF16)
        onesM = sb("onesM_sb", [128, 128], BF16)
        gain_sb = sb("gain_sb", [128, 2], F32)
        nag_sb = sb("nag_sb", [128, 2], F32)
        eps_sb = sb("eps_sb", [128, 4], F32)
        sg_sb = sb("sg_sb", [128, HC], F32)
        su_sb = sb("su_sb", [128, HC], F32)
        sd_sb = sb("sd_sb", [128, KC], F32)

        class Ctr:
            def __init__(self, h):
                self.h = h
                self.n = 0

        sem = {k: Ctr(es.enter_context(nc.semaphore(k)))
               for k in ("sW", "sG", "sC", "sT", "sS", "sV",
                         "bW", "bG", "bT", "bS", "bV",
                         "pS0", "pS1", "sX0", "sX1", "sQ0", "sQ1",
                         "sU0", "sU1", "sL0", "sL1", "sP0", "sP1",
                         "sF0", "sF1", "sFu")}
        sW, sG, sC = sem["sW"], sem["sG"], sem["sC"]
        sT, sS, sV = sem["sT"], sem["sS"], sem["sV"]
        pS = [sem["pS0"], sem["pS1"]]
        sX = [sem["sX0"], sem["sX1"]]
        sQ = [sem["sQ0"], sem["sQ1"]]
        sU = [sem["sU0"], sem["sU1"]]
        sL = [sem["sL0"], sem["sL1"]]
        sP = [sem["sP0"], sem["sP1"]]
        sF = [sem["sF0"], sem["sF1"]]
        sFu = sem["sFu"]
        SYNC_DMA_SEMS = [sW] + pS + sX + sQ + sU + sL + sP
        GP_DMA_SEMS = [sG] + sF + [sFu]
        BARS = [sem[k] for k in ("bW", "bG", "bT", "bS", "bV")]
        ENG = ("sync", "gpsimd", "tensor", "scalar", "vector")

        block = es.enter_context(nc.Block())

        _DATA_OPS = frozenset((
            "matmul", "transpose", "tensor_tensor", "tensor_copy",
            "tensor_scalar", "activation", "reciprocal", "reduce_sum",
            "reduce_max", "scalar_tensor_tensor"))

        class _IW:
            def __init__(self, i, o):
                self.i = i
                self.o = o

            def then_inc(self, sem, value=1):
                self.i.then_inc(sem, value)
                self.o._has_inc = True
                return self

            def __getattr__(self, k):
                return getattr(self.i, k)

        class _EP:
            def __init__(self, e, elide_solo=False):
                self._e = e
                self._last = None
                self._has_inc = False
                self._pend = False
                self._elide = elide_solo

            def _flush(self):
                if self._pend:
                    if not self._elide:
                        self._e.drain()
                    self._pend = False

            def drain(self, *a, **k):
                self._pend = True
                return None

            def nop(self, *a, **k):
                if (self._pend and self._last is not None
                        and not self._has_inc):
                    self._pend = False
                    return _IW(self._last, self)
                self._flush()
                i = self._e.nop()
                self._last = i
                self._has_inc = False
                return _IW(i, self)

            def sem_inc(self, sem, val=1):
                self._pend = False
                self._e.drain()
                self._last = None
                return self._e.sem_inc(sem, val)

            def wait_ge(self, *a, **kw):
                self._flush()
                return self._e.wait_ge(*a, **kw)

            def __getattr__(self, k):
                f = getattr(self._e, k)
                if k in _DATA_OPS:
                    def g(*a, _f=f, **kw):
                        self._flush()
                        i = _f(*a, **kw)
                        self._last = i
                        self._has_inc = False
                        return _IW(i, self)
                    return g
                if callable(f):
                    def h(*a, _f=f, **kw):
                        self._flush()
                        return _f(*a, **kw)
                    return h
                return f

        _proxies = {}

        def sec(name, fn):
            if name in ("tensor", "scalar", "vector"):
                def wrapped(e, fn=fn, name=name):
                    p = _proxies.get(name)
                    if p is None:
                        p = _EP(e, elide_solo=name == "scalar")
                        _proxies[name] = p
                    elif p._e is not e:
                        p._e = e
                    fn(p)
                getattr(block, name)(wrapped)
            else:
                getattr(block, name)(fn)

        def barrier():
            def _w(e):
                for s_ in SYNC_DMA_SEMS:
                    e.wait_ge(s_.h, s_.n)
                e.sem_inc(BARS[0].h, 1)
            def _g(e):
                for s_ in GP_DMA_SEMS:
                    e.wait_ge(s_.h, s_.n)
                e.wait_ge(sC.h, sC.n)
                e.sem_inc(BARS[1].h, 1)
            sec("sync", _w)
            sec("gpsimd", _g)
            sec("tensor", lambda e: e.sem_inc(BARS[2].h, 1))
            sec("scalar", lambda e: e.sem_inc(BARS[3].h, 1))
            sec("vector", lambda e: e.sem_inc(BARS[4].h, 1))
            for b_ in BARS:
                b_.n += 1
            tg = [(b_.h, b_.n) for b_ in BARS]
            for nm in ENG:
                def _wt(e, tg=tg):
                    for h, v in tg:
                        e.wait_ge(h, v)
                sec(nm, _wt)

        def _p0w(e):
            e.dma_start(out=x_sb[:].rearrange("p (t d) -> p t d", d=D),
                        in_=x_in.rearrange("(t p) d -> p t d", p=128)
                        ).then_inc(sW.h, 16)
            sW.n += 16
            for dst, src in ((gain_sb, gain2), (eps_sb, epsv_in),
                             (sg_sb, sg_in), (su_sb, su_in),
                             (sd_sb, sd_in)):
                e.dma_start(out=dst[:], in_=src).then_inc(sW.h, 16)
                sW.n += 16
        sec("sync", _p0w)
        w_small = sW.n

        def _p0g(e):
            for dst, src in ((ident_sb, ident_in), (onesM, onesM_in)):
                e.dma_start(out=dst[:], in_=src).then_inc(sG.h, 16)
                sG.n += 16
        sec("gpsimd", _p0g)

        def _p0s(e):
            e.wait_ge(sW.h, w_small)
            e.activation(out=nag_sb[:], in_=gain_sb[:], func=ACT.Abs
                         ).then_inc(sS.h, 1)
            sS.n += 1
        sec("scalar", _p0s)
        s_abs = sS.n

        def _p0v(e):
            e.wait_ge(sS.h, s_abs)
            e.tensor_scalar(out=nag_sb[:], in0=nag_sb[:], scalar1=-1.0,
                            scalar2=None, op0=ALU.mult)
            e.drain()
        sec("vector", _p0v)
        barrier()

        def emit_ln(tag, ag_ap, xT_sb):
            ph = contextlib.ExitStack()
            with ph:
                lsb = lambda n, s, d: ph.enter_context(
                    nc.sbuf_tensor(tag + n, s, d))
                xh = lsb("xh", [128, NT * D], BF16)
                sqx = lsb("sqx", [128, 2 * D], BF16)
                st8 = lsb("st8", [128, 8], F32)
                sq8 = lsb("sq8", [128, 8], F32)
                m8 = lsb("m8", [128, 8], F32)
                v8 = lsb("v8", [128, 8], F32)
                r8 = lsb("r8", [128, 8], F32)
                r8i = lsb("r8i", [128, 8], F32)
                stg = lsb("stg", [128, 2 * T], BF16) if ag_ap is not None else None
                ptr = ph.enter_context(
                    nc.psum_tensor(tag + "ptr", [128, 2 * T], BF16))

                def _s(e):
                    for t in range(NT):
                        e.activation(out=sqx[:, (t % 2) * D:(t % 2) * D + D],
                                     in_=x_sb[:, t * D:(t + 1) * D],
                                     func=ACT.Square,
                                     accum_out=sq8[:, t:t + 1])
                        e.drain()
                    e.nop().then_inc(sS.h, 1)
                    sS.n += 1
                sec("scalar", _s)
                s_sq = sS.n

                def _v(e):
                    for t in range(NT):
                        e.reduce_sum(out=st8[:, t:t + 1],
                                     in_=x_sb[:, t * D:(t + 1) * D], axis=AX.X)
                    e.drain()
                    e.tensor_scalar(out=m8[:], in0=st8[:], scalar1=1.0 / D,
                                    scalar2=None, op0=ALU.mult)
                    e.drain()
                    e.tensor_tensor(out=v8[:], in0=m8[:], in1=m8[:],
                                    op=ALU.mult)
                    e.drain()
                    e.wait_ge(sS.h, s_sq)
                    e.scalar_tensor_tensor(out=v8[:], in0=sq8[:],
                                           scalar=1.0 / D, in1=v8[:],
                                           op0=ALU.mult, op1=ALU.subtract)
                    e.drain()
                    e.nop().then_inc(sV.h, 1)
                    sV.n += 1
                sec("vector", _v)
                v_var = sV.n

                def _s2(e):
                    e.wait_ge(sV.h, v_var)
                    e.activation(out=r8[:], in_=v8[:], func=ACT.Sqrt,
                                 bias=eps_sb[:, 0:1]).then_inc(sS.h, 1)
                    sS.n += 1
                sec("scalar", _s2)
                s_r = sS.n

                def _v2(e):
                    e.wait_ge(sS.h, s_r)
                    e.reciprocal(out=r8i[:], in_=r8[:])
                    e.drain()
                    for t in range(NT):
                        e.tensor_scalar(out=xh[:, t * D:(t + 1) * D],
                                        in0=x_sb[:, t * D:(t + 1) * D],
                                        scalar1=m8[:, t:t + 1],
                                        scalar2=r8i[:, t:t + 1],
                                        op0=ALU.subtract, op1=ALU.mult)
                    e.drain()
                    e.nop().then_inc(sV.h, 1)
                    sV.n += 1
                sec("vector", _v2)
                v_xh = sV.n

                evac_done = [0, 0]
                dma_done = [0, 0]
                for c in range(KC):
                    half = (c % 2) * T

                    def _t(e, c=c, half=half):
                        if c == 0:
                            e.wait_ge(sV.h, v_xh)
                        if c >= 2:
                            e.wait_ge(sV.h, evac_done[c % 2])
                        for t in range(NT):
                            e.transpose(
                                ptr[:, half + t * 128: half + t * 128 + 128],
                                xh[:, t * D + c * 128: t * D + c * 128 + 128],
                                ident_sb[:])
                        e.drain()
                        e.nop().then_inc(sT.h, 1)
                        sT.n += 1
                    sec("tensor", _t)
                    t_tr = sT.n

                    def _ve(e, c=c, half=half, t_tr=t_tr):
                        e.wait_ge(sT.h, t_tr)
                        if ag_ap is not None and c >= 2:
                            e.wait_ge(pS[c % 2].h, dma_done[c % 2])
                        if xT_sb is not None:
                            e.tensor_copy(out=xT_sb[:, c * T:(c + 1) * T],
                                          in_=ptr[:, half:half + T])
                        if ag_ap is not None:
                            e.tensor_copy(out=stg[:, half:half + T],
                                          in_=ptr[:, half:half + T])
                        e.drain()
                        e.nop().then_inc(sV.h, 1)
                        sV.n += 1
                        evac_done[c % 2] = sV.n
                    sec("vector", _ve)
                    v_e = sV.n

                    if ag_ap is not None:
                        def _w(e, c=c, half=half, v_e=v_e):
                            e.wait_ge(sV.h, v_e)
                            e.dma_start(out=ag_ap[c * 128:(c + 1) * 128, :],
                                        in_=stg[:, half:half + T]
                                        ).then_inc(pS[c % 2].h, 16)
                            pS[c % 2].n += 16
                            dma_done[c % 2] = pS[c % 2].n
                        sec("sync", _w)

        def emit_collective(kind, op, in_ap, out_ap):
            wvs = [(s_.h, s_.n) for s_ in SYNC_DMA_SEMS]

            def _g(e):
                for h_, v_ in wvs:
                    e.wait_ge(h_, v_)
                e.collective_compute(kind, op,
                                     replica_groups=[[0, 1, 2, 3, 4, 5, 6, 7]],
                                     ins=[in_ap], outs=[out_ap]
                                     ).then_inc(sC.h, 1)
            sec("gpsimd", _g)
            sC.n += 1

        def emit_qkv(layer):
            P = f"q{layer}"
            ph = contextlib.ExitStack()
            with ph:
                lsb = lambda n, s, d: ph.enter_context(
                    nc.sbuf_tensor(P + n, s, d))
                psb = lambda n, s, d: ph.enter_context(
                    nc.psum_tensor(P + n, s, d))
                if layer == 0:
                    W = lsb("W", [128, KC * 768], BF16)
                    c0_sb = lsb("c0", [128, 2048], F32)
                    s0_sb = lsb("s0", [128, 2048], F32)
                else:
                    W = lsb("W", [128, KC * 256], BF16)
                    Wkv = lsb("Wkv", [128, 4 * 512], BF16)
                    lg = lsb("lg", [128, 2 * 4 * 512], BF16)
                    wqbtm = lsb("wqb", [128, 1024], F32)
                    c2_sb = lsb("c2", [128, 256], F32)
                    s2_sb = lsb("s2", [128, 256], F32)
                xg = lsb("xg", [128, 2 * KC * 512], BF16)
                raw = lsb("raw", [128, 512], BF16)
                sq = lsb("sq", [128, 512], BF16)
                ss4 = lsb("ss4", [128, 4], F32)
                sd4 = lsb("sd4", [128, 4], F32)
                ri4 = lsb("ri4", [128, 4], F32)
                qn = lsb("qn", [128, 512], BF16)
                ro = lsb("ro", [128, 512], BF16)
                tA = lsb("tA", [128, 256], F32)
                tB = lsb("tB", [128, 256], F32)
                qo = lsb("qo", [128, 2 * 4 * 512], BF16)
                vf = lsb("vf", [128, 2 * 2 * 512], BF16)
                vt = lsb("vt", [128, 2 * 2 * 512], BF16)
                pq = [psb(f"pq{i}", [128, 512], F32) for i in range(2)]
                ptm = psb("ptm", [128, 512], BF16)
                pfm = psb("pfm", [128, 512], BF16)
                pvt = [psb(f"pvt{i}", [128, 512], BF16) for i in range(2)]

                def _g(e):
                    if layer == 0:
                        e.dma_start(
                            out=W[:].rearrange("p (k o) -> p k o", o=768),
                            in_=wqkv_my.rearrange("(k p) o -> p k o", p=128)
                        ).then_inc(sG.h, 16)
                        sG.n += 16
                    else:
                        e.dma_start(
                            out=W[:].rearrange("p (k o) -> p k o", o=256),
                            in_=wq1_my.rearrange("(k p) o -> p k o", p=128)
                        ).then_inc(sG.h, 16)
                        e.dma_start(
                            out=Wkv[:].rearrange("p (k o) -> p k o", o=512),
                            in_=wkvb_my.rearrange("(k p) o -> p k o", p=128)
                        ).then_inc(sG.h, 16)
                        sG.n += 32
                sec("gpsimd", _g)
                g_w = sG.n

                def _w0(e):
                    if layer == 0:
                        for dst, src in ((c0_sb, c0tm), (s0_sb, s0tm)):
                            e.dma_start(out=dst[:], in_=src).then_inc(sW.h, 16)
                            sW.n += 16
                    else:
                        for dst, src in ((wqbtm, wqbtm_in), (c2_sb, c2tm_in),
                                         (s2_sb, s2tm_in)):
                            e.dma_start(out=dst[:], in_=src).then_inc(sW.h, 16)
                            sW.n += 16
                sec("sync", _w0)
                w_tabs = sW.n

                ag_out = ag_out_d[layer]
                scq, sck, scv = sc_q[layer], sc_k[layer], sc_v[layer]
                ocw = 768 if layer == 0 else 256

                xg_ready = [0, 0]
                lg_ready = [0, 0]
                xg_free = [0, 0]
                qo_free = [0, 0]
                vt_free = [0, 0]
                pvt_free = [0, 0]
                raw_free = [0]
                pfm_free = [0]
                evacq = [0, 0]
                mm_done = [0] * 6

                for ts in range(NTS):
                    par = ts % 2
                    xoff = par * KC * 512

                    def _w(e, ts=ts, par=par, xoff=xoff):
                        e.wait_ge(sT.h, xg_free[par])
                        for kc in range(KC):
                            e.dma_start(
                                out=xg[:, xoff + kc * 512:
                                       xoff + kc * 512 + 512],
                                in_=ag_out[2048 * (ts // 2) + kc * 128:
                                           2048 * (ts // 2) + kc * 128 + 128,
                                           512 * (ts % 2):
                                           512 * (ts % 2) + 512]
                            ).then_inc(sX[par].h, 16)
                            sX[par].n += 16
                        if layer == 1:
                            for lc in range(4):
                                e.dma_start(
                                    out=lg[:, par * 2048 + lc * 512:
                                           par * 2048 + lc * 512 + 512],
                                    in_=agl_out[512 * (ts // 2) + lc * 128:
                                                512 * (ts // 2) + lc * 128 + 128,
                                                512 * (ts % 2):
                                                512 * (ts % 2) + 512]
                                ).then_inc(sX[par].h, 16)
                                sX[par].n += 16
                        xg_ready[par] = sX[par].n
                        lg_ready[par] = sX[par].n
                    sec("sync", _w)

                    if os.environ.get("QSKIP") == "5":
                        def _t5(e, par=par):
                            e.wait_ge(sW.h, xg_ready[par])
                            e.drain()
                            e.nop().then_inc(sT.h, 1)
                            sT.n += 1
                        sec("tensor", _t5)
                        xg_free[par] = sT.n
                        continue
                    for oc in range(6):
                        def _t(e, ts=ts, oc=oc, par=par, xoff=xoff):
                            e.wait_ge(sG.h, g_w)
                            e.wait_ge(sW.h, w_tabs)
                            e.wait_ge(sX[par].h, xg_ready[par])
                            e.wait_ge(sV.h, evacq[oc % 2])
                            if layer == 0 or oc < 2:
                                for kc in range(KC):
                                    e.matmul(
                                        pq[oc % 2][:],
                                        W[:, kc * ocw + oc * 128:
                                          kc * ocw + oc * 128 + 128],
                                        xg[:, xoff + kc * 512:
                                           xoff + kc * 512 + 512],
                                        start=(kc == 0), stop=(kc == KC - 1))
                            else:
                                kvoc = oc - 2
                                for lc in range(4):
                                    e.matmul(
                                        pq[oc % 2][:],
                                        Wkv[:, lc * 512 + kvoc * 128:
                                            lc * 512 + kvoc * 128 + 128],
                                        lg[:, par * 2048 + lc * 512:
                                           par * 2048 + lc * 512 + 512],
                                        start=(lc == 0), stop=(lc == 3))
                            e.drain()
                            e.nop().then_inc(sT.h, 1)
                            sT.n += 1
                        sec("tensor", _t)
                        mm_done[oc] = sT.n
                        if oc == 5:
                            xg_free[par] = sT.n
                        if os.environ.get("QSKIP") == "4":
                            def _v4(e, oc=oc):
                                e.wait_ge(sT.h, mm_done[oc])
                                e.drain()
                                e.nop().then_inc(sV.h, 1)
                                sV.n += 1
                                evacq[oc % 2] = sV.n
                            sec("vector", _v4)
                            continue

                        if oc < 4:
                            hh = oc % 2
                            is_q = oc < 2
                            do_rope = (layer == 1) or (hh == 0)
                            if os.environ.get("QSKIP") in ("1", "3", "6"):
                                qoff = par * 2048 + (oc % 4) * 512

                                def _vz(e, oc=oc, par=par, qoff=qoff):
                                    e.wait_ge(sT.h, mm_done[oc])
                                    e.wait_ge(sQ[par].h, qo_free[par])
                                    e.tensor_copy(out=qo[:, qoff:qoff + 512],
                                                  in_=pq[oc % 2][:])
                                    e.drain()
                                    e.nop().then_inc(sV.h, 1)
                                    sV.n += 1
                                    evacq[oc % 2] = sV.n
                                sec("vector", _vz)
                                v_qo = sV.n
                                dst = scq if is_q else sck
                                if os.environ.get("QSKIP") == "6":
                                    qo_free[par] = 0
                                    continue

                                def _wz(e, ts=ts, oc=oc, hh=hh, dst=dst,
                                        par=par, qoff=qoff, v_qo=v_qo):
                                    e.wait_ge(sV.h, v_qo)
                                    e.dma_start(
                                        out=dst[hh * 128:(hh + 1) * 128,
                                                ts * 512: ts * 512 + 512],
                                        in_=qo[:, qoff:qoff + 512]
                                    ).then_inc(sQ[par].h, 16)
                                    sQ[par].n += 16
                                    if oc == 3:
                                        qo_free[par] = sQ[par].n
                                sec("sync", _wz)
                                continue

                            def _v(e, oc=oc):
                                e.wait_ge(sT.h, mm_done[oc])
                                e.wait_ge(sT.h, raw_free[0])
                                e.tensor_copy(out=raw[:], in_=pq[oc % 2][:])
                                e.drain()
                                e.nop().then_inc(sV.h, 1)
                                sV.n += 1
                                evacq[oc % 2] = sV.n
                            sec("vector", _v)
                            v_raw = sV.n

                            def _t2(e, v_raw=v_raw):
                                e.wait_ge(sV.h, v_raw)
                                for q in range(4):
                                    e.transpose(
                                        ptm[:, q * 128: q * 128 + 128],
                                        raw[:, q * 128: q * 128 + 128],
                                        ident_sb[:])
                                e.drain()
                                e.nop().then_inc(sT.h, 1)
                                sT.n += 1
                            sec("tensor", _t2)
                            t_tm = sT.n
                            raw_free[0] = t_tm

                            if layer == 0:
                                def _vn(e, t_tm=t_tm):
                                    e.wait_ge(sT.h, t_tm)
                                    e.wait_ge(sS.h, sS.n)
                                    e.tensor_copy(out=qn[:], in_=ptm[:])
                                    e.drain()
                                    e.tensor_tensor(out=sq[:], in0=qn[:],
                                                    in1=qn[:], op=ALU.mult)
                                    e.drain()
                                    e.reduce_sum(
                                        out=ss4[:],
                                        in_=sq[:].rearrange(
                                            "p (q d) -> p q d", d=128),
                                        axis=AX.X)
                                    e.drain()
                                    e.nop().then_inc(sV.h, 1)
                                    sV.n += 1
                                sec("vector", _vn)
                                v_ss = sV.n

                                def _sn(e, v_ss=v_ss):
                                    e.wait_ge(sV.h, v_ss)
                                    e.activation(out=sd4[:], in_=ss4[:],
                                                 func=ACT.Sqrt,
                                                 bias=eps_sb[:, 1:2]
                                                 ).then_inc(sS.h, 1)
                                    e.drain()
                                    sS.n += 1
                                sec("scalar", _sn)
                                s_n = sS.n

                                def _vap(e, ts=ts, do_rope=do_rope, s_n=s_n):
                                    e.wait_ge(sS.h, s_n)
                                    e.reciprocal(out=ri4[:], in_=sd4[:])
                                    e.drain()
                                    for q in range(4):
                                        e.tensor_scalar(
                                            out=qn[:, q * 128: q * 128 + 128],
                                            in0=qn[:, q * 128: q * 128 + 128],
                                            scalar1=ri4[:, q:q + 1],
                                            scalar2=None, op0=ALU.mult)
                                    e.drain()
                                    if do_rope:
                                        po_ = (ts % 8) * 256
                                        cv = c0_sb[:, po_: po_ + 256] \
                                            .rearrange("p (q d) -> p q d", d=64)
                                        sv = s0_sb[:, po_: po_ + 256] \
                                            .rearrange("p (q d) -> p q d", d=64)
                                        qv = qn[:].rearrange(
                                            "p (q h d) -> p q h d", h=2, d=64)
                                        rv = ro[:].rearrange(
                                            "p (q h d) -> p q h d", h=2, d=64)
                                        tAv = tA[:].rearrange(
                                            "p (q d) -> p q d", d=64)
                                        tBv = tB[:].rearrange(
                                            "p (q d) -> p q d", d=64)
                                        a = qv[:, :, 0, :]
                                        bq = qv[:, :, 1, :]
                                        e.tensor_tensor(out=tAv, in0=a,
                                                        in1=cv, op=ALU.mult)
                                        e.tensor_tensor(out=tBv, in0=bq,
                                                        in1=sv, op=ALU.mult)
                                        e.drain()
                                        e.tensor_tensor(out=rv[:, :, 0, :],
                                                        in0=tAv, in1=tBv,
                                                        op=ALU.subtract)
                                        e.drain()
                                        e.tensor_tensor(out=tAv, in0=a,
                                                        in1=sv, op=ALU.mult)
                                        e.tensor_tensor(out=tBv, in0=bq,
                                                        in1=cv, op=ALU.mult)
                                        e.drain()
                                        e.tensor_tensor(out=rv[:, :, 1, :],
                                                        in0=tAv, in1=tBv,
                                                        op=ALU.add)
                                        e.drain()
                                    e.nop().then_inc(sV.h, 1)
                                    sV.n += 1
                                sec("vector", _vap)
                                v_ro = sV.n
                                src_tile = ro if do_rope else qn
                            else:
                                def _vap(e, hh=hh, is_q=is_q, t_tm=t_tm):
                                    e.wait_ge(sT.h, t_tm)
                                    if is_q:
                                        e.tensor_tensor(
                                            out=qn[:], in0=ptm[:],
                                            in1=wqbtm[:, hh * 512:
                                                      hh * 512 + 512],
                                            op=ALU.add)
                                    else:
                                        e.tensor_copy(out=qn[:], in_=ptm[:])
                                    e.drain()
                                    qlo = qn[:].rearrange(
                                        "p (q h d) -> p q h d", h=2, d=64)
                                    rlo = ro[:].rearrange(
                                        "p (q h d) -> p q h d", h=2, d=64)
                                    e.tensor_copy(out=rlo[:, :, 0, :],
                                                  in_=qlo[:, :, 0, :])
                                    e.drain()
                                    qv = qn[:].rearrange(
                                        "p (q h j t) -> p q h j t",
                                        h=2, j=32, t=2)
                                    rv = ro[:].rearrange(
                                        "p (q h j t) -> p q h j t",
                                        h=2, j=32, t=2)
                                    a = qv[:, :, 1, :, 0]
                                    bq = qv[:, :, 1, :, 1]
                                    cv = c2_sb[:, hh * 128: hh * 128 + 128] \
                                        .rearrange("p (q j) -> p q j", j=32)
                                    sv = s2_sb[:, hh * 128: hh * 128 + 128] \
                                        .rearrange("p (q j) -> p q j", j=32)
                                    tAv = tA[:, 0:128].rearrange(
                                        "p (q j) -> p q j", j=32)
                                    tBv = tB[:, 0:128].rearrange(
                                        "p (q j) -> p q j", j=32)
                                    e.tensor_tensor(out=tAv, in0=a, in1=cv,
                                                    op=ALU.mult)
                                    e.tensor_tensor(out=tBv, in0=bq, in1=sv,
                                                    op=ALU.mult)
                                    e.drain()
                                    e.tensor_tensor(out=rv[:, :, 1, :, 0],
                                                    in0=tAv, in1=tBv,
                                                    op=ALU.subtract)
                                    e.drain()
                                    e.tensor_tensor(out=tAv, in0=a, in1=sv,
                                                    op=ALU.mult)
                                    e.tensor_tensor(out=tBv, in0=bq, in1=cv,
                                                    op=ALU.mult)
                                    e.drain()
                                    e.tensor_tensor(out=rv[:, :, 1, :, 1],
                                                    in0=tAv, in1=tBv,
                                                    op=ALU.add)
                                    e.drain()
                                    e.nop().then_inc(sV.h, 1)
                                    sV.n += 1
                                sec("vector", _vap)
                                v_ro = sV.n
                                src_tile = ro

                            def _t3(e, v_ro=v_ro, src_tile=src_tile):
                                e.wait_ge(sV.h, v_ro)
                                e.wait_ge(sV.h, pfm_free[0])
                                for q in range(4):
                                    e.transpose(
                                        pfm[:, q * 128: q * 128 + 128],
                                        src_tile[:, q * 128: q * 128 + 128],
                                        ident_sb[:])
                                e.drain()
                                e.nop().then_inc(sT.h, 1)
                                sT.n += 1
                            sec("tensor", _t3)
                            t_fm = sT.n

                            qoff = par * 2048 + (oc % 4) * 512

                            def _vq(e, par=par, qoff=qoff, t_fm=t_fm):
                                e.wait_ge(sT.h, t_fm)
                                e.wait_ge(sQ[par].h, qo_free[par])
                                e.tensor_copy(out=qo[:, qoff:qoff + 512],
                                              in_=pfm[:])
                                e.drain()
                                e.nop().then_inc(sV.h, 1)
                                sV.n += 1
                                pfm_free[0] = sV.n
                            sec("vector", _vq)
                            v_qo = sV.n

                            dst = scq if is_q else sck

                            def _wq(e, ts=ts, oc=oc, hh=hh, dst=dst, par=par,
                                    qoff=qoff, v_qo=v_qo):
                                e.wait_ge(sV.h, v_qo)
                                e.dma_start(
                                    out=dst[hh * 128:(hh + 1) * 128,
                                            ts * 512: ts * 512 + 512],
                                    in_=qo[:, qoff:qoff + 512]
                                ).then_inc(sQ[par].h, 16)
                                sQ[par].n += 16
                                if oc == 3:
                                    qo_free[par] = sQ[par].n
                            sec("sync", _wq)
                        else:
                            j = oc - 4
                            if os.environ.get("QSKIP") in ("2", "3", "6"):
                                def _vz2(e, oc=oc, j=j, par=par):
                                    e.wait_ge(sT.h, mm_done[oc])
                                    e.tensor_copy(
                                        out=vf[:, par * 1024 + j * 512:
                                                par * 1024 + j * 512 + 512],
                                        in_=pq[oc % 2][:])
                                    e.drain()
                                    e.nop().then_inc(sV.h, 1)
                                    sV.n += 1
                                    evacq[oc % 2] = sV.n
                                sec("vector", _vz2)
                                continue

                            def _v(e, oc=oc, j=j, par=par):
                                e.wait_ge(sT.h, mm_done[oc])
                                e.wait_ge(sV.h, pvt_free[j])
                                e.tensor_copy(
                                    out=vf[:, par * 1024 + j * 512:
                                            par * 1024 + j * 512 + 512],
                                    in_=pq[oc % 2][:])
                                e.drain()
                                e.nop().then_inc(sV.h, 1)
                                sV.n += 1
                                evacq[oc % 2] = sV.n
                            sec("vector", _v)
                            v_vf = sV.n

                            def _t2(e, j=j, par=par, v_vf=v_vf):
                                e.wait_ge(sV.h, v_vf)
                                for q in range(4):
                                    e.transpose(
                                        pvt[j][:, q * 128: q * 128 + 128],
                                        vf[:, par * 1024 + j * 512 + q * 128:
                                           par * 1024 + j * 512 + q * 128 + 128],
                                        ident_sb[:])
                                e.drain()
                                e.nop().then_inc(sT.h, 1)
                                sT.n += 1
                            sec("tensor", _t2)
                            t_vt = sT.n

                            def _v2(e, j=j, par=par, t_vt=t_vt):
                                e.wait_ge(sT.h, t_vt)
                                e.wait_ge(sU[par].h, vt_free[par])
                                e.tensor_copy(
                                    out=vt[:, par * 1024 + j * 512:
                                            par * 1024 + j * 512 + 512],
                                    in_=pvt[j][:])
                                e.drain()
                                e.nop().then_inc(sV.h, 1)
                                sV.n += 1
                                pvt_free[j] = sV.n
                            sec("vector", _v2)
                            v_vt = sV.n

                            def _w2(e, ts=ts, j=j, par=par, v_vt=v_vt):
                                e.wait_ge(sV.h, v_vt)
                                e.dma_start(
                                    out=scv[j * 8192 + ts * 512:
                                            j * 8192 + ts * 512 + 512, :]
                                    .rearrange("(q p) d -> p q d", p=128),
                                    in_=vt[:, par * 1024 + j * 512:
                                           par * 1024 + j * 512 + 512]
                                    .rearrange("p (q d) -> p q d", d=128)
                                ).then_inc(sU[par].h, 16)
                                sU[par].n += 16
                                if j == 1:
                                    vt_free[par] = sU[par].n
                            sec("sync", _w2)

        def emit_attn(layer):
            P = f"a{layer}"
            pho = contextlib.ExitStack()
            with pho:
                y_sb = pho.enter_context(
                    nc.sbuf_tensor(P + "y", [128, 64 * 256], BF16))
                scq, sck, scv = sc_q[layer], sc_k[layer], sc_v[layer]
                rs_in = rs_in_d[layer]

                ph = contextlib.ExitStack()
                with ph:
                    lsb = lambda n, s, d: ph.enter_context(
                        nc.sbuf_tensor(P + n, s, d))
                    psb = lambda n, s, d: ph.enter_context(
                        nc.psum_tensor(P + n, s, d))
                    qs = lsb("qs", [128, 2 * S], BF16)
                    ks = lsb("ks", [128, 2 * S], BF16)
                    vs = lsb("vs", [128, 2 * 32 * 129], BF16)
                    mask_sb = lsb("mask", [128, 4 * 512], BF16)
                    pr = [lsb(f"pr{i}", [128, 512], BF16) for i in range(2)]
                    lp = lsb("lp", [128, 1], F32)
                    rp = lsb("rp", [128, 1], F32)
                    ps = [psb(f"ps{i}", [128, 512], F32) for i in range(2)]
                    py = [psb(f"py{i}", [128, 129], F32) for i in range(4)]
                    if layer == 1:
                        pcb = psb("pcb", [128, 512], F32)
                        q2f = lsb("q2f", [128, 512], BF16)
                        kkm = lsb("kkm", [128, 1], F32)
                        skk = lsb("skk", [128, 1], F32)
                        sqf = lsb("sqf", [128, S], BF16)
                        cbc = lsb("cbc", [128, 2 * S], BF16)

                    def _gm(e):
                        e.dma_start(out=mask_sb[:], in_=maskT_in
                                    ).then_inc(sG.h, 16)
                        for pp_ in range(2):
                            e.dma_start(
                                out=vs[:, pp_ * 32 * 129:
                                       (pp_ + 1) * 32 * 129]
                                .rearrange("p (j c) -> p j c", c=129)
                                [:, :, 128:129],
                                in_=onescol_in[:, 0:32]
                                .rearrange("p (j o) -> p j o", o=1),
                                allow_slow_non_contiguous=True
                            ).then_inc(sG.h, 16)
                        sG.n += 48
                    sec("gpsimd", _gm)
                    g_mask = sG.n

                    seq_free = [0, 0]
                    py_free = [0]
                    s_cbc = 0
                    for si in range(4):
                        b, hh = si // 2, si % 2
                        par = si % 2

                        def _w(e, b=b, hh=hh, par=par, si=si):
                            e.wait_ge(sT.h, seq_free[par])
                            if si == 0:
                                e.wait_ge(sG.h, g_mask)
                            e.dma_start(out=qs[:, par * S: par * S + S],
                                        in_=scq[hh * 128:(hh + 1) * 128,
                                                b * S:(b + 1) * S]
                                        ).then_inc(sL[par].h, 16)
                            e.dma_start(out=ks[:, par * S: par * S + S],
                                        in_=sck[hh * 128:(hh + 1) * 128,
                                                b * S:(b + 1) * S]
                                        ).then_inc(sL[par].h, 16)
                            e.dma_start(
                                out=vs[:, par * 32 * 129:
                                       (par + 1) * 32 * 129]
                                .rearrange("p (j c) -> p j c", c=129)
                                [:, :, 0:128],
                                in_=scv[hh * 8192 + b * S:
                                        hh * 8192 + (b + 1) * S, :]
                                .rearrange("(j p) c -> p j c", p=128)
                            ).then_inc(sL[par].h, 16)
                            sL[par].n += 48
                        sec("sync", _w)
                        w_seq = sL[par].n
                        w_par = par

                        if layer == 1:
                            for i in range(8):
                                def _vk(e, i=i, par=par, w_seq=w_seq):
                                    if i == 0:
                                        e.wait_ge(sL[par].h, w_seq)
                                    e.wait_ge(sT.h, sT.n)
                                    e.tensor_tensor(
                                        out=q2f[:],
                                        in0=ks[:, par * S + i * 512:
                                               par * S + i * 512 + 512],
                                        in1=ks[:, par * S + i * 512:
                                               par * S + i * 512 + 512],
                                        op=ALU.mult)
                                    e.drain()
                                    e.nop().then_inc(sV.h, 1)
                                    sV.n += 1
                                sec("vector", _vk)
                                v_sq = sV.n

                                def _tk(e, v_sq=v_sq):
                                    e.wait_ge(sV.h, v_sq)
                                    e.matmul(pcb[:], onesM[:], q2f[:],
                                             start=True, stop=True)
                                    e.drain()
                                    e.nop().then_inc(sT.h, 1)
                                    sT.n += 1
                                sec("tensor", _tk)
                                t_i = sT.n

                                def _vk2(e, i=i, t_i=t_i):
                                    e.wait_ge(sT.h, t_i)
                                    if i == 0:
                                        e.reduce_max(out=kkm[:], in_=pcb[:],
                                                     axis=AX.X)
                                        e.drain()
                                    else:
                                        e.reduce_max(out=skk[:], in_=pcb[:],
                                                     axis=AX.X)
                                        e.drain()
                                        e.tensor_tensor(out=kkm[:],
                                                        in0=kkm[:],
                                                        in1=skk[:],
                                                        op=ALU.max)
                                        e.drain()
                                    e.nop().then_inc(sV.h, 1)
                                    sV.n += 1
                                sec("vector", _vk2)
                            for i in range(8):
                                def _vq(e, i=i, par=par):
                                    e.wait_ge(sT.h, sT.n)
                                    e.tensor_tensor(
                                        out=q2f[:],
                                        in0=qs[:, par * S + i * 512:
                                               par * S + i * 512 + 512],
                                        in1=qs[:, par * S + i * 512:
                                               par * S + i * 512 + 512],
                                        op=ALU.mult)
                                    e.drain()
                                    e.nop().then_inc(sV.h, 1)
                                    sV.n += 1
                                sec("vector", _vq)
                                v_sq = sV.n

                                def _tq(e, v_sq=v_sq):
                                    e.wait_ge(sV.h, v_sq)
                                    e.wait_ge(sS.h, sS.n)
                                    e.matmul(pcb[:], onesM[:], q2f[:],
                                             start=True, stop=True)
                                    e.drain()
                                    e.nop().then_inc(sT.h, 1)
                                    sT.n += 1
                                sec("tensor", _tq)
                                t_i = sT.n

                                def _sq(e, i=i, t_i=t_i):
                                    e.wait_ge(sT.h, t_i)
                                    e.activation(out=sqf[:, i * 512:
                                                        i * 512 + 512],
                                                 in_=pcb[:], func=ACT.Sqrt
                                                 ).then_inc(sS.h, 1)
                                    e.drain()
                                    sS.n += 1
                                sec("scalar", _sq)
                            v_kkm = sV.n
                            s_sqq = sS.n

                            def _sc(e, v_kkm=v_kkm, s_sqq=s_sqq,
                                    w_par=w_par):
                                e.wait_ge(sV.h, v_kkm)
                                e.wait_ge(sS.h, s_sqq)
                                e.activation(out=skk[:], in_=kkm[:],
                                             func=ACT.Sqrt)
                                e.drain()
                                e.nop().then_inc(sS.h, 1)
                                sS.n += 1
                                e.activation(out=cbc[:, w_par * S:
                                                    w_par * S + S],
                                             in_=sqf[:],
                                             func=ACT.Relu,
                                             scale=skk[:, 0:1],
                                             bias=eps_sb[:, 2:3]
                                             ).then_inc(sS.h, 1)
                                sS.n += 1
                            sec("scalar", _sc)
                        s_cbc = sS.n

                        for qc in range(8):
                            nkc = 4 * qc + 4
                            T0 = sT.n
                            S0 = sS.n
                            V0 = sV.n
                            sub_idx = {}
                            mask_idx = {}
                            vidx = 0
                            for kc in range(nkc):
                                if layer == 1:
                                    vidx += 1
                                    sub_idx[kc] = vidx
                                if kc >= 4 * qc:
                                    vidx += 1
                                    mask_idx[kc] = vidx
                            nvec = vidx

                            def _t(e, qc=qc, nkc=nkc, par=par, T0=T0,
                                   S0=S0, V0=V0, mask_idx=mask_idx,
                                   w_seq=w_seq):
                                e.wait_ge(sL[par].h, w_seq)
                                e.wait_ge(sV.h, V0)
                                e.wait_ge(sV.h, py_free[0])

                                def score(kc):
                                    if kc >= 2:
                                        e.wait_ge(sS.h, S0 + kc - 1)
                                    e.matmul(ps[kc % 2][:],
                                             ks[:, par * S + kc * 128:
                                                par * S + kc * 128 + 128],
                                             qs[:, par * S + qc * 512:
                                                par * S + qc * 512 + 512],
                                             start=True, stop=True)
                                    e.drain()
                                    e.nop().then_inc(sT.h, 1)

                                def pv(kc):
                                    e.wait_ge(sS.h, S0 + kc + 1)
                                    if kc >= 4 * qc:
                                        e.wait_ge(sV.h, V0 + mask_idx[kc])
                                    for qt in range(4):
                                        e.matmul(
                                            py[qt][:],
                                            pr[kc % 2][:, qt * 128:
                                                       qt * 128 + 128],
                                            vs[:, par * 32 * 129 + kc * 129:
                                               par * 32 * 129 + kc * 129 + 129],
                                            start=(kc == 0),
                                            stop=(kc == nkc - 1))
                                    e.drain()
                                    e.nop().then_inc(sT.h, 1)

                                score(0)
                                for kc in range(1, nkc):
                                    score(kc)
                                    pv(kc - 1)
                                pv(nkc - 1)
                                sT.n += 2 * nkc
                            sec("tensor", _t)
                            t_qc_end = sT.n

                            def idx_score(kc):
                                return 1 if kc == 0 else 2 * kc

                            def idx_pv(kc):
                                return 2 * kc + 3 if kc < nkc - 1 else 2 * nkc

                            def _vsm(e, qc=qc, nkc=nkc, par=par, T0=T0,
                                     S0=S0, s_cbc=(s_cbc if layer == 1
                                                   else 0)):
                                if layer == 1:
                                    e.wait_ge(sS.h, s_cbc)
                                for kc in range(nkc):
                                    if layer == 1:
                                        e.wait_ge(sT.h, T0 + idx_score(kc))
                                        e.tensor_tensor(
                                            out=ps[kc % 2][:],
                                            in0=ps[kc % 2][:],
                                            in1=cbc[:, par * S + qc * 512:
                                                    par * S + qc * 512 + 512],
                                            op=ALU.subtract)
                                        e.drain()
                                        e.nop().then_inc(sV.h, 1)
                                    if kc >= 4 * qc:
                                        m = kc - 4 * qc
                                        e.wait_ge(sS.h, S0 + kc + 1)
                                        e.tensor_tensor(
                                            out=pr[kc % 2][:],
                                            in0=pr[kc % 2][:],
                                            in1=mask_sb[:, m * 512:
                                                        m * 512 + 512],
                                            op=ALU.mult)
                                        e.drain()
                                        e.nop().then_inc(sV.h, 1)
                                sV.n += (nkc if layer == 1 else 0) + 4
                            sec("vector", _vsm)

                            def _s(e, nkc=nkc, T0=T0, V0=V0, hh=hh,
                                   sub_idx=sub_idx):
                                for kc in range(nkc):
                                    if layer == 1:
                                        e.wait_ge(sV.h, V0 + sub_idx[kc])
                                    else:
                                        e.wait_ge(sT.h, T0 + idx_score(kc))
                                    if kc >= 2:
                                        e.wait_ge(sT.h, T0 + idx_pv(kc - 2))
                                    if layer == 0:
                                        e.activation(
                                            out=pr[kc % 2][:],
                                            in_=ps[kc % 2][:], func=ACT.Exp,
                                            scale=gain_sb[:, hh:hh + 1],
                                            bias=nag_sb[:, hh:hh + 1]
                                        ).then_inc(sS.h, 1)
                                        e.drain()
                                    else:
                                        e.activation(
                                            out=pr[kc % 2][:],
                                            in_=ps[kc % 2][:], func=ACT.Exp,
                                            scale=INV_SQRT_DH
                                        ).then_inc(sS.h, 1)
                                        e.drain()
                                sS.n += nkc
                            sec("scalar", _s)

                            def _ve(e, qc=qc, b=b, hh=hh,
                                    t_qc_end=t_qc_end):
                                e.wait_ge(sT.h, t_qc_end)
                                for qt in range(4):
                                    g = 32 * b + 4 * qc + qt
                                    e.tensor_scalar(out=lp[:],
                                                    in0=py[qt][:, 128:129],
                                                    scalar1=1e-30,
                                                    scalar2=None, op0=ALU.add)
                                    e.drain()
                                    e.reciprocal(out=rp[:], in_=lp[:])
                                    e.drain()
                                    e.tensor_scalar(
                                        out=y_sb[:, g * 256 + hh * 128:
                                                 g * 256 + hh * 128 + 128],
                                        in0=py[qt][:, 0:128],
                                        scalar1=rp[:, 0:1], scalar2=None,
                                        op0=ALU.mult)
                                    e.drain()
                                e.nop().then_inc(sV.h, 1)
                                sV.n += 1
                            sec("vector", _ve)
                            py_free[0] = sV.n
                        seq_free[par] = t_qc_end
                barrier()

                ph2 = contextlib.ExitStack()
                with ph2:
                    lsb2 = lambda n, s, d: ph2.enter_context(
                        nc.sbuf_tensor(P + n, s, d))
                    psb2 = lambda n, s, d: ph2.enter_context(
                        nc.psum_tensor(P + n, s, d))
                    yT = lsb2("yT", [128, 2 * 8192], BF16)
                    po = lsb2("po", [128, 2 * D], BF16)
                    Wo = lsb2("Wo", [128, 2 * D], BF16)
                    ptp = [psb2(f"ptp{i}", [128, 512], BF16)
                           for i in range(2)]
                    pso = [psb2(f"pso{i}", [128, 512], F32)
                           for i in range(2)]
                    wo_in = wout0_my if layer == 0 else wout1_my

                    def _g(e):
                        e.dma_start(
                            out=Wo[:].rearrange("p (k o) -> p k o", o=D),
                            in_=wo_in.rearrange("(k p) o -> p k o", p=128)
                        ).then_inc(sG.h, 16)
                        sG.n += 16
                    sec("gpsimd", _g)
                    g_wo = sG.n

                    tp_free = [0, 0]
                    for hh2 in range(2):
                        for gg in range(16):
                            pbuf = gg % 2

                            def _t(e, hh2=hh2, gg=gg, pbuf=pbuf):
                                e.wait_ge(sV.h, tp_free[pbuf])
                                for k in range(4):
                                    g4 = gg * 4 + k
                                    e.transpose(
                                        ptp[pbuf][:, k * 128: k * 128 + 128],
                                        y_sb[:, g4 * 256 + hh2 * 128:
                                             g4 * 256 + hh2 * 128 + 128],
                                        ident_sb[:])
                                e.drain()
                                e.nop().then_inc(sT.h, 1)
                                sT.n += 1
                            sec("tensor", _t)
                            t_tp = sT.n

                            def _v(e, hh2=hh2, gg=gg, pbuf=pbuf, t_tp=t_tp):
                                e.wait_ge(sT.h, t_tp)
                                e.tensor_copy(
                                    out=yT[:, hh2 * 8192 + gg * 512:
                                           hh2 * 8192 + gg * 512 + 512],
                                    in_=ptp[pbuf][:])
                                e.drain()
                                e.nop().then_inc(sV.h, 1)
                                sV.n += 1
                                tp_free[pbuf] = sV.n
                            sec("vector", _v)
                    v_yT = sV.n

                    po_free = [0, 0]
                    for tt in range(64):
                        par2 = tt % 2
                        vbase = sV.n

                        def _t(e, tt=tt, par2=par2, vbase=vbase):
                            if tt == 0:
                                e.wait_ge(sV.h, v_yT)
                                e.wait_ge(sG.h, g_wo)
                            e.wait_ge(sV.h, vbase)
                            e.wait_ge(sP[par2].h, po_free[par2])
                            for grp in range(4):
                                if grp >= 2:
                                    e.wait_ge(sV.h, vbase + grp - 1)
                                for hh2 in range(2):
                                    e.matmul(
                                        pso[grp % 2][:],
                                        yT[:, hh2 * 8192 + tt * 128:
                                           hh2 * 8192 + tt * 128 + 128],
                                        Wo[:, hh2 * D + grp * 512:
                                           hh2 * D + grp * 512 + 512],
                                        start=(hh2 == 0), stop=(hh2 == 1))
                                e.drain()
                                e.nop().then_inc(sT.h, 1)
                            sT.n += 4
                        sec("tensor", _t)
                        t_mm = sT.n

                        def _v(e, par2=par2, t_mm=t_mm):
                            for grp in range(4):
                                e.wait_ge(sT.h, t_mm - 4 + grp + 1)
                                e.tensor_copy(
                                    out=po[:, par2 * D + grp * 512:
                                           par2 * D + grp * 512 + 512],
                                    in_=pso[grp % 2][:])
                                e.drain()
                                e.nop().then_inc(sV.h, 1)
                            sV.n += 4
                        sec("vector", _v)
                        v_po = sV.n

                        def _w(e, tt=tt, par2=par2, v_po=v_po):
                            e.wait_ge(sV.h, v_po)
                            e.dma_start(
                                out=rs_in[tt * 128:(tt + 1) * 128, :],
                                in_=po[:, par2 * D: par2 * D + D]
                            ).then_inc(sP[par2].h, 16)
                            sP[par2].n += 16
                            po_free[par2] = sP[par2].n
                        sec("sync", _w)

                    emit_collective("ReduceScatter", ALU.add, rs_in,
                                    rs_out_d[layer])

        def emit_residual(layer, add_bias):
            ph = contextlib.ExitStack()
            with ph:
                ro2 = ph.enter_context(
                    nc.sbuf_tensor(f"rr{layer}", [128, NT * D], BF16))
                outb_sb = ph.enter_context(
                    nc.sbuf_tensor(f"ob{layer}", [128, D], F32))
                cwait = sC.n

                def _w(e):
                    e.wait_ge(sC.h, cwait)
                    e.dma_start(
                        out=ro2[:].rearrange("p (t d) -> p t d", d=D),
                        in_=rs_out_d[layer].rearrange("(t p) d -> p t d",
                                                      p=128)
                    ).then_inc(sW.h, 16)
                    sW.n += 16
                    if add_bias:
                        e.dma_start(out=outb_sb[:], in_=outb_in
                                    ).then_inc(sW.h, 16)
                        sW.n += 16
                sec("sync", _w)
                w_ro = sW.n

                def _v(e):
                    e.wait_ge(sW.h, w_ro)
                    for t in range(NT):
                        e.tensor_tensor(out=x_sb[:, t * D:(t + 1) * D],
                                        in0=x_sb[:, t * D:(t + 1) * D],
                                        in1=ro2[:, t * D:(t + 1) * D],
                                        op=ALU.add)
                        if add_bias:
                            e.drain()
                            e.tensor_tensor(out=x_sb[:, t * D:(t + 1) * D],
                                            in0=x_sb[:, t * D:(t + 1) * D],
                                            in1=outb_sb[:], op=ALU.add)
                    e.drain()
                    e.nop().then_inc(sV.h, 1)
                    sV.n += 1
                sec("vector", _v)

        def emit_ffn():
            ph = contextlib.ExitStack()
            with ph:
                lsb = lambda n, s, d: ph.enter_context(
                    nc.sbuf_tensor("f" + n, s, d))
                psb = lambda n, s, d: ph.enter_context(
                    nc.psum_tensor("f" + n, s, d))
                pbc = psb("pbc", [128, 512], F32)
                phg = contextlib.ExitStack()
                gsbf = lambda n, s, d: phg.enter_context(
                    nc.sbuf_tensor("fg" + n, s, d, side="right"))
                xT = gsbf("xT", [128, KC * T], BF16)
                emit_ln("L2", None, xT)
                barrier()
                hT = lsb("hT", [128, HC * T], BF16)
                S128 = gsbf("S128", [128, T], BF16)
                wg = gsbf("wg", [128, 2 * 2048], BF16)
                wu = gsbf("wu", [128, 2048], BF16)
                tgs = gsbf("tgs", [128, 512], F32)
                gsb = gsbf("gsb", [128, 512], BF16)
                pg = phg.enter_context(nc.psum_tensor("fpg", [128, 512], F32))
                pu = phg.enter_context(nc.psum_tensor("fpu", [128, 512], F32))

                for half in range(2):
                    def _t(e, half=half):
                        if half == 1:
                            e.wait_ge(sV.h, sV.n)
                        for kc in range(KC):
                            e.matmul(pbc[:], onesM[:],
                                     xT[:, kc * T + half * 512:
                                        kc * T + half * 512 + 512],
                                     start=(kc == 0), stop=(kc == KC - 1))
                        e.drain()
                        e.nop().then_inc(sT.h, 1)
                        sT.n += 1
                    sec("tensor", _t)
                    t_sx = sT.n

                    def _v(e, half=half, t_sx=t_sx):
                        e.wait_ge(sT.h, t_sx)
                        e.tensor_scalar(out=S128[:, half * 512:
                                                half * 512 + 512],
                                        in0=pbc[:], scalar1=-128.0,
                                        scalar2=None, op0=ALU.mult)
                        e.drain()
                        e.nop().then_inc(sV.h, 1)
                        sV.n += 1
                    sec("vector", _v)
                v_s128 = sV.n

                w_free = [0, 0]
                wu_free = [0]
                gd = [0, 0]
                gdu = [0]
                for hc in range(HC):
                    par = hc % 2

                    def _g(e, hc=hc, par=par):
                        e.wait_ge(sT.h, w_free[par])
                        e.dma_start(
                            out=wg[:, par * 2048: par * 2048 + 2048
                                   ].rearrange("p (kc c) -> p kc c", c=128),
                            in_=wgT_in[:, hc * 128:(hc + 1) * 128
                                       ].rearrange("(kc p) c -> p kc c",
                                                   p=128)
                        ).then_inc(sF[par].h, 16)
                        sF[par].n += 16
                        gd[par] = sF[par].n
                        e.wait_ge(sT.h, wu_free[0])
                        e.dma_start(
                            out=wu[:].rearrange("p (kc c) -> p kc c", c=128),
                            in_=wuT_in[:, hc * 128:(hc + 1) * 128
                                       ].rearrange("(kc p) c -> p kc c",
                                                   p=128)
                        ).then_inc(sFu.h, 16)
                        sFu.n += 16
                        gdu[0] = sFu.n
                    sec("gpsimd", _g)

                    for half in range(2):
                        def _t(e, hc=hc, par=par, half=half):
                            e.wait_ge(sF[par].h, gd[par])
                            e.wait_ge(sV.h, v_s128 if (hc == 0 and half == 0)
                                      else sV.n)
                            for kc in range(KC):
                                e.matmul(pg[:],
                                         wg[:, par * 2048 + kc * 128:
                                            par * 2048 + kc * 128 + 128],
                                         xT[:, kc * T + half * 512:
                                            kc * T + half * 512 + 512],
                                         start=(kc == 0), stop=(kc == KC - 1))
                            e.drain()
                            e.nop().then_inc(sT.h, 1)
                            e.wait_ge(sFu.h, gdu[0])
                            for kc in range(KC):
                                e.matmul(pu[:],
                                         wu[:, kc * 128: kc * 128 + 128],
                                         xT[:, kc * T + half * 512:
                                            kc * T + half * 512 + 512],
                                         start=(kc == 0), stop=(kc == KC - 1))
                            e.drain()
                            e.nop().then_inc(sT.h, 1)
                            sT.n += 2
                            if half == 1:
                                w_free[par] = sT.n
                                wu_free[0] = sT.n
                        sec("tensor", _t)
                        t_gu = sT.n

                        def _v(e, half=half, t_gu=t_gu):
                            e.wait_ge(sT.h, t_gu - 1)
                            e.tensor_tensor(out=tgs[:], in0=pg[:],
                                            in1=S128[:, half * 512:
                                                     half * 512 + 512],
                                            op=ALU.add)
                            e.drain()
                            e.nop().then_inc(sV.h, 1)
                            sV.n += 1
                        sec("vector", _v)
                        v_tg = sV.n

                        def _s(e, hc=hc, v_tg=v_tg):
                            e.wait_ge(sV.h, v_tg)
                            e.activation(out=gsb[:], in_=tgs[:],
                                         func=ACT.Silu,
                                         scale=sg_sb[:, hc:hc + 1]
                                         ).then_inc(sS.h, 1)
                            e.drain()
                            sS.n += 1
                        sec("scalar", _s)
                        s_g = sS.n

                        def _v2(e, hc=hc, half=half, s_g=s_g, t_gu=t_gu):
                            e.wait_ge(sS.h, s_g)
                            e.wait_ge(sT.h, t_gu)
                            e.tensor_tensor(out=tgs[:], in0=pu[:],
                                            in1=S128[:, half * 512:
                                                     half * 512 + 512],
                                            op=ALU.add)
                            e.drain()
                            e.scalar_tensor_tensor(
                                out=hT[:, hc * T + half * 512:
                                       hc * T + half * 512 + 512],
                                in0=tgs[:], scalar=su_sb[:, hc:hc + 1],
                                in1=gsb[:], op0=ALU.mult, op1=ALU.mult)
                            e.drain()
                            e.nop().then_inc(sV.h, 1)
                            sV.n += 1
                        sec("vector", _v2)
                barrier()
                phg.close()

                mshb = lsb("mshb", [128, T], BF16)
                for half in range(2):
                    def _t(e, half=half):
                        if half == 1:
                            e.wait_ge(sV.h, sV.n)
                        for hc in range(HC):
                            e.matmul(pbc[:], onesM[:],
                                     hT[:, hc * T + half * 512:
                                        hc * T + half * 512 + 512],
                                     start=(hc == 0), stop=(hc == HC - 1))
                        e.drain()
                        e.nop().then_inc(sT.h, 1)
                        sT.n += 1
                    sec("tensor", _t)
                    t_sh = sT.n

                    def _v(e, half=half, t_sh=t_sh):
                        e.wait_ge(sT.h, t_sh)
                        e.tensor_scalar(out=mshb[:, half * 512:
                                                half * 512 + 512],
                                        in0=pbc[:], scalar1=-128.0,
                                        scalar2=None, op0=ALU.mult)
                        e.drain()
                        e.nop().then_inc(sV.h, 1)
                        sV.n += 1
                    sec("vector", _v)
                v_mshb = sV.n

                wd = lsb("wd", [128, 2 * HC * 128], BF16)
                tds = lsb("tds", [128, 512], F32)
                dfm = lsb("dfm", [128, 512], BF16)
                pd = [psb(f"pd{i}", [128, 512], F32) for i in range(2)]
                ptd = psb("ptd", [128, 512], BF16)
                wd_free = [0, 0]
                gdd = [0, 0]
                for oc in range(KC):
                    par = oc % 2

                    def _g(e, oc=oc, par=par):
                        e.wait_ge(sT.h, wd_free[par])
                        e.dma_start(
                            out=wd[:, par * HC * 128:
                                   par * HC * 128 + HC * 128
                                   ].rearrange("p (hc c) -> p hc c", c=128),
                            in_=wdT_in[:, oc * 128:(oc + 1) * 128
                                       ].rearrange("(hc p) c -> p hc c",
                                                   p=128)
                        ).then_inc(sF[par].h, 16)
                        sF[par].n += 16
                        gdd[par] = sF[par].n
                    sec("gpsimd", _g)

                    for half in range(2):
                        def _t(e, oc=oc, par=par, half=half):
                            e.wait_ge(sF[par].h, gdd[par])
                            e.wait_ge(sV.h, v_mshb if (oc == 0 and half == 0)
                                      else sV.n)
                            for hc in range(HC):
                                e.matmul(pd[half][:],
                                         wd[:, par * HC * 128 + hc * 128:
                                            par * HC * 128 + hc * 128 + 128],
                                         hT[:, hc * T + half * 512:
                                            hc * T + half * 512 + 512],
                                         start=(hc == 0), stop=(hc == HC - 1))
                            e.drain()
                            e.nop().then_inc(sT.h, 1)
                            sT.n += 1
                            if half == 1:
                                wd_free[par] = sT.n
                        sec("tensor", _t)
                        t_d = sT.n

                        def _v(e, oc=oc, half=half, t_d=t_d):
                            e.wait_ge(sT.h, t_d)
                            e.tensor_tensor(out=tds[:], in0=pd[half][:],
                                            in1=mshb[:, half * 512:
                                                     half * 512 + 512],
                                            op=ALU.add)
                            e.drain()
                            e.tensor_scalar(out=dfm[:], in0=tds[:],
                                            scalar1=sd_sb[:, oc:oc + 1],
                                            scalar2=None, op0=ALU.mult)
                            e.drain()
                            e.nop().then_inc(sV.h, 1)
                            sV.n += 1
                        sec("vector", _v)
                        v_d = sV.n

                        def _t2(e, v_d=v_d):
                            e.wait_ge(sV.h, v_d)
                            for j in range(4):
                                e.transpose(ptd[:, j * 128: j * 128 + 128],
                                            dfm[:, j * 128: j * 128 + 128],
                                            ident_sb[:])
                            e.drain()
                            e.nop().then_inc(sT.h, 1)
                            sT.n += 1
                        sec("tensor", _t2)
                        t_tr = sT.n

                        def _v2(e, oc=oc, half=half, t_tr=t_tr):
                            e.wait_ge(sT.h, t_tr)
                            for j in range(4):
                                lt = half * 4 + j
                                e.tensor_tensor(
                                    out=x_sb[:, lt * D + oc * 128:
                                             lt * D + oc * 128 + 128],
                                    in0=x_sb[:, lt * D + oc * 128:
                                             lt * D + oc * 128 + 128],
                                    in1=ptd[:, j * 128: j * 128 + 128],
                                    op=ALU.add)
                            e.drain()
                            e.nop().then_inc(sV.h, 1)
                            sV.n += 1
                        sec("vector", _v2)

        def emit_lat(xT):
            ph = contextlib.ExitStack()
            with ph:
                lsb = lambda n, s, d: ph.enter_context(
                    nc.sbuf_tensor("l" + n, s, d))
                psb = lambda n, s, d: ph.enter_context(
                    nc.psum_tensor("l" + n, s, d))
                Wa = lsb("Wa", [128, KC * DL], BF16)
                lat = lsb("lat", [128, 4 * T], BF16)
                nwtm = lsb("nwtm", [128, 512], F32)
                ssl = lsb("ssl", [128, 1], F32)
                sdl = lsb("sdl", [128, 1], F32)
                ril = lsb("ril", [128, 1], F32)
                sql = lsb("sql", [128, 512], BF16)
                ltc = lsb("ltc", [128, 512], BF16)
                lnt = lsb("lnt", [128, 512], BF16)
                latn = lsb("latn", [128, 4 * T], BF16)
                pl = [psb(f"pl{i}", [128, 512], F32) for i in range(2)]
                ptl = psb("ptl", [128, 512], BF16)
                plf = psb("plf", [128, 512], BF16)
                v_entry = sV.n

                def _g(e):
                    e.dma_start(
                        out=Wa[:].rearrange("p (k o) -> p k o", o=DL),
                        in_=wkvaT_in.rearrange("(k p) o -> p k o", p=128)
                    ).then_inc(sG.h, 16)
                    sG.n += 16
                sec("gpsimd", _g)
                g_wa = sG.n

                def _w0(e):
                    e.dma_start(out=nwtm[:], in_=normwtm_in).then_inc(sW.h, 16)
                    sW.n += 16
                sec("sync", _w0)
                w_nw = sW.n

                for lc in range(4):
                    for half in range(2):
                        def _t(e, lc=lc, half=half):
                            e.wait_ge(sG.h, g_wa)
                            e.wait_ge(sV.h, v_entry if (lc == 0 and half == 0)
                                      else sV.n)
                            for kc in range(KC):
                                e.matmul(pl[half][:],
                                         Wa[:, kc * DL + lc * 128:
                                            kc * DL + lc * 128 + 128],
                                         xT[:, kc * T + half * 512:
                                            kc * T + half * 512 + 512],
                                         start=(kc == 0), stop=(kc == KC - 1))
                            e.drain()
                            e.nop().then_inc(sT.h, 1)
                            sT.n += 1
                        sec("tensor", _t)
                        t_l = sT.n

                        def _v(e, lc=lc, half=half, t_l=t_l):
                            e.wait_ge(sT.h, t_l)
                            e.tensor_copy(out=lat[:, lc * T + half * 512:
                                                  lc * T + half * 512 + 512],
                                          in_=pl[half][:])
                            e.drain()
                            e.nop().then_inc(sV.h, 1)
                            sV.n += 1
                        sec("vector", _v)
                v_lat = sV.n

                for t in range(NT):
                    def _t(e, t=t):
                        e.wait_ge(sV.h, v_lat if t == 0 else sV.n)
                        for lc in range(4):
                            e.transpose(ptl[:, lc * 128: lc * 128 + 128],
                                        lat[:, lc * T + t * 128:
                                            lc * T + t * 128 + 128],
                                        ident_sb[:])
                        e.drain()
                        e.nop().then_inc(sT.h, 1)
                        sT.n += 1
                    sec("tensor", _t)
                    t_tm = sT.n

                    def _v(e, t=t, t_tm=t_tm):
                        e.wait_ge(sT.h, t_tm)
                        e.wait_ge(sW.h, w_nw)
                        e.tensor_copy(out=ltc[:], in_=ptl[:])
                        e.drain()
                        e.tensor_tensor(out=sql[:], in0=ltc[:], in1=ltc[:],
                                        op=ALU.mult)
                        e.drain()
                        e.reduce_sum(out=ssl[:], in_=sql[:], axis=AX.X)
                        e.drain()
                        e.nop().then_inc(sV.h, 1)
                        sV.n += 1
                    sec("vector", _v)
                    v_ss = sV.n

                    def _s(e, v_ss=v_ss):
                        e.wait_ge(sV.h, v_ss)
                        e.activation(out=sdl[:], in_=ssl[:], func=ACT.Sqrt,
                                     scale=1.0 / DL, bias=eps_sb[:, 1:2]
                                     ).then_inc(sS.h, 1)
                        e.drain()
                        sS.n += 1
                    sec("scalar", _s)
                    s_d = sS.n

                    def _v2(e, s_d=s_d):
                        e.wait_ge(sS.h, s_d)
                        e.reciprocal(out=ril[:], in_=sdl[:])
                        e.drain()
                        e.tensor_scalar(out=lnt[:], in0=ptl[:],
                                        scalar1=ril[:, 0:1], scalar2=None,
                                        op0=ALU.mult)
                        e.drain()
                        e.tensor_tensor(out=lnt[:], in0=lnt[:], in1=nwtm[:],
                                        op=ALU.mult)
                        e.drain()
                        e.nop().then_inc(sV.h, 1)
                        sV.n += 1
                    sec("vector", _v2)
                    v_ln = sV.n

                    def _t2(e, v_ln=v_ln):
                        e.wait_ge(sV.h, v_ln)
                        for lc in range(4):
                            e.transpose(plf[:, lc * 128: lc * 128 + 128],
                                        lnt[:, lc * 128: lc * 128 + 128],
                                        ident_sb[:])
                        e.drain()
                        e.nop().then_inc(sT.h, 1)
                        sT.n += 1
                    sec("tensor", _t2)
                    t_fm = sT.n

                    def _v3(e, t=t, t_fm=t_fm):
                        e.wait_ge(sT.h, t_fm)
                        for lc in range(4):
                            e.tensor_copy(
                                out=latn[:, lc * T + t * 128:
                                         lc * T + t * 128 + 128],
                                in_=plf[:, lc * 128: lc * 128 + 128])
                        e.drain()
                        e.nop().then_inc(sV.h, 1)
                        sV.n += 1
                    sec("vector", _v3)
                v_latn = sV.n

                def _w(e, v_latn=v_latn):
                    e.wait_ge(sV.h, v_latn)
                    for lc in range(4):
                        e.dma_start(out=agl_in[lc * 128:(lc + 1) * 128, :],
                                    in_=latn[:, lc * T:(lc + 1) * T]
                                    ).then_inc(sW.h, 16)
                        sW.n += 16
                sec("sync", _w)

        def emit_moe():
            ph = contextlib.ExitStack()
            with ph:
                lsb = lambda n, s, d: ph.enter_context(
                    nc.sbuf_tensor("m" + n, s, d))
                psb = lambda n, s, d: ph.enter_context(
                    nc.psum_tensor("m" + n, s, d))
                xT = lsb("xT", [128, KC * T], BF16)
                emit_ln("L4", None, xT)
                barrier()
                WA = lsb("WA", [128, KC * 520], BF16)
                BW = lsb("BW", [128, 4 * D], BF16)
                lgs = lsb("lgs", [128, 8], F32)
                hsf = lsb("hsf", [128, 512], F32)
                nmx = lsb("nmx", [128, 1], F32)
                mxp = lsb("mxp", [128, 1], F32)
                expv = lsb("expv", [128, 8], F32)
                sme = lsb("sme", [128, 1], F32)
                rcp = lsb("rcp", [128, 1], F32)
                ps4 = lsb("ps4", [128, 1], F32)
                msk = lsb("msk", [128, 8], F32)
                hh2 = lsb("hh2", [128, 512], BF16)
                hmT = lsb("hmT", [128, 4 * T], BF16)
                plh = [psb(f"plh{i}", [128, 520], F32) for i in range(2)]
                pht = psb("pht", [128, 512], BF16)
                pyy = [psb(f"pyy{i}", [128, 512], F32) for i in range(2)]

                def _g(e):
                    e.dma_start(
                        out=WA[:].rearrange("p (k o) -> p k o", o=520),
                        in_=wgA_in.rearrange("(k p) o -> p k o", p=128)
                    ).then_inc(sG.h, 16)
                    e.dma_start(
                        out=BW[:].rearrange("p (k o) -> p k o", o=D),
                        in_=bwT_in.rearrange("(k p) o -> p k o", p=128)
                    ).then_inc(sG.h, 16)
                    sG.n += 32
                sec("gpsimd", _g)
                g_w = sG.n

                evd = [0, 0]
                for t in range(NT):
                    par = t % 2

                    def _t(e, t=t, par=par):
                        if t == 0:
                            e.wait_ge(sG.h, g_w)
                        e.wait_ge(sV.h, evd[par])
                        for kc in range(KC):
                            e.matmul(plh[par][:, 0:512],
                                     xT[:, kc * T + t * 128:
                                        kc * T + t * 128 + 128],
                                     WA[:, kc * 520: kc * 520 + 512],
                                     start=(kc == 0), stop=(kc == KC - 1))
                        for kc in range(KC):
                            e.matmul(plh[par][:, 512:520],
                                     xT[:, kc * T + t * 128:
                                        kc * T + t * 128 + 128],
                                     WA[:, kc * 520 + 512: kc * 520 + 520],
                                     start=(kc == 0), stop=(kc == KC - 1))
                        e.drain()
                        e.nop().then_inc(sT.h, 1)
                        sT.n += 1
                    sec("tensor", _t)
                    t_lh = sT.n

                    def _v(e, par=par, t_lh=t_lh):
                        e.wait_ge(sT.h, t_lh)
                        e.tensor_copy(out=lgs[:], in_=plh[par][:, 512:520])
                        e.tensor_copy(out=hsf[:], in_=plh[par][:, 0:512])
                        e.drain()
                        e.reduce_max(out=nmx[:], in_=lgs[:], axis=AX.X,
                                     negate=True)
                        e.drain()
                        e.tensor_scalar(out=mxp[:], in0=nmx[:], scalar1=-1.0,
                                        scalar2=None, op0=ALU.mult)
                        e.drain()
                        e.tensor_scalar(out=msk[:], in0=lgs[:],
                                        scalar1=mxp[:, 0:1], scalar2=None,
                                        op0=ALU.is_ge)
                        e.drain()
                        e.nop().then_inc(sV.h, 1)
                        sV.n += 1
                    sec("vector", _v)
                    v_sm = sV.n

                    def _s(e, v_sm=v_sm):
                        e.wait_ge(sV.h, v_sm)
                        e.activation(out=expv[:], in_=lgs[:], func=ACT.Exp,
                                     bias=nmx[:, 0:1],
                                     accum_out=sme[:, 0:1]).then_inc(sS.h, 1)
                        e.drain()
                        sS.n += 1
                    sec("scalar", _s)
                    s_e = sS.n

                    def _v2(e, par=par, s_e=s_e):
                        e.wait_ge(sS.h, s_e)
                        e.reciprocal(out=rcp[:], in_=sme[:])
                        e.drain()
                        e.tensor_scalar(out=ps4[:], in0=rcp[:], scalar1=4.0,
                                        scalar2=None, op0=ALU.mult)
                        e.drain()
                        for ex in range(8):
                            e.tensor_scalar(out=hh2[:, ex * 64: ex * 64 + 64],
                                            in0=hsf[:, ex * 64: ex * 64 + 64],
                                            scalar1=msk[:, ex:ex + 1],
                                            scalar2=ps4[:, 0:1],
                                            op0=ALU.mult, op1=ALU.mult)
                        e.drain()
                        e.nop().then_inc(sV.h, 1)
                        sV.n += 1
                        evd[par] = sV.n
                    sec("vector", _v2)
                    v_h = sV.n

                    def _t2(e, v_h=v_h):
                        e.wait_ge(sV.h, v_h)
                        for j in range(4):
                            e.transpose(pht[:, j * 128: j * 128 + 128],
                                        hh2[:, j * 128: j * 128 + 128],
                                        ident_sb[:])
                        e.drain()
                        e.nop().then_inc(sT.h, 1)
                        sT.n += 1
                    sec("tensor", _t2)
                    t_ht = sT.n

                    def _v3(e, t=t, t_ht=t_ht):
                        e.wait_ge(sT.h, t_ht)
                        for j in range(4):
                            e.tensor_copy(out=hmT[:, j * T + t * 128:
                                                  j * T + t * 128 + 128],
                                          in_=pht[:, j * 128: j * 128 + 128])
                        e.drain()
                        e.nop().then_inc(sV.h, 1)
                        sV.n += 1
                    sec("vector", _v3)
                v_hmT = sV.n

                for t in range(NT):
                    for grp in range(4):
                        gpar = grp % 2

                        def _t(e, t=t, grp=grp, gpar=gpar):
                            if t == 0 and grp == 0:
                                e.wait_ge(sV.h, v_hmT)
                            if not (t == 0 and grp < 2):
                                e.wait_ge(sV.h, sV.n)
                            for rc in range(4):
                                e.matmul(pyy[gpar][:],
                                         hmT[:, rc * T + t * 128:
                                             rc * T + t * 128 + 128],
                                         BW[:, rc * D + grp * 512:
                                            rc * D + grp * 512 + 512],
                                         start=(rc == 0), stop=(rc == 3))
                            e.drain()
                            e.nop().then_inc(sT.h, 1)
                            sT.n += 1
                        sec("tensor", _t)
                        t_y = sT.n

                        def _v(e, t=t, grp=grp, gpar=gpar, t_y=t_y):
                            e.wait_ge(sT.h, t_y)
                            e.tensor_tensor(
                                out=x_sb[:, t * D + grp * 512:
                                         t * D + grp * 512 + 512],
                                in0=x_sb[:, t * D + grp * 512:
                                         t * D + grp * 512 + 512],
                                in1=pyy[gpar][:], op=ALU.add)
                            e.drain()
                            e.nop().then_inc(sV.h, 1)
                            sV.n += 1
                        sec("vector", _v)

        _ORDER = ["init", "ln1", "qkv0", "attn0", "res0", "ffn",
                  "ln3", "qkv1", "attn1", "res1", "moe", "all"]

        def want(name):
            return _ORDER.index(name) <= _ORDER.index(stop_after)

        if want("ln1"):
            emit_ln("L1", ag_in_d[0], None)
            emit_collective("AllGather", ALU.bypass, ag_in_d[0], ag_out_d[0])
            barrier()
        if want("qkv0"):
            emit_qkv(0)
            barrier()
        if want("attn0"):
            emit_attn(0)
            barrier()
        if want("res0"):
            emit_residual(0, add_bias=False)
            barrier()
        if want("ffn"):
            emit_ffn()
            barrier()

        ph_l1 = contextlib.ExitStack()
        with ph_l1:
            if want("ln3"):
                xT3 = ph_l1.enter_context(
                    nc.sbuf_tensor("xT3", [128, KC * T], BF16))
                emit_ln("L3", ag_in_d[1], xT3)
                emit_collective("AllGather", ALU.bypass, ag_in_d[1],
                                ag_out_d[1])
                emit_lat(xT3)
                emit_collective("AllGather", ALU.bypass, agl_in, agl_out)
                barrier()
            if want("qkv1"):
                emit_qkv(1)
                barrier()
        if want("attn1"):
            emit_attn(1)
            barrier()
        if want("res1"):
            emit_residual(1, add_bias=True)
            barrier()
        if want("moe"):
            emit_moe()
            barrier()

        def _wout(e):
            e.dma_start(out=out_t.rearrange("(t p) d -> p t d", p=128),
                        in_=x_sb[:].rearrange("p (t d) -> p t d", d=D)
                        ).then_inc(sW.h, 16)
            sW.n += 16
            e.wait_ge(sW.h, sW.n)
        sec("sync", _wout)

    return nc


def _host_constants():
    pos = np.arange(S, dtype=np.float32)
    j64 = np.arange(64, dtype=np.float32)
    inv = (np.float32(10000.0) ** (-j64 / np.float32(64.0))).astype(np.float32)
    theta = (pos[None, :] * inv[:, None]).astype(np.float32)
    cos = np.cos(theta).astype(np.float32)
    sin = np.sin(theta).astype(np.float32)
    c0tmh = np.zeros((128, 2048), dtype=np.float32)
    s0tmh = np.zeros((128, 2048), dtype=np.float32)
    for pb in range(32):
        c0tmh[:, pb * 64:(pb + 1) * 64] = cos[:, pb * 128:(pb + 1) * 128].T
        s0tmh[:, pb * 64:(pb + 1) * 64] = sin[:, pb * 128:(pb + 1) * 128].T
    ident = np.eye(128, dtype=np.float32)
    onesM = np.ones((128, 128), dtype=np.float32)
    onescol = np.ones((128, 64), dtype=np.float32)
    epsv = np.broadcast_to(
        np.array([1e-5, 1e-6, -C_BOUND, 0.0], np.float32)[None, :],
        (128, 4)).copy()
    maskT = np.zeros((128, 4 * 512), dtype=np.float32)
    qpos = np.arange(512)
    for m in range(4):
        maskT[:, m * 512:(m + 1) * 512] = (
            qpos[None, :] >= (m * 128 + np.arange(128))[:, None]
        ).astype(np.float32)
    return c0tmh, s0tmh, ident, onesM, onescol, epsv, maskT


def kernel(L0_attn_x, L0_attn_w_qkv, L0_attn_w_out, L0_attn_q_gain,
           L0_ffn_wq_gate, L0_ffn_s_gate, L0_ffn_wq_up, L0_ffn_s_up,
           L0_ffn_wq_down, L0_ffn_s_down, L1_attn_freqs, L1_attn_wq_w,
           L1_attn_wq_b, L1_attn_wkv_a_w, L1_attn_wkv_b_w, L1_attn_norm_w,
           L1_attn_out_w, L1_attn_out_b, L1_ffn_A, L1_ffn_Bw, L1_ffn_Wg):
    def hnp(a, dt):
        a = np.asarray(a)
        if a.dtype != dt:
            a = a.astype(dt)
        return a

    c0, s0, ident, onesM, onescol, epsv, maskT = _host_constants()
    x_full = np.ascontiguousarray(
        hnp(L0_attn_x, np.float32).reshape(8 * T, D))
    wqkv = hnp(L0_attn_w_qkv, np.float32)
    wout0 = hnp(L0_attn_w_out, np.float32)
    qgain = hnp(L0_attn_q_gain, np.float32)
    wgTh = np.ascontiguousarray(hnp(L0_ffn_wq_gate, np.int32).T.astype(np.uint8))
    wuTh = np.ascontiguousarray(hnp(L0_ffn_wq_up, np.int32).T.astype(np.uint8))
    wdTh = np.ascontiguousarray(hnp(L0_ffn_wq_down, np.int32).T.astype(np.uint8))
    sg = hnp(L0_ffn_s_gate, np.float32)
    su = hnp(L0_ffn_s_up, np.float32)
    sd = hnp(L0_ffn_s_down, np.float32)
    freqs = hnp(L1_attn_freqs, np.float32)
    wq1 = hnp(L1_attn_wq_w, np.float32)
    wqb = hnp(L1_attn_wq_b, np.float32)
    wkva = hnp(L1_attn_wkv_a_w, np.float32)
    wkvb = hnp(L1_attn_wkv_b_w, np.float32)
    normw = hnp(L1_attn_norm_w, np.float32)
    wout1 = hnp(L1_attn_out_w, np.float32)
    outb = hnp(L1_attn_out_b, np.float32)
    A = hnp(L1_ffn_A, np.float32)
    Bw = hnp(L1_ffn_Bw, np.float32)
    Wg = hnp(L1_ffn_Wg, np.float32)

    sg_pp = np.ascontiguousarray(sg.reshape(HC, 128).T)
    su_pp = np.ascontiguousarray(su.reshape(HC, 128).T)
    sd_pp = np.ascontiguousarray(sd.reshape(KC, 128).T)
    normwtm = np.ascontiguousarray(
        np.broadcast_to(normw[None, :], (128, 512)))
    wkvaT = np.ascontiguousarray(wkva.T)
    wgA = np.ascontiguousarray(
        np.concatenate([A.reshape(512, D).T, Wg.T], axis=1))
    bwT = np.ascontiguousarray(Bw.transpose(0, 2, 1).reshape(512, D))
    outb_bc = np.ascontiguousarray(np.broadcast_to(outb[None, :], (128, D)))

    in_maps = []
    for c in range(8):
        h0 = 2 * c
        cols = []
        for t3 in range(3):
            for hh in range(2):
                base = t3 * D + (h0 + hh) * 128
                cols.append(wqkv[:, base:base + 128])
        wqkv_myc = np.ascontiguousarray(np.concatenate(cols, axis=1))
        wout0_myc = np.ascontiguousarray(wout0[256 * c:256 * c + 256, :])
        gain2 = np.ascontiguousarray(
            np.broadcast_to(qgain[h0:h0 + 2][None, :], (128, 2)))
        wq1_myc = np.ascontiguousarray(wq1[256 * c:256 * c + 256, :].T)
        wqbtm = np.zeros((128, 1024), dtype=np.float32)
        for hh in range(2):
            bvec = wqb[(h0 + hh) * 128:(h0 + hh + 1) * 128]
            wqbtm[:, hh * 512: hh * 512 + 512] = np.broadcast_to(
                np.tile(bvec, 4)[None, :], (128, 512))
        krows = wkvb[h0 * 128:(h0 + 2) * 128, :]
        vrows = wkvb[2048 + h0 * 128: 2048 + (h0 + 2) * 128, :]
        wkvb_myc = np.ascontiguousarray(
            np.concatenate([krows, vrows], axis=0).T)
        wout1_myc = np.ascontiguousarray(wout1[:, 256 * c:256 * c + 256].T)
        c2tm = np.zeros((128, 256), dtype=np.float32)
        s2tm = np.zeros((128, 256), dtype=np.float32)
        for hh in range(2):
            h = h0 + hh
            crow = freqs[h, :, 0]
            srow = freqs[h, :, 1]
            c2tm[:, hh * 128: hh * 128 + 128] = np.broadcast_to(
                np.tile(crow, 4)[None, :], (128, 128))
            s2tm[:, hh * 128: hh * 128 + 128] = np.broadcast_to(
                np.tile(srow, 4)[None, :], (128, 128))
        in_maps.append(dict(
            x=np.ascontiguousarray(x_full[c * T:(c + 1) * T]),
            wqkv_my=wqkv_myc, wout0_my=wout0_myc, gain2=gain2,
            c0tm=c0, s0tm=s0, ident=ident, onesM=onesM, onescol=onescol,
            epsv=epsv, maskT=maskT, wgT=wgTh, wuT=wuTh, wdT=wdTh,
            sg_pp=sg_pp, su_pp=su_pp, sd_pp=sd_pp,
            wq1_my=wq1_myc, wqbtm=wqbtm, wkvaT=wkvaT, normwtm=normwtm,
            wkvb_my=wkvb_myc, wout1_my=wout1_myc, outb_bc=outb_bc,
            c2tm=c2tm, s2tm=s2tm, wgA=wgA, bwT=bwT))

    nc = build_nc()
    res = run_bass_kernel_spmd(nc, in_maps, core_ids=list(range(8)))
    out = np.concatenate([np.asarray(res.results[c]["out"])
                          for c in range(8)], axis=0)
    return out.reshape(B, S, D).astype(np.float32)


if __name__ == "__main__":
    build_nc()
    print("build ok")


# version 14
# speedup vs baseline: 1.2087x; 1.0163x over previous
import math
import contextlib
import os
import numpy as np
import ml_dtypes
import concourse.bass as bass
import concourse.mybir as mybir
from concourse.bass_utils import run_bass_kernel_spmd

F32 = mybir.dt.float32
BF16 = mybir.dt.bfloat16
I32 = mybir.dt.int32
U8 = mybir.dt.uint8
ACT = mybir.ActivationFunctionType
ALU = mybir.AluOpType
AX = mybir.AxisListType

B = 2
S = 4096
D = 2048
T = 1024
NT = 8
KC = 16
HID = 5632
HC = 44
DL = 512
NTS = 16
INV_SQRT_DH = 1.0 / math.sqrt(128.0)
C_BOUND = 40.0 * math.sqrt(128.0)


def build_nc(stop_after="all"):
    nc = bass.Bass("TRN2", target_bir_lowering=False, debug=False)

    def din(name, shape, dt=F32):
        return nc.dram_tensor(name, shape, dt, kind="ExternalInput").ap()

    def dint(name, shape, dt=BF16, **kw):
        return nc.dram_tensor(name, shape, dt, kind="Internal", **kw).ap()

    x_in = din("x", [T, D])
    wqkv_my = din("wqkv_my", [D, 768])
    wout0_my = din("wout0_my", [256, D])
    gain2 = din("gain2", [128, 2])
    c0tm = din("c0tm", [128, 2048])
    s0tm = din("s0tm", [128, 2048])
    ident_in = din("ident", [128, 128])
    onesM_in = din("onesM", [128, 128])
    onescol_in = din("onescol", [128, 64])
    epsv_in = din("epsv", [128, 4])
    maskT_in = din("maskT", [128, 4 * 512])
    wgT_in = din("wgT", [D, HID], U8)
    wuT_in = din("wuT", [D, HID], U8)
    wdT_in = din("wdT", [HID, D], U8)
    sg_in = din("sg_pp", [128, HC])
    su_in = din("su_pp", [128, HC])
    sd_in = din("sd_pp", [128, KC])
    wq1_my = din("wq1_my", [D, 256])
    wqbtm_in = din("wqbtm", [128, 1024])
    wkvaT_in = din("wkvaT", [D, DL])
    normwtm_in = din("normwtm", [128, 512])
    wkvb_my = din("wkvb_my", [DL, 512])
    wout1_my = din("wout1_my", [256, D])
    outb_in = din("outb_bc", [128, D])
    c2tm_in = din("c2tm", [128, 256])
    s2tm_in = din("s2tm", [128, 256])
    wgA_in = din("wgA", [D, 520])
    bwT_in = din("bwT", [512, D])
    out_t = nc.dram_tensor("out", [T, D], F32, kind="ExternalOutput").ap()

    ag_in_d = [dint("ag0_in", [D, T]), dint("ag1_in", [D, T])]
    ag_out_d = [dint("ag0_out", [8 * D, T], addr_space="Shared"),
                dint("ag1_out", [8 * D, T], addr_space="Shared")]
    agl_in = dint("agl_in", [DL, T])
    agl_out = dint("agl_out", [8 * DL, T], addr_space="Shared")
    rs_in_d = [dint("rs0_in", [8 * T, D]), dint("rs1_in", [8 * T, D])]
    rs_out_d = [dint("rs0_out", [T, D]), dint("rs1_out", [T, D])]
    sc_q = [dint("scq0", [256, 8 * T]), dint("scq1", [256, 8 * T])]
    sc_k = [dint("sck0", [256, 8 * T]), dint("sck1", [256, 8 * T])]
    sc_v = [dint("scv0", [2 * 8 * T, 128]), dint("scv1", [2 * 8 * T, 128])]

    es = contextlib.ExitStack()
    with es:
        sb = lambda n, s, d: es.enter_context(nc.sbuf_tensor(n, s, d))
        x_sb = sb("x_sb", [128, NT * D], F32)
        ident_sb = sb("ident_sb", [128, 128], BF16)
        onesM = sb("onesM_sb", [128, 128], BF16)
        gain_sb = sb("gain_sb", [128, 2], F32)
        nag_sb = sb("nag_sb", [128, 2], F32)
        eps_sb = sb("eps_sb", [128, 4], F32)
        sg_sb = sb("sg_sb", [128, HC], F32)
        su_sb = sb("su_sb", [128, HC], F32)
        sd_sb = sb("sd_sb", [128, KC], F32)

        class Ctr:
            def __init__(self, h):
                self.h = h
                self.n = 0

        sem = {k: Ctr(es.enter_context(nc.semaphore(k)))
               for k in ("sW", "sG", "sC", "sT", "sS", "sV",
                         "bW", "bG", "bT", "bS", "bV",
                         "pS0", "pS1", "sX0", "sX1", "sQ0", "sQ1",
                         "sU0", "sU1", "sL0", "sL1", "sP0", "sP1",
                         "sF0", "sF1", "sFu")}
        sW, sG, sC = sem["sW"], sem["sG"], sem["sC"]
        sT, sS, sV = sem["sT"], sem["sS"], sem["sV"]
        pS = [sem["pS0"], sem["pS1"]]
        sX = [sem["sX0"], sem["sX1"]]
        sQ = [sem["sQ0"], sem["sQ1"]]
        sU = [sem["sU0"], sem["sU1"]]
        sL = [sem["sL0"], sem["sL1"]]
        sP = [sem["sP0"], sem["sP1"]]
        sF = [sem["sF0"], sem["sF1"]]
        sFu = sem["sFu"]
        SYNC_DMA_SEMS = [sW] + pS + sX + sQ + sU + sL + sP
        GP_DMA_SEMS = [sG] + sF + [sFu]
        BARS = [sem[k] for k in ("bW", "bG", "bT", "bS", "bV")]
        ENG = ("sync", "gpsimd", "tensor", "scalar", "vector")

        block = es.enter_context(nc.Block())

        _DATA_OPS = frozenset((
            "matmul", "transpose", "tensor_tensor", "tensor_copy",
            "tensor_scalar", "activation", "reciprocal", "reduce_sum",
            "reduce_max", "scalar_tensor_tensor"))

        class _IW:
            def __init__(self, i, o):
                self.i = i
                self.o = o

            def then_inc(self, sem, value=1):
                self.i.then_inc(sem, value)
                self.o._has_inc = True
                return self

            def __getattr__(self, k):
                return getattr(self.i, k)

        class _EP:
            def __init__(self, e, elide_solo=False):
                self._e = e
                self._last = None
                self._has_inc = False
                self._pend = False
                self._elide = elide_solo

            def _flush(self):
                if self._pend:
                    if not self._elide:
                        self._e.drain()
                    self._pend = False

            def drain(self, *a, **k):
                self._pend = True
                return None

            def nop(self, *a, **k):
                if (self._pend and self._last is not None
                        and not self._has_inc):
                    self._pend = False
                    return _IW(self._last, self)
                self._flush()
                i = self._e.nop()
                self._last = i
                self._has_inc = False
                return _IW(i, self)

            def sem_inc(self, sem, val=1):
                self._pend = False
                self._e.drain()
                self._last = None
                return self._e.sem_inc(sem, val)

            def wait_ge(self, *a, **kw):
                self._flush()
                return self._e.wait_ge(*a, **kw)

            def __getattr__(self, k):
                f = getattr(self._e, k)
                if k in _DATA_OPS:
                    def g(*a, _f=f, **kw):
                        self._flush()
                        i = _f(*a, **kw)
                        self._last = i
                        self._has_inc = False
                        return _IW(i, self)
                    return g
                if callable(f):
                    def h(*a, _f=f, **kw):
                        self._flush()
                        return _f(*a, **kw)
                    return h
                return f

        _proxies = {}

        def sec(name, fn):
            if name in ("tensor", "scalar", "vector"):
                def wrapped(e, fn=fn, name=name):
                    p = _proxies.get(name)
                    if p is None:
                        p = _EP(e, elide_solo=name == "scalar")
                        _proxies[name] = p
                    elif p._e is not e:
                        p._e = e
                    fn(p)
                getattr(block, name)(wrapped)
            else:
                getattr(block, name)(fn)

        def barrier():
            def _w(e):
                for s_ in SYNC_DMA_SEMS:
                    e.wait_ge(s_.h, s_.n)
                e.sem_inc(BARS[0].h, 1)
            def _g(e):
                for s_ in GP_DMA_SEMS:
                    e.wait_ge(s_.h, s_.n)
                e.wait_ge(sC.h, sC.n)
                e.sem_inc(BARS[1].h, 1)
            sec("sync", _w)
            sec("gpsimd", _g)
            sec("tensor", lambda e: e.sem_inc(BARS[2].h, 1))
            sec("scalar", lambda e: e.sem_inc(BARS[3].h, 1))
            sec("vector", lambda e: e.sem_inc(BARS[4].h, 1))
            for b_ in BARS:
                b_.n += 1
            tg = [(b_.h, b_.n) for b_ in BARS]
            for nm in ENG:
                def _wt(e, tg=tg):
                    for h, v in tg:
                        e.wait_ge(h, v)
                sec(nm, _wt)

        def _p0w(e):
            e.dma_start(out=x_sb[:].rearrange("p (t d) -> p t d", d=D),
                        in_=x_in.rearrange("(t p) d -> p t d", p=128)
                        ).then_inc(sW.h, 16)
            sW.n += 16
            for dst, src in ((gain_sb, gain2), (eps_sb, epsv_in),
                             (sg_sb, sg_in), (su_sb, su_in),
                             (sd_sb, sd_in)):
                e.dma_start(out=dst[:], in_=src).then_inc(sW.h, 16)
                sW.n += 16
        sec("sync", _p0w)
        w_small = sW.n

        def _p0g(e):
            for dst, src in ((ident_sb, ident_in), (onesM, onesM_in)):
                e.dma_start(out=dst[:], in_=src).then_inc(sG.h, 16)
                sG.n += 16
        sec("gpsimd", _p0g)

        def _p0s(e):
            e.wait_ge(sW.h, w_small)
            e.activation(out=nag_sb[:], in_=gain_sb[:], func=ACT.Abs
                         ).then_inc(sS.h, 1)
            sS.n += 1
        sec("scalar", _p0s)
        s_abs = sS.n

        def _p0v(e):
            e.wait_ge(sS.h, s_abs)
            e.tensor_scalar(out=nag_sb[:], in0=nag_sb[:], scalar1=-1.0,
                            scalar2=None, op0=ALU.mult)
            e.drain()
        sec("vector", _p0v)
        barrier()

        def emit_ln(tag, ag_ap, xT_sb):
            ph = contextlib.ExitStack()
            with ph:
                lsb = lambda n, s, d: ph.enter_context(
                    nc.sbuf_tensor(tag + n, s, d))
                xh = lsb("xh", [128, NT * D], BF16)
                sqx = lsb("sqx", [128, 2 * D], BF16)
                st8 = lsb("st8", [128, 8], F32)
                sq8 = lsb("sq8", [128, 8], F32)
                m8 = lsb("m8", [128, 8], F32)
                v8 = lsb("v8", [128, 8], F32)
                r8 = lsb("r8", [128, 8], F32)
                r8i = lsb("r8i", [128, 8], F32)
                stg = lsb("stg", [128, 2 * T], BF16) if ag_ap is not None else None
                ptr = ph.enter_context(
                    nc.psum_tensor(tag + "ptr", [128, 2 * T], BF16))

                def _s(e):
                    for t in range(NT):
                        e.activation(out=sqx[:, (t % 2) * D:(t % 2) * D + D],
                                     in_=x_sb[:, t * D:(t + 1) * D],
                                     func=ACT.Square,
                                     accum_out=sq8[:, t:t + 1])
                        e.drain()
                    e.nop().then_inc(sS.h, 1)
                    sS.n += 1
                sec("scalar", _s)
                s_sq = sS.n

                def _v(e):
                    for t in range(NT):
                        e.reduce_sum(out=st8[:, t:t + 1],
                                     in_=x_sb[:, t * D:(t + 1) * D], axis=AX.X)
                    e.drain()
                    e.tensor_scalar(out=m8[:], in0=st8[:], scalar1=1.0 / D,
                                    scalar2=None, op0=ALU.mult)
                    e.drain()
                    e.tensor_tensor(out=v8[:], in0=m8[:], in1=m8[:],
                                    op=ALU.mult)
                    e.drain()
                    e.wait_ge(sS.h, s_sq)
                    e.scalar_tensor_tensor(out=v8[:], in0=sq8[:],
                                           scalar=1.0 / D, in1=v8[:],
                                           op0=ALU.mult, op1=ALU.subtract)
                    e.drain()
                    e.nop().then_inc(sV.h, 1)
                    sV.n += 1
                sec("vector", _v)
                v_var = sV.n

                def _s2(e):
                    e.wait_ge(sV.h, v_var)
                    e.activation(out=r8[:], in_=v8[:], func=ACT.Sqrt,
                                 bias=eps_sb[:, 0:1]).then_inc(sS.h, 1)
                    sS.n += 1
                sec("scalar", _s2)
                s_r = sS.n

                def _v2(e):
                    e.wait_ge(sS.h, s_r)
                    e.reciprocal(out=r8i[:], in_=r8[:])
                    e.drain()
                    for t in range(NT):
                        e.tensor_scalar(out=xh[:, t * D:(t + 1) * D],
                                        in0=x_sb[:, t * D:(t + 1) * D],
                                        scalar1=m8[:, t:t + 1],
                                        scalar2=r8i[:, t:t + 1],
                                        op0=ALU.subtract, op1=ALU.mult)
                    e.drain()
                    e.nop().then_inc(sV.h, 1)
                    sV.n += 1
                sec("vector", _v2)
                v_xh = sV.n

                evac_done = [0, 0]
                dma_done = [0, 0]
                for c in range(KC):
                    half = (c % 2) * T

                    def _t(e, c=c, half=half):
                        if c == 0:
                            e.wait_ge(sV.h, v_xh)
                        if c >= 2:
                            e.wait_ge(sV.h, evac_done[c % 2])
                        for t in range(NT):
                            e.transpose(
                                ptr[:, half + t * 128: half + t * 128 + 128],
                                xh[:, t * D + c * 128: t * D + c * 128 + 128],
                                ident_sb[:])
                        e.drain()
                        e.nop().then_inc(sT.h, 1)
                        sT.n += 1
                    sec("tensor", _t)
                    t_tr = sT.n

                    def _ve(e, c=c, half=half, t_tr=t_tr):
                        e.wait_ge(sT.h, t_tr)
                        if ag_ap is not None and c >= 2:
                            e.wait_ge(pS[c % 2].h, dma_done[c % 2])
                        if xT_sb is not None:
                            e.tensor_copy(out=xT_sb[:, c * T:(c + 1) * T],
                                          in_=ptr[:, half:half + T])
                        if ag_ap is not None:
                            e.tensor_copy(out=stg[:, half:half + T],
                                          in_=ptr[:, half:half + T])
                        e.drain()
                        e.nop().then_inc(sV.h, 1)
                        sV.n += 1
                        evac_done[c % 2] = sV.n
                    sec("vector", _ve)
                    v_e = sV.n

                    if ag_ap is not None:
                        def _w(e, c=c, half=half, v_e=v_e):
                            e.wait_ge(sV.h, v_e)
                            e.dma_start(out=ag_ap[c * 128:(c + 1) * 128, :],
                                        in_=stg[:, half:half + T]
                                        ).then_inc(pS[c % 2].h, 16)
                            pS[c % 2].n += 16
                            dma_done[c % 2] = pS[c % 2].n
                        sec("sync", _w)

        def emit_collective(kind, op, in_ap, out_ap):
            wvs = [(s_.h, s_.n) for s_ in SYNC_DMA_SEMS]

            def _g(e):
                for h_, v_ in wvs:
                    e.wait_ge(h_, v_)
                e.collective_compute(kind, op,
                                     replica_groups=[[0, 1, 2, 3, 4, 5, 6, 7]],
                                     ins=[in_ap], outs=[out_ap]
                                     ).then_inc(sC.h, 1)
            sec("gpsimd", _g)
            sC.n += 1

        def emit_qkv(layer):
            P = f"q{layer}"
            ph = contextlib.ExitStack()
            with ph:
                lsb = lambda n, s, d: ph.enter_context(
                    nc.sbuf_tensor(P + n, s, d))
                psb = lambda n, s, d: ph.enter_context(
                    nc.psum_tensor(P + n, s, d))
                if layer == 0:
                    W = lsb("W", [128, KC * 768], BF16)
                    c0_sb = lsb("c0", [128, 2048], F32)
                    s0_sb = lsb("s0", [128, 2048], F32)
                else:
                    W = lsb("W", [128, KC * 256], BF16)
                    Wkv = lsb("Wkv", [128, 4 * 512], BF16)
                    lg = lsb("lg", [128, 2 * 4 * 512], BF16)
                    wqbtm = lsb("wqb", [128, 1024], F32)
                    c2_sb = lsb("c2", [128, 256], F32)
                    s2_sb = lsb("s2", [128, 256], F32)
                xg = lsb("xg", [128, 2 * KC * 512], BF16)
                raw = lsb("raw", [128, 512], BF16)
                sq = lsb("sq", [128, 512], BF16)
                ss4 = lsb("ss4", [128, 4], F32)
                sd4 = lsb("sd4", [128, 4], F32)
                ri4 = lsb("ri4", [128, 4], F32)
                qn = lsb("qn", [128, 512], BF16)
                ro = lsb("ro", [128, 512], BF16)
                tA = lsb("tA", [128, 256], F32)
                tB = lsb("tB", [128, 256], F32)
                qo = lsb("qo", [128, 2 * 4 * 512], BF16)
                vf = lsb("vf", [128, 2 * 2 * 512], BF16)
                vt = lsb("vt", [128, 2 * 2 * 512], BF16)
                pq = [psb(f"pq{i}", [128, 512], F32) for i in range(2)]
                ptm = psb("ptm", [128, 512], BF16)
                pfm = psb("pfm", [128, 512], BF16)
                pvt = [psb(f"pvt{i}", [128, 512], BF16) for i in range(2)]

                def _g(e):
                    if layer == 0:
                        e.dma_start(
                            out=W[:].rearrange("p (k o) -> p k o", o=768),
                            in_=wqkv_my.rearrange("(k p) o -> p k o", p=128)
                        ).then_inc(sG.h, 16)
                        sG.n += 16
                    else:
                        e.dma_start(
                            out=W[:].rearrange("p (k o) -> p k o", o=256),
                            in_=wq1_my.rearrange("(k p) o -> p k o", p=128)
                        ).then_inc(sG.h, 16)
                        e.dma_start(
                            out=Wkv[:].rearrange("p (k o) -> p k o", o=512),
                            in_=wkvb_my.rearrange("(k p) o -> p k o", p=128)
                        ).then_inc(sG.h, 16)
                        sG.n += 32
                sec("gpsimd", _g)
                g_w = sG.n

                def _w0(e):
                    if layer == 0:
                        for dst, src in ((c0_sb, c0tm), (s0_sb, s0tm)):
                            e.dma_start(out=dst[:], in_=src).then_inc(sW.h, 16)
                            sW.n += 16
                    else:
                        for dst, src in ((wqbtm, wqbtm_in), (c2_sb, c2tm_in),
                                         (s2_sb, s2tm_in)):
                            e.dma_start(out=dst[:], in_=src).then_inc(sW.h, 16)
                            sW.n += 16
                sec("sync", _w0)
                w_tabs = sW.n

                ag_out = ag_out_d[layer]
                scq, sck, scv = sc_q[layer], sc_k[layer], sc_v[layer]
                ocw = 768 if layer == 0 else 256

                xg_ready = [0, 0]
                lg_ready = [0, 0]
                xg_free = [0, 0]
                qo_free = [0, 0]
                vt_free = [0, 0]
                pvt_free = [0, 0]
                raw_free = [0]
                pfm_free = [0]
                evacq = [0, 0]
                mm_done = [0] * 6

                for ts in range(NTS):
                    par = ts % 2
                    xoff = par * KC * 512

                    def _w(e, ts=ts, par=par, xoff=xoff):
                        e.wait_ge(sT.h, xg_free[par])
                        for kc in range(KC):
                            e.dma_start(
                                out=xg[:, xoff + kc * 512:
                                       xoff + kc * 512 + 512],
                                in_=ag_out[2048 * (ts // 2) + kc * 128:
                                           2048 * (ts // 2) + kc * 128 + 128,
                                           512 * (ts % 2):
                                           512 * (ts % 2) + 512]
                            ).then_inc(sX[par].h, 16)
                            sX[par].n += 16
                        if layer == 1:
                            for lc in range(4):
                                e.dma_start(
                                    out=lg[:, par * 2048 + lc * 512:
                                           par * 2048 + lc * 512 + 512],
                                    in_=agl_out[512 * (ts // 2) + lc * 128:
                                                512 * (ts // 2) + lc * 128 + 128,
                                                512 * (ts % 2):
                                                512 * (ts % 2) + 512]
                                ).then_inc(sX[par].h, 16)
                                sX[par].n += 16
                        xg_ready[par] = sX[par].n
                        lg_ready[par] = sX[par].n
                    sec("sync", _w)

                    if os.environ.get("QSKIP") == "5":
                        def _t5(e, par=par):
                            e.wait_ge(sW.h, xg_ready[par])
                            e.drain()
                            e.nop().then_inc(sT.h, 1)
                            sT.n += 1
                        sec("tensor", _t5)
                        xg_free[par] = sT.n
                        continue
                    for oc in range(6):
                        def _t(e, ts=ts, oc=oc, par=par, xoff=xoff):
                            e.wait_ge(sG.h, g_w)
                            e.wait_ge(sW.h, w_tabs)
                            e.wait_ge(sX[par].h, xg_ready[par])
                            e.wait_ge(sV.h, evacq[oc % 2])
                            if layer == 0 or oc < 2:
                                for kc in range(KC):
                                    e.matmul(
                                        pq[oc % 2][:],
                                        W[:, kc * ocw + oc * 128:
                                          kc * ocw + oc * 128 + 128],
                                        xg[:, xoff + kc * 512:
                                           xoff + kc * 512 + 512],
                                        start=(kc == 0), stop=(kc == KC - 1))
                            else:
                                kvoc = oc - 2
                                for lc in range(4):
                                    e.matmul(
                                        pq[oc % 2][:],
                                        Wkv[:, lc * 512 + kvoc * 128:
                                            lc * 512 + kvoc * 128 + 128],
                                        lg[:, par * 2048 + lc * 512:
                                           par * 2048 + lc * 512 + 512],
                                        start=(lc == 0), stop=(lc == 3))
                            e.drain()
                            e.nop().then_inc(sT.h, 1)
                            sT.n += 1
                        sec("tensor", _t)
                        mm_done[oc] = sT.n
                        if oc == 5:
                            xg_free[par] = sT.n
                        if os.environ.get("QSKIP") == "4":
                            def _v4(e, oc=oc):
                                e.wait_ge(sT.h, mm_done[oc])
                                e.drain()
                                e.nop().then_inc(sV.h, 1)
                                sV.n += 1
                                evacq[oc % 2] = sV.n
                            sec("vector", _v4)
                            continue

                        if oc < 4:
                            hh = oc % 2
                            is_q = oc < 2
                            do_rope = (layer == 1) or (hh == 0)
                            if os.environ.get("QSKIP") in ("1", "3", "6"):
                                qoff = par * 2048 + (oc % 4) * 512

                                def _vz(e, oc=oc, par=par, qoff=qoff):
                                    e.wait_ge(sT.h, mm_done[oc])
                                    e.wait_ge(sQ[par].h, qo_free[par])
                                    e.tensor_copy(out=qo[:, qoff:qoff + 512],
                                                  in_=pq[oc % 2][:])
                                    e.drain()
                                    e.nop().then_inc(sV.h, 1)
                                    sV.n += 1
                                    evacq[oc % 2] = sV.n
                                sec("vector", _vz)
                                v_qo = sV.n
                                dst = scq if is_q else sck
                                if os.environ.get("QSKIP") == "6":
                                    qo_free[par] = 0
                                    continue

                                def _wz(e, ts=ts, oc=oc, hh=hh, dst=dst,
                                        par=par, qoff=qoff, v_qo=v_qo):
                                    e.wait_ge(sV.h, v_qo)
                                    e.dma_start(
                                        out=dst[hh * 128:(hh + 1) * 128,
                                                ts * 512: ts * 512 + 512],
                                        in_=qo[:, qoff:qoff + 512]
                                    ).then_inc(sQ[par].h, 16)
                                    sQ[par].n += 16
                                    if oc == 3:
                                        qo_free[par] = sQ[par].n
                                sec("sync", _wz)
                                continue

                            def _v(e, oc=oc):
                                e.wait_ge(sT.h, mm_done[oc])
                                e.wait_ge(sT.h, raw_free[0])
                                e.tensor_copy(out=raw[:], in_=pq[oc % 2][:])
                                e.drain()
                                e.nop().then_inc(sV.h, 1)
                                sV.n += 1
                                evacq[oc % 2] = sV.n
                            sec("vector", _v)
                            v_raw = sV.n

                            def _t2(e, v_raw=v_raw):
                                e.wait_ge(sV.h, v_raw)
                                for q in range(4):
                                    e.transpose(
                                        ptm[:, q * 128: q * 128 + 128],
                                        raw[:, q * 128: q * 128 + 128],
                                        ident_sb[:])
                                e.drain()
                                e.nop().then_inc(sT.h, 1)
                                sT.n += 1
                            sec("tensor", _t2)
                            t_tm = sT.n
                            raw_free[0] = t_tm

                            if layer == 0:
                                def _vn(e, t_tm=t_tm):
                                    e.wait_ge(sT.h, t_tm)
                                    e.wait_ge(sS.h, sS.n)
                                    e.tensor_copy(out=qn[:], in_=ptm[:])
                                    e.drain()
                                    e.tensor_tensor(out=sq[:], in0=qn[:],
                                                    in1=qn[:], op=ALU.mult)
                                    e.drain()
                                    e.reduce_sum(
                                        out=ss4[:],
                                        in_=sq[:].rearrange(
                                            "p (q d) -> p q d", d=128),
                                        axis=AX.X)
                                    e.drain()
                                    e.nop().then_inc(sV.h, 1)
                                    sV.n += 1
                                sec("vector", _vn)
                                v_ss = sV.n

                                def _sn(e, v_ss=v_ss):
                                    e.wait_ge(sV.h, v_ss)
                                    e.activation(out=sd4[:], in_=ss4[:],
                                                 func=ACT.Sqrt,
                                                 bias=eps_sb[:, 1:2]
                                                 ).then_inc(sS.h, 1)
                                    e.drain()
                                    sS.n += 1
                                sec("scalar", _sn)
                                s_n = sS.n

                                def _vap(e, ts=ts, do_rope=do_rope, s_n=s_n):
                                    e.wait_ge(sS.h, s_n)
                                    e.reciprocal(out=ri4[:], in_=sd4[:])
                                    e.drain()
                                    for q in range(4):
                                        e.tensor_scalar(
                                            out=qn[:, q * 128: q * 128 + 128],
                                            in0=qn[:, q * 128: q * 128 + 128],
                                            scalar1=ri4[:, q:q + 1],
                                            scalar2=None, op0=ALU.mult)
                                    e.drain()
                                    if do_rope:
                                        po_ = (ts % 8) * 256
                                        cv = c0_sb[:, po_: po_ + 256] \
                                            .rearrange("p (q d) -> p q d", d=64)
                                        sv = s0_sb[:, po_: po_ + 256] \
                                            .rearrange("p (q d) -> p q d", d=64)
                                        qv = qn[:].rearrange(
                                            "p (q h d) -> p q h d", h=2, d=64)
                                        rv = ro[:].rearrange(
                                            "p (q h d) -> p q h d", h=2, d=64)
                                        tAv = tA[:].rearrange(
                                            "p (q d) -> p q d", d=64)
                                        tBv = tB[:].rearrange(
                                            "p (q d) -> p q d", d=64)
                                        a = qv[:, :, 0, :]
                                        bq = qv[:, :, 1, :]
                                        e.tensor_tensor(out=tAv, in0=a,
                                                        in1=cv, op=ALU.mult)
                                        e.tensor_tensor(out=tBv, in0=bq,
                                                        in1=sv, op=ALU.mult)
                                        e.drain()
                                        e.tensor_tensor(out=rv[:, :, 0, :],
                                                        in0=tAv, in1=tBv,
                                                        op=ALU.subtract)
                                        e.drain()
                                        e.tensor_tensor(out=tAv, in0=a,
                                                        in1=sv, op=ALU.mult)
                                        e.tensor_tensor(out=tBv, in0=bq,
                                                        in1=cv, op=ALU.mult)
                                        e.drain()
                                        e.tensor_tensor(out=rv[:, :, 1, :],
                                                        in0=tAv, in1=tBv,
                                                        op=ALU.add)
                                        e.drain()
                                    e.nop().then_inc(sV.h, 1)
                                    sV.n += 1
                                sec("vector", _vap)
                                v_ro = sV.n
                                src_tile = ro if do_rope else qn
                            else:
                                def _vap(e, hh=hh, is_q=is_q, t_tm=t_tm):
                                    e.wait_ge(sT.h, t_tm)
                                    if is_q:
                                        e.tensor_tensor(
                                            out=qn[:], in0=ptm[:],
                                            in1=wqbtm[:, hh * 512:
                                                      hh * 512 + 512],
                                            op=ALU.add)
                                    else:
                                        e.tensor_copy(out=qn[:], in_=ptm[:])
                                    e.drain()
                                    qlo = qn[:].rearrange(
                                        "p (q h d) -> p q h d", h=2, d=64)
                                    rlo = ro[:].rearrange(
                                        "p (q h d) -> p q h d", h=2, d=64)
                                    e.tensor_copy(out=rlo[:, :, 0, :],
                                                  in_=qlo[:, :, 0, :])
                                    e.drain()
                                    qv = qn[:].rearrange(
                                        "p (q h j t) -> p q h j t",
                                        h=2, j=32, t=2)
                                    rv = ro[:].rearrange(
                                        "p (q h j t) -> p q h j t",
                                        h=2, j=32, t=2)
                                    a = qv[:, :, 1, :, 0]
                                    bq = qv[:, :, 1, :, 1]
                                    cv = c2_sb[:, hh * 128: hh * 128 + 128] \
                                        .rearrange("p (q j) -> p q j", j=32)
                                    sv = s2_sb[:, hh * 128: hh * 128 + 128] \
                                        .rearrange("p (q j) -> p q j", j=32)
                                    tAv = tA[:, 0:128].rearrange(
                                        "p (q j) -> p q j", j=32)
                                    tBv = tB[:, 0:128].rearrange(
                                        "p (q j) -> p q j", j=32)
                                    e.tensor_tensor(out=tAv, in0=a, in1=cv,
                                                    op=ALU.mult)
                                    e.tensor_tensor(out=tBv, in0=bq, in1=sv,
                                                    op=ALU.mult)
                                    e.drain()
                                    e.tensor_tensor(out=rv[:, :, 1, :, 0],
                                                    in0=tAv, in1=tBv,
                                                    op=ALU.subtract)
                                    e.drain()
                                    e.tensor_tensor(out=tAv, in0=a, in1=sv,
                                                    op=ALU.mult)
                                    e.tensor_tensor(out=tBv, in0=bq, in1=cv,
                                                    op=ALU.mult)
                                    e.drain()
                                    e.tensor_tensor(out=rv[:, :, 1, :, 1],
                                                    in0=tAv, in1=tBv,
                                                    op=ALU.add)
                                    e.drain()
                                    e.nop().then_inc(sV.h, 1)
                                    sV.n += 1
                                sec("vector", _vap)
                                v_ro = sV.n
                                src_tile = ro

                            def _t3(e, v_ro=v_ro, src_tile=src_tile):
                                e.wait_ge(sV.h, v_ro)
                                e.wait_ge(sV.h, pfm_free[0])
                                for q in range(4):
                                    e.transpose(
                                        pfm[:, q * 128: q * 128 + 128],
                                        src_tile[:, q * 128: q * 128 + 128],
                                        ident_sb[:])
                                e.drain()
                                e.nop().then_inc(sT.h, 1)
                                sT.n += 1
                            sec("tensor", _t3)
                            t_fm = sT.n

                            qoff = par * 2048 + (oc % 4) * 512

                            def _vq(e, par=par, qoff=qoff, t_fm=t_fm):
                                e.wait_ge(sT.h, t_fm)
                                e.wait_ge(sQ[par].h, qo_free[par])
                                e.tensor_copy(out=qo[:, qoff:qoff + 512],
                                              in_=pfm[:])
                                e.drain()
                                e.nop().then_inc(sV.h, 1)
                                sV.n += 1
                                pfm_free[0] = sV.n
                            sec("vector", _vq)
                            v_qo = sV.n

                            dst = scq if is_q else sck

                            def _wq(e, ts=ts, oc=oc, hh=hh, dst=dst, par=par,
                                    qoff=qoff, v_qo=v_qo):
                                e.wait_ge(sV.h, v_qo)
                                e.dma_start(
                                    out=dst[hh * 128:(hh + 1) * 128,
                                            ts * 512: ts * 512 + 512],
                                    in_=qo[:, qoff:qoff + 512]
                                ).then_inc(sQ[par].h, 16)
                                sQ[par].n += 16
                                if oc == 3:
                                    qo_free[par] = sQ[par].n
                            sec("sync", _wq)
                        else:
                            j = oc - 4
                            if os.environ.get("QSKIP") in ("2", "3", "6"):
                                def _vz2(e, oc=oc, j=j, par=par):
                                    e.wait_ge(sT.h, mm_done[oc])
                                    e.tensor_copy(
                                        out=vf[:, par * 1024 + j * 512:
                                                par * 1024 + j * 512 + 512],
                                        in_=pq[oc % 2][:])
                                    e.drain()
                                    e.nop().then_inc(sV.h, 1)
                                    sV.n += 1
                                    evacq[oc % 2] = sV.n
                                sec("vector", _vz2)
                                continue

                            def _v(e, oc=oc, j=j, par=par):
                                e.wait_ge(sT.h, mm_done[oc])
                                e.wait_ge(sV.h, pvt_free[j])
                                e.tensor_copy(
                                    out=vf[:, par * 1024 + j * 512:
                                            par * 1024 + j * 512 + 512],
                                    in_=pq[oc % 2][:])
                                e.drain()
                                e.nop().then_inc(sV.h, 1)
                                sV.n += 1
                                evacq[oc % 2] = sV.n
                            sec("vector", _v)
                            v_vf = sV.n

                            def _t2(e, j=j, par=par, v_vf=v_vf):
                                e.wait_ge(sV.h, v_vf)
                                for q in range(4):
                                    e.transpose(
                                        pvt[j][:, q * 128: q * 128 + 128],
                                        vf[:, par * 1024 + j * 512 + q * 128:
                                           par * 1024 + j * 512 + q * 128 + 128],
                                        ident_sb[:])
                                e.drain()
                                e.nop().then_inc(sT.h, 1)
                                sT.n += 1
                            sec("tensor", _t2)
                            t_vt = sT.n

                            def _v2(e, j=j, par=par, t_vt=t_vt):
                                e.wait_ge(sT.h, t_vt)
                                e.wait_ge(sU[par].h, vt_free[par])
                                e.tensor_copy(
                                    out=vt[:, par * 1024 + j * 512:
                                            par * 1024 + j * 512 + 512],
                                    in_=pvt[j][:])
                                e.drain()
                                e.nop().then_inc(sV.h, 1)
                                sV.n += 1
                                pvt_free[j] = sV.n
                            sec("vector", _v2)
                            v_vt = sV.n

                            def _w2(e, ts=ts, j=j, par=par, v_vt=v_vt):
                                e.wait_ge(sV.h, v_vt)
                                e.dma_start(
                                    out=scv[j * 8192 + ts * 512:
                                            j * 8192 + ts * 512 + 512, :]
                                    .rearrange("(q p) d -> p q d", p=128),
                                    in_=vt[:, par * 1024 + j * 512:
                                           par * 1024 + j * 512 + 512]
                                    .rearrange("p (q d) -> p q d", d=128)
                                ).then_inc(sU[par].h, 16)
                                sU[par].n += 16
                                if j == 1:
                                    vt_free[par] = sU[par].n
                            sec("sync", _w2)

        def emit_attn(layer):
            P = f"a{layer}"
            pho = contextlib.ExitStack()
            with pho:
                y_sb = pho.enter_context(
                    nc.sbuf_tensor(P + "y", [128, 64 * 256], BF16))
                scq, sck, scv = sc_q[layer], sc_k[layer], sc_v[layer]
                rs_in = rs_in_d[layer]

                ph = contextlib.ExitStack()
                with ph:
                    lsb = lambda n, s, d: ph.enter_context(
                        nc.sbuf_tensor(P + n, s, d))
                    psb = lambda n, s, d: ph.enter_context(
                        nc.psum_tensor(P + n, s, d))
                    qs = lsb("qs", [128, 2 * S], BF16)
                    ks = lsb("ks", [128, 2 * S], BF16)
                    vs = lsb("vs", [128, 2 * 32 * 129], BF16)
                    mask_sb = lsb("mask", [128, 4 * 512], BF16)
                    pr = [lsb(f"pr{i}", [128, 512], BF16) for i in range(2)]
                    lp = lsb("lp", [128, 1], F32)
                    rp = lsb("rp", [128, 1], F32)
                    ps = [psb(f"ps{i}", [128, 512], F32) for i in range(2)]
                    py = [psb(f"py{i}", [128, 129], F32) for i in range(4)]
                    if layer == 1:
                        pcb = psb("pcb", [128, 512], F32)
                        q2f = lsb("q2f", [128, 512], BF16)
                        kkm = lsb("kkm", [128, 1], F32)
                        skk = lsb("skk", [128, 1], F32)
                        sqf = lsb("sqf", [128, S], BF16)
                        cbc = lsb("cbc", [128, 2 * S], BF16)

                    def _gm(e):
                        e.dma_start(out=mask_sb[:], in_=maskT_in
                                    ).then_inc(sG.h, 16)
                        for pp_ in range(2):
                            e.dma_start(
                                out=vs[:, pp_ * 32 * 129:
                                       (pp_ + 1) * 32 * 129]
                                .rearrange("p (j c) -> p j c", c=129)
                                [:, :, 128:129],
                                in_=onescol_in[:, 0:32]
                                .rearrange("p (j o) -> p j o", o=1),
                                allow_slow_non_contiguous=True
                            ).then_inc(sG.h, 16)
                        sG.n += 48
                    sec("gpsimd", _gm)
                    g_mask = sG.n

                    seq_free = [0, 0]
                    py_free = [0]
                    s_cbc = 0
                    for si in range(4):
                        b, hh = si // 2, si % 2
                        par = si % 2

                        def _w(e, b=b, hh=hh, par=par, si=si):
                            e.wait_ge(sT.h, seq_free[par])
                            if si == 0:
                                e.wait_ge(sG.h, g_mask)
                            e.dma_start(out=qs[:, par * S: par * S + S],
                                        in_=scq[hh * 128:(hh + 1) * 128,
                                                b * S:(b + 1) * S]
                                        ).then_inc(sL[par].h, 16)
                            e.dma_start(out=ks[:, par * S: par * S + S],
                                        in_=sck[hh * 128:(hh + 1) * 128,
                                                b * S:(b + 1) * S]
                                        ).then_inc(sL[par].h, 16)
                            e.dma_start(
                                out=vs[:, par * 32 * 129:
                                       (par + 1) * 32 * 129]
                                .rearrange("p (j c) -> p j c", c=129)
                                [:, :, 0:128],
                                in_=scv[hh * 8192 + b * S:
                                        hh * 8192 + (b + 1) * S, :]
                                .rearrange("(j p) c -> p j c", p=128)
                            ).then_inc(sL[par].h, 16)
                            sL[par].n += 48
                        sec("sync", _w)
                        w_seq = sL[par].n
                        w_par = par

                        if layer == 1:
                            for i in range(8):
                                def _vk(e, i=i, par=par, w_seq=w_seq):
                                    if i == 0:
                                        e.wait_ge(sL[par].h, w_seq)
                                    e.wait_ge(sT.h, sT.n)
                                    e.tensor_tensor(
                                        out=q2f[:],
                                        in0=ks[:, par * S + i * 512:
                                               par * S + i * 512 + 512],
                                        in1=ks[:, par * S + i * 512:
                                               par * S + i * 512 + 512],
                                        op=ALU.mult)
                                    e.drain()
                                    e.nop().then_inc(sV.h, 1)
                                    sV.n += 1
                                sec("vector", _vk)
                                v_sq = sV.n

                                def _tk(e, v_sq=v_sq):
                                    e.wait_ge(sV.h, v_sq)
                                    e.matmul(pcb[:], onesM[:], q2f[:],
                                             start=True, stop=True)
                                    e.drain()
                                    e.nop().then_inc(sT.h, 1)
                                    sT.n += 1
                                sec("tensor", _tk)
                                t_i = sT.n

                                def _vk2(e, i=i, t_i=t_i):
                                    e.wait_ge(sT.h, t_i)
                                    if i == 0:
                                        e.reduce_max(out=kkm[:], in_=pcb[:],
                                                     axis=AX.X)
                                        e.drain()
                                    else:
                                        e.reduce_max(out=skk[:], in_=pcb[:],
                                                     axis=AX.X)
                                        e.drain()
                                        e.tensor_tensor(out=kkm[:],
                                                        in0=kkm[:],
                                                        in1=skk[:],
                                                        op=ALU.max)
                                        e.drain()
                                    e.nop().then_inc(sV.h, 1)
                                    sV.n += 1
                                sec("vector", _vk2)
                            for i in range(8):
                                def _vq(e, i=i, par=par):
                                    e.wait_ge(sT.h, sT.n)
                                    e.tensor_tensor(
                                        out=q2f[:],
                                        in0=qs[:, par * S + i * 512:
                                               par * S + i * 512 + 512],
                                        in1=qs[:, par * S + i * 512:
                                               par * S + i * 512 + 512],
                                        op=ALU.mult)
                                    e.drain()
                                    e.nop().then_inc(sV.h, 1)
                                    sV.n += 1
                                sec("vector", _vq)
                                v_sq = sV.n

                                def _tq(e, v_sq=v_sq):
                                    e.wait_ge(sV.h, v_sq)
                                    e.wait_ge(sS.h, sS.n)
                                    e.matmul(pcb[:], onesM[:], q2f[:],
                                             start=True, stop=True)
                                    e.drain()
                                    e.nop().then_inc(sT.h, 1)
                                    sT.n += 1
                                sec("tensor", _tq)
                                t_i = sT.n

                                def _sq(e, i=i, t_i=t_i):
                                    e.wait_ge(sT.h, t_i)
                                    e.activation(out=sqf[:, i * 512:
                                                        i * 512 + 512],
                                                 in_=pcb[:], func=ACT.Sqrt
                                                 ).then_inc(sS.h, 1)
                                    e.drain()
                                    sS.n += 1
                                sec("scalar", _sq)
                            v_kkm = sV.n
                            s_sqq = sS.n

                            def _sc(e, v_kkm=v_kkm, s_sqq=s_sqq,
                                    w_par=w_par):
                                e.wait_ge(sV.h, v_kkm)
                                e.wait_ge(sS.h, s_sqq)
                                e.activation(out=skk[:], in_=kkm[:],
                                             func=ACT.Sqrt)
                                e.drain()
                                e.nop().then_inc(sS.h, 1)
                                sS.n += 1
                                e.activation(out=cbc[:, w_par * S:
                                                    w_par * S + S],
                                             in_=sqf[:],
                                             func=ACT.Relu,
                                             scale=skk[:, 0:1],
                                             bias=eps_sb[:, 2:3]
                                             ).then_inc(sS.h, 1)
                                sS.n += 1
                            sec("scalar", _sc)
                        s_cbc = sS.n

                        for qc in range(8):
                            nkc = 4 * qc + 4
                            T0 = sT.n
                            S0 = sS.n
                            V0 = sV.n
                            sub_idx = {}
                            mask_idx = {}
                            vidx = 0
                            for kc in range(nkc):
                                if layer == 1:
                                    vidx += 1
                                    sub_idx[kc] = vidx
                                if kc >= 4 * qc:
                                    vidx += 1
                                    mask_idx[kc] = vidx
                            nvec = vidx

                            def _t(e, qc=qc, nkc=nkc, par=par, T0=T0,
                                   S0=S0, V0=V0, mask_idx=mask_idx,
                                   w_seq=w_seq):
                                e.wait_ge(sL[par].h, w_seq)
                                e.wait_ge(sV.h, V0)
                                e.wait_ge(sV.h, py_free[0])

                                def score(kc):
                                    if kc >= 2:
                                        e.wait_ge(sS.h, S0 + kc - 1)
                                    e.matmul(ps[kc % 2][:],
                                             ks[:, par * S + kc * 128:
                                                par * S + kc * 128 + 128],
                                             qs[:, par * S + qc * 512:
                                                par * S + qc * 512 + 512],
                                             start=True, stop=True)
                                    e.drain()
                                    e.nop().then_inc(sT.h, 1)

                                def pv(kc):
                                    e.wait_ge(sS.h, S0 + kc + 1)
                                    if kc >= 4 * qc:
                                        e.wait_ge(sV.h, V0 + mask_idx[kc])
                                    for qt in range(4):
                                        e.matmul(
                                            py[qt][:],
                                            pr[kc % 2][:, qt * 128:
                                                       qt * 128 + 128],
                                            vs[:, par * 32 * 129 + kc * 129:
                                               par * 32 * 129 + kc * 129 + 129],
                                            start=(kc == 0),
                                            stop=(kc == nkc - 1))
                                    e.drain()
                                    e.nop().then_inc(sT.h, 1)

                                score(0)
                                for kc in range(1, nkc):
                                    score(kc)
                                    pv(kc - 1)
                                pv(nkc - 1)
                                sT.n += 2 * nkc
                            sec("tensor", _t)
                            t_qc_end = sT.n

                            def idx_score(kc):
                                return 1 if kc == 0 else 2 * kc

                            def idx_pv(kc):
                                return 2 * kc + 3 if kc < nkc - 1 else 2 * nkc

                            def _vsm(e, qc=qc, nkc=nkc, par=par, T0=T0,
                                     S0=S0, s_cbc=(s_cbc if layer == 1
                                                   else 0)):
                                if layer == 1:
                                    e.wait_ge(sS.h, s_cbc)
                                for kc in range(nkc):
                                    if layer == 1:
                                        e.wait_ge(sT.h, T0 + idx_score(kc))
                                        e.tensor_tensor(
                                            out=ps[kc % 2][:],
                                            in0=ps[kc % 2][:],
                                            in1=cbc[:, par * S + qc * 512:
                                                    par * S + qc * 512 + 512],
                                            op=ALU.subtract)
                                        e.drain()
                                        e.nop().then_inc(sV.h, 1)
                                    if kc >= 4 * qc:
                                        m = kc - 4 * qc
                                        e.wait_ge(sS.h, S0 + kc + 1)
                                        e.tensor_tensor(
                                            out=pr[kc % 2][:],
                                            in0=pr[kc % 2][:],
                                            in1=mask_sb[:, m * 512:
                                                        m * 512 + 512],
                                            op=ALU.mult)
                                        e.drain()
                                        e.nop().then_inc(sV.h, 1)
                                sV.n += (nkc if layer == 1 else 0) + 4
                            sec("vector", _vsm)

                            def _s(e, nkc=nkc, T0=T0, V0=V0, hh=hh,
                                   sub_idx=sub_idx):
                                for kc in range(nkc):
                                    if layer == 1:
                                        e.wait_ge(sV.h, V0 + sub_idx[kc])
                                    else:
                                        e.wait_ge(sT.h, T0 + idx_score(kc))
                                    if kc >= 2:
                                        e.wait_ge(sT.h, T0 + idx_pv(kc - 2))
                                    if layer == 0:
                                        e.activation(
                                            out=pr[kc % 2][:],
                                            in_=ps[kc % 2][:], func=ACT.Exp,
                                            scale=gain_sb[:, hh:hh + 1],
                                            bias=nag_sb[:, hh:hh + 1]
                                        ).then_inc(sS.h, 1)
                                        e.drain()
                                    else:
                                        e.activation(
                                            out=pr[kc % 2][:],
                                            in_=ps[kc % 2][:], func=ACT.Exp,
                                            scale=INV_SQRT_DH
                                        ).then_inc(sS.h, 1)
                                        e.drain()
                                sS.n += nkc
                            sec("scalar", _s)

                            def _ve(e, qc=qc, b=b, hh=hh,
                                    t_qc_end=t_qc_end):
                                e.wait_ge(sT.h, t_qc_end)
                                for qt in range(4):
                                    g = 32 * b + 4 * qc + qt
                                    e.tensor_scalar(out=lp[:],
                                                    in0=py[qt][:, 128:129],
                                                    scalar1=1e-30,
                                                    scalar2=None, op0=ALU.add)
                                    e.drain()
                                    e.reciprocal(out=rp[:], in_=lp[:])
                                    e.drain()
                                    e.tensor_scalar(
                                        out=y_sb[:, g * 256 + hh * 128:
                                                 g * 256 + hh * 128 + 128],
                                        in0=py[qt][:, 0:128],
                                        scalar1=rp[:, 0:1], scalar2=None,
                                        op0=ALU.mult)
                                    e.drain()
                                e.nop().then_inc(sV.h, 1)
                                sV.n += 1
                            sec("vector", _ve)
                            py_free[0] = sV.n
                        seq_free[par] = t_qc_end
                barrier()

                ph2 = contextlib.ExitStack()
                with ph2:
                    lsb2 = lambda n, s, d: ph2.enter_context(
                        nc.sbuf_tensor(P + n, s, d))
                    psb2 = lambda n, s, d: ph2.enter_context(
                        nc.psum_tensor(P + n, s, d))
                    yT = lsb2("yT", [128, 2 * 8192], BF16)
                    po = lsb2("po", [128, 2 * D], BF16)
                    Wo = lsb2("Wo", [128, 2 * D], BF16)
                    ptp = [psb2(f"ptp{i}", [128, 512], BF16)
                           for i in range(2)]
                    pso = [psb2(f"pso{i}", [128, 512], F32)
                           for i in range(2)]
                    wo_in = wout0_my if layer == 0 else wout1_my

                    def _g(e):
                        e.dma_start(
                            out=Wo[:].rearrange("p (k o) -> p k o", o=D),
                            in_=wo_in.rearrange("(k p) o -> p k o", p=128)
                        ).then_inc(sG.h, 16)
                        sG.n += 16
                    sec("gpsimd", _g)
                    g_wo = sG.n

                    tp_free = [0, 0]
                    for hh2 in range(2):
                        for gg in range(16):
                            pbuf = gg % 2

                            def _t(e, hh2=hh2, gg=gg, pbuf=pbuf):
                                e.wait_ge(sV.h, tp_free[pbuf])
                                for k in range(4):
                                    g4 = gg * 4 + k
                                    e.transpose(
                                        ptp[pbuf][:, k * 128: k * 128 + 128],
                                        y_sb[:, g4 * 256 + hh2 * 128:
                                             g4 * 256 + hh2 * 128 + 128],
                                        ident_sb[:])
                                e.drain()
                                e.nop().then_inc(sT.h, 1)
                                sT.n += 1
                            sec("tensor", _t)
                            t_tp = sT.n

                            def _v(e, hh2=hh2, gg=gg, pbuf=pbuf, t_tp=t_tp):
                                e.wait_ge(sT.h, t_tp)
                                e.tensor_copy(
                                    out=yT[:, hh2 * 8192 + gg * 512:
                                           hh2 * 8192 + gg * 512 + 512],
                                    in_=ptp[pbuf][:])
                                e.drain()
                                e.nop().then_inc(sV.h, 1)
                                sV.n += 1
                                tp_free[pbuf] = sV.n
                            sec("vector", _v)
                    v_yT = sV.n

                    po_free = [0, 0]
                    for tt in range(64):
                        par2 = tt % 2
                        vbase = sV.n

                        def _t(e, tt=tt, par2=par2, vbase=vbase):
                            if tt == 0:
                                e.wait_ge(sV.h, v_yT)
                                e.wait_ge(sG.h, g_wo)
                            e.wait_ge(sV.h, vbase)
                            e.wait_ge(sP[par2].h, po_free[par2])
                            for grp in range(4):
                                if grp >= 2:
                                    e.wait_ge(sV.h, vbase + grp - 1)
                                for hh2 in range(2):
                                    e.matmul(
                                        pso[grp % 2][:],
                                        yT[:, hh2 * 8192 + tt * 128:
                                           hh2 * 8192 + tt * 128 + 128],
                                        Wo[:, hh2 * D + grp * 512:
                                           hh2 * D + grp * 512 + 512],
                                        start=(hh2 == 0), stop=(hh2 == 1))
                                e.drain()
                                e.nop().then_inc(sT.h, 1)
                            sT.n += 4
                        sec("tensor", _t)
                        t_mm = sT.n

                        def _v(e, par2=par2, t_mm=t_mm):
                            for grp in range(4):
                                e.wait_ge(sT.h, t_mm - 4 + grp + 1)
                                e.tensor_copy(
                                    out=po[:, par2 * D + grp * 512:
                                           par2 * D + grp * 512 + 512],
                                    in_=pso[grp % 2][:])
                                e.drain()
                                e.nop().then_inc(sV.h, 1)
                            sV.n += 4
                        sec("vector", _v)
                        v_po = sV.n

                        def _w(e, tt=tt, par2=par2, v_po=v_po):
                            e.wait_ge(sV.h, v_po)
                            e.dma_start(
                                out=rs_in[tt * 128:(tt + 1) * 128, :],
                                in_=po[:, par2 * D: par2 * D + D]
                            ).then_inc(sP[par2].h, 16)
                            sP[par2].n += 16
                            po_free[par2] = sP[par2].n
                        sec("sync", _w)

                    emit_collective("ReduceScatter", ALU.add, rs_in,
                                    rs_out_d[layer])

        def emit_residual(layer, add_bias):
            ph = contextlib.ExitStack()
            with ph:
                ro2 = ph.enter_context(
                    nc.sbuf_tensor(f"rr{layer}", [128, NT * D], BF16))
                outb_sb = ph.enter_context(
                    nc.sbuf_tensor(f"ob{layer}", [128, D], F32))
                cwait = sC.n

                def _w(e):
                    e.wait_ge(sC.h, cwait)
                    e.dma_start(
                        out=ro2[:].rearrange("p (t d) -> p t d", d=D),
                        in_=rs_out_d[layer].rearrange("(t p) d -> p t d",
                                                      p=128)
                    ).then_inc(sW.h, 16)
                    sW.n += 16
                    if add_bias:
                        e.dma_start(out=outb_sb[:], in_=outb_in
                                    ).then_inc(sW.h, 16)
                        sW.n += 16
                sec("sync", _w)
                w_ro = sW.n

                def _v(e):
                    e.wait_ge(sW.h, w_ro)
                    for t in range(NT):
                        e.tensor_tensor(out=x_sb[:, t * D:(t + 1) * D],
                                        in0=x_sb[:, t * D:(t + 1) * D],
                                        in1=ro2[:, t * D:(t + 1) * D],
                                        op=ALU.add)
                        if add_bias:
                            e.drain()
                            e.tensor_tensor(out=x_sb[:, t * D:(t + 1) * D],
                                            in0=x_sb[:, t * D:(t + 1) * D],
                                            in1=outb_sb[:], op=ALU.add)
                    e.drain()
                    e.nop().then_inc(sV.h, 1)
                    sV.n += 1
                sec("vector", _v)

        def emit_ffn():
            ph = contextlib.ExitStack()
            with ph:
                lsb = lambda n, s, d: ph.enter_context(
                    nc.sbuf_tensor("f" + n, s, d))
                psb = lambda n, s, d: ph.enter_context(
                    nc.psum_tensor("f" + n, s, d))
                pbc = psb("pbc", [128, 512], F32)
                phg = contextlib.ExitStack()
                gsbf = lambda n, s, d: phg.enter_context(
                    nc.sbuf_tensor("fg" + n, s, d, side="right"))
                xT = gsbf("xT", [128, KC * T], BF16)
                emit_ln("L2", None, xT)
                barrier()
                hT = lsb("hT", [128, HC * T], BF16)
                S128 = gsbf("S128", [128, T], BF16)
                wg = gsbf("wg", [128, 2 * 2048], BF16)
                wu = gsbf("wu", [128, 2048], BF16)
                tgs = gsbf("tgs", [128, 512], F32)
                gsb = gsbf("gsb", [128, 512], BF16)
                pg = phg.enter_context(nc.psum_tensor("fpg", [128, 512], F32))
                pu = phg.enter_context(nc.psum_tensor("fpu", [128, 512], F32))

                for half in range(2):
                    def _t(e, half=half):
                        if half == 1:
                            e.wait_ge(sV.h, sV.n)
                        for kc in range(KC):
                            e.matmul(pbc[:], onesM[:],
                                     xT[:, kc * T + half * 512:
                                        kc * T + half * 512 + 512],
                                     start=(kc == 0), stop=(kc == KC - 1))
                        e.drain()
                        e.nop().then_inc(sT.h, 1)
                        sT.n += 1
                    sec("tensor", _t)
                    t_sx = sT.n

                    def _v(e, half=half, t_sx=t_sx):
                        e.wait_ge(sT.h, t_sx)
                        e.tensor_scalar(out=S128[:, half * 512:
                                                half * 512 + 512],
                                        in0=pbc[:], scalar1=-128.0,
                                        scalar2=None, op0=ALU.mult)
                        e.drain()
                        e.nop().then_inc(sV.h, 1)
                        sV.n += 1
                    sec("vector", _v)
                v_s128 = sV.n

                w_free = [0, 0]
                wu_free = [0]
                gd = [0, 0]
                gdu = [0]
                for hc in range(HC):
                    par = hc % 2

                    def _g(e, hc=hc, par=par):
                        e.wait_ge(sT.h, w_free[par])
                        e.dma_start(
                            out=wg[:, par * 2048: par * 2048 + 2048
                                   ].rearrange("p (kc c) -> p kc c", c=128),
                            in_=wgT_in[:, hc * 128:(hc + 1) * 128
                                       ].rearrange("(kc p) c -> p kc c",
                                                   p=128)
                        ).then_inc(sF[par].h, 16)
                        sF[par].n += 16
                        gd[par] = sF[par].n
                        e.wait_ge(sT.h, wu_free[0])
                        e.dma_start(
                            out=wu[:].rearrange("p (kc c) -> p kc c", c=128),
                            in_=wuT_in[:, hc * 128:(hc + 1) * 128
                                       ].rearrange("(kc p) c -> p kc c",
                                                   p=128)
                        ).then_inc(sFu.h, 16)
                        sFu.n += 16
                        gdu[0] = sFu.n
                    sec("gpsimd", _g)

                    for half in range(2):
                        def _t(e, hc=hc, par=par, half=half):
                            e.wait_ge(sF[par].h, gd[par])
                            e.wait_ge(sV.h, v_s128 if (hc == 0 and half == 0)
                                      else sV.n)
                            for kc in range(KC):
                                e.matmul(pg[:],
                                         wg[:, par * 2048 + kc * 128:
                                            par * 2048 + kc * 128 + 128],
                                         xT[:, kc * T + half * 512:
                                            kc * T + half * 512 + 512],
                                         start=(kc == 0), stop=(kc == KC - 1))
                            e.drain()
                            e.nop().then_inc(sT.h, 1)
                            e.wait_ge(sFu.h, gdu[0])
                            for kc in range(KC):
                                e.matmul(pu[:],
                                         wu[:, kc * 128: kc * 128 + 128],
                                         xT[:, kc * T + half * 512:
                                            kc * T + half * 512 + 512],
                                         start=(kc == 0), stop=(kc == KC - 1))
                            e.drain()
                            e.nop().then_inc(sT.h, 1)
                            sT.n += 2
                            if half == 1:
                                w_free[par] = sT.n
                                wu_free[0] = sT.n
                        sec("tensor", _t)
                        t_gu = sT.n

                        def _v(e, half=half, t_gu=t_gu):
                            e.wait_ge(sT.h, t_gu - 1)
                            e.tensor_tensor(out=tgs[:], in0=pg[:],
                                            in1=S128[:, half * 512:
                                                     half * 512 + 512],
                                            op=ALU.add)
                            e.drain()
                            e.nop().then_inc(sV.h, 1)
                            sV.n += 1
                        sec("vector", _v)
                        v_tg = sV.n

                        def _s(e, hc=hc, v_tg=v_tg):
                            e.wait_ge(sV.h, v_tg)
                            e.activation(out=gsb[:], in_=tgs[:],
                                         func=ACT.Silu,
                                         scale=sg_sb[:, hc:hc + 1]
                                         ).then_inc(sS.h, 1)
                            e.drain()
                            sS.n += 1
                        sec("scalar", _s)
                        s_g = sS.n

                        def _v2(e, hc=hc, half=half, s_g=s_g, t_gu=t_gu):
                            e.wait_ge(sS.h, s_g)
                            e.wait_ge(sT.h, t_gu)
                            e.tensor_tensor(out=tgs[:], in0=pu[:],
                                            in1=S128[:, half * 512:
                                                     half * 512 + 512],
                                            op=ALU.add)
                            e.drain()
                            e.scalar_tensor_tensor(
                                out=hT[:, hc * T + half * 512:
                                       hc * T + half * 512 + 512],
                                in0=tgs[:], scalar=su_sb[:, hc:hc + 1],
                                in1=gsb[:], op0=ALU.mult, op1=ALU.mult)
                            e.drain()
                            e.nop().then_inc(sV.h, 1)
                            sV.n += 1
                        sec("vector", _v2)
                barrier()
                phg.close()

                mshb = lsb("mshb", [128, T], BF16)
                for half in range(2):
                    def _t(e, half=half):
                        if half == 1:
                            e.wait_ge(sV.h, sV.n)
                        for hc in range(HC):
                            e.matmul(pbc[:], onesM[:],
                                     hT[:, hc * T + half * 512:
                                        hc * T + half * 512 + 512],
                                     start=(hc == 0), stop=(hc == HC - 1))
                        e.drain()
                        e.nop().then_inc(sT.h, 1)
                        sT.n += 1
                    sec("tensor", _t)
                    t_sh = sT.n

                    def _v(e, half=half, t_sh=t_sh):
                        e.wait_ge(sT.h, t_sh)
                        e.tensor_scalar(out=mshb[:, half * 512:
                                                half * 512 + 512],
                                        in0=pbc[:], scalar1=-128.0,
                                        scalar2=None, op0=ALU.mult)
                        e.drain()
                        e.nop().then_inc(sV.h, 1)
                        sV.n += 1
                    sec("vector", _v)
                v_mshb = sV.n

                wd = lsb("wd", [128, 2 * HC * 128], BF16)
                tds = lsb("tds", [128, 512], F32)
                dfm = lsb("dfm", [128, 512], BF16)
                pd = [psb(f"pd{i}", [128, 512], F32) for i in range(2)]
                ptd = psb("ptd", [128, 512], BF16)
                wd_free = [0, 0]
                gdd = [0, 0]
                for oc in range(KC):
                    par = oc % 2

                    def _g(e, oc=oc, par=par):
                        e.wait_ge(sT.h, wd_free[par])
                        e.dma_start(
                            out=wd[:, par * HC * 128:
                                   par * HC * 128 + HC * 128
                                   ].rearrange("p (hc c) -> p hc c", c=128),
                            in_=wdT_in[:, oc * 128:(oc + 1) * 128
                                       ].rearrange("(hc p) c -> p hc c",
                                                   p=128)
                        ).then_inc(sF[par].h, 16)
                        sF[par].n += 16
                        gdd[par] = sF[par].n
                    sec("gpsimd", _g)

                    for half in range(2):
                        def _t(e, oc=oc, par=par, half=half):
                            e.wait_ge(sF[par].h, gdd[par])
                            e.wait_ge(sV.h, v_mshb if (oc == 0 and half == 0)
                                      else sV.n)
                            for hc in range(HC):
                                e.matmul(pd[half][:],
                                         wd[:, par * HC * 128 + hc * 128:
                                            par * HC * 128 + hc * 128 + 128],
                                         hT[:, hc * T + half * 512:
                                            hc * T + half * 512 + 512],
                                         start=(hc == 0), stop=(hc == HC - 1))
                            e.drain()
                            e.nop().then_inc(sT.h, 1)
                            sT.n += 1
                            if half == 1:
                                wd_free[par] = sT.n
                        sec("tensor", _t)
                        t_d = sT.n

                        def _v(e, oc=oc, half=half, t_d=t_d):
                            e.wait_ge(sT.h, t_d)
                            e.tensor_tensor(out=tds[:], in0=pd[half][:],
                                            in1=mshb[:, half * 512:
                                                     half * 512 + 512],
                                            op=ALU.add)
                            e.drain()
                            e.tensor_scalar(out=dfm[:], in0=tds[:],
                                            scalar1=sd_sb[:, oc:oc + 1],
                                            scalar2=None, op0=ALU.mult)
                            e.drain()
                            e.nop().then_inc(sV.h, 1)
                            sV.n += 1
                        sec("vector", _v)
                        v_d = sV.n

                        def _t2(e, v_d=v_d):
                            e.wait_ge(sV.h, v_d)
                            for j in range(4):
                                e.transpose(ptd[:, j * 128: j * 128 + 128],
                                            dfm[:, j * 128: j * 128 + 128],
                                            ident_sb[:])
                            e.drain()
                            e.nop().then_inc(sT.h, 1)
                            sT.n += 1
                        sec("tensor", _t2)
                        t_tr = sT.n

                        def _v2(e, oc=oc, half=half, t_tr=t_tr):
                            e.wait_ge(sT.h, t_tr)
                            for j in range(4):
                                lt = half * 4 + j
                                e.tensor_tensor(
                                    out=x_sb[:, lt * D + oc * 128:
                                             lt * D + oc * 128 + 128],
                                    in0=x_sb[:, lt * D + oc * 128:
                                             lt * D + oc * 128 + 128],
                                    in1=ptd[:, j * 128: j * 128 + 128],
                                    op=ALU.add)
                            e.drain()
                            e.nop().then_inc(sV.h, 1)
                            sV.n += 1
                        sec("vector", _v2)

        def emit_lat(xT):
            ph = contextlib.ExitStack()
            with ph:
                lsb = lambda n, s, d: ph.enter_context(
                    nc.sbuf_tensor("l" + n, s, d))
                psb = lambda n, s, d: ph.enter_context(
                    nc.psum_tensor("l" + n, s, d))
                Wa = lsb("Wa", [128, KC * DL], BF16)
                lat = lsb("lat", [128, 4 * T], BF16)
                nwtm = lsb("nwtm", [128, 512], F32)
                ssl = lsb("ssl", [128, 1], F32)
                sdl = lsb("sdl", [128, 1], F32)
                ril = lsb("ril", [128, 1], F32)
                sql = lsb("sql", [128, 512], BF16)
                ltc = lsb("ltc", [128, 512], BF16)
                lnt = lsb("lnt", [128, 512], BF16)
                latn = lsb("latn", [128, 4 * T], BF16)
                pl = [psb(f"pl{i}", [128, 512], F32) for i in range(2)]
                ptl = psb("ptl", [128, 512], BF16)
                plf = psb("plf", [128, 512], BF16)
                v_entry = sV.n

                def _g(e):
                    e.dma_start(
                        out=Wa[:].rearrange("p (k o) -> p k o", o=DL),
                        in_=wkvaT_in.rearrange("(k p) o -> p k o", p=128)
                    ).then_inc(sG.h, 16)
                    sG.n += 16
                sec("gpsimd", _g)
                g_wa = sG.n

                def _w0(e):
                    e.dma_start(out=nwtm[:], in_=normwtm_in).then_inc(sW.h, 16)
                    sW.n += 16
                sec("sync", _w0)
                w_nw = sW.n

                for lc in range(4):
                    for half in range(2):
                        def _t(e, lc=lc, half=half):
                            e.wait_ge(sG.h, g_wa)
                            e.wait_ge(sV.h, v_entry if (lc == 0 and half == 0)
                                      else sV.n)
                            for kc in range(KC):
                                e.matmul(pl[half][:],
                                         Wa[:, kc * DL + lc * 128:
                                            kc * DL + lc * 128 + 128],
                                         xT[:, kc * T + half * 512:
                                            kc * T + half * 512 + 512],
                                         start=(kc == 0), stop=(kc == KC - 1))
                            e.drain()
                            e.nop().then_inc(sT.h, 1)
                            sT.n += 1
                        sec("tensor", _t)
                        t_l = sT.n

                        def _v(e, lc=lc, half=half, t_l=t_l):
                            e.wait_ge(sT.h, t_l)
                            e.tensor_copy(out=lat[:, lc * T + half * 512:
                                                  lc * T + half * 512 + 512],
                                          in_=pl[half][:])
                            e.drain()
                            e.nop().then_inc(sV.h, 1)
                            sV.n += 1
                        sec("vector", _v)
                v_lat = sV.n

                for t in range(NT):
                    def _t(e, t=t):
                        e.wait_ge(sV.h, v_lat if t == 0 else sV.n)
                        for lc in range(4):
                            e.transpose(ptl[:, lc * 128: lc * 128 + 128],
                                        lat[:, lc * T + t * 128:
                                            lc * T + t * 128 + 128],
                                        ident_sb[:])
                        e.drain()
                        e.nop().then_inc(sT.h, 1)
                        sT.n += 1
                    sec("tensor", _t)
                    t_tm = sT.n

                    def _v(e, t=t, t_tm=t_tm):
                        e.wait_ge(sT.h, t_tm)
                        e.wait_ge(sW.h, w_nw)
                        e.tensor_copy(out=ltc[:], in_=ptl[:])
                        e.drain()
                        e.tensor_tensor(out=sql[:], in0=ltc[:], in1=ltc[:],
                                        op=ALU.mult)
                        e.drain()
                        e.reduce_sum(out=ssl[:], in_=sql[:], axis=AX.X)
                        e.drain()
                        e.nop().then_inc(sV.h, 1)
                        sV.n += 1
                    sec("vector", _v)
                    v_ss = sV.n

                    def _s(e, v_ss=v_ss):
                        e.wait_ge(sV.h, v_ss)
                        e.activation(out=sdl[:], in_=ssl[:], func=ACT.Sqrt,
                                     scale=1.0 / DL, bias=eps_sb[:, 1:2]
                                     ).then_inc(sS.h, 1)
                        e.drain()
                        sS.n += 1
                    sec("scalar", _s)
                    s_d = sS.n

                    def _v2(e, s_d=s_d):
                        e.wait_ge(sS.h, s_d)
                        e.reciprocal(out=ril[:], in_=sdl[:])
                        e.drain()
                        e.tensor_scalar(out=lnt[:], in0=ptl[:],
                                        scalar1=ril[:, 0:1], scalar2=None,
                                        op0=ALU.mult)
                        e.drain()
                        e.tensor_tensor(out=lnt[:], in0=lnt[:], in1=nwtm[:],
                                        op=ALU.mult)
                        e.drain()
                        e.nop().then_inc(sV.h, 1)
                        sV.n += 1
                    sec("vector", _v2)
                    v_ln = sV.n

                    def _t2(e, v_ln=v_ln):
                        e.wait_ge(sV.h, v_ln)
                        for lc in range(4):
                            e.transpose(plf[:, lc * 128: lc * 128 + 128],
                                        lnt[:, lc * 128: lc * 128 + 128],
                                        ident_sb[:])
                        e.drain()
                        e.nop().then_inc(sT.h, 1)
                        sT.n += 1
                    sec("tensor", _t2)
                    t_fm = sT.n

                    def _v3(e, t=t, t_fm=t_fm):
                        e.wait_ge(sT.h, t_fm)
                        for lc in range(4):
                            e.tensor_copy(
                                out=latn[:, lc * T + t * 128:
                                         lc * T + t * 128 + 128],
                                in_=plf[:, lc * 128: lc * 128 + 128])
                        e.drain()
                        e.nop().then_inc(sV.h, 1)
                        sV.n += 1
                    sec("vector", _v3)
                v_latn = sV.n

                def _w(e, v_latn=v_latn):
                    e.wait_ge(sV.h, v_latn)
                    for lc in range(4):
                        e.dma_start(out=agl_in[lc * 128:(lc + 1) * 128, :],
                                    in_=latn[:, lc * T:(lc + 1) * T]
                                    ).then_inc(sW.h, 16)
                        sW.n += 16
                sec("sync", _w)

        def emit_moe():
            ph = contextlib.ExitStack()
            with ph:
                lsb = lambda n, s, d: ph.enter_context(
                    nc.sbuf_tensor("m" + n, s, d))
                psb = lambda n, s, d: ph.enter_context(
                    nc.psum_tensor("m" + n, s, d))
                xT = lsb("xT", [128, KC * T], BF16)
                emit_ln("L4", None, xT)
                barrier()
                WA = lsb("WA", [128, KC * 520], BF16)
                BW = lsb("BW", [128, 4 * D], BF16)
                lgs = lsb("lgs", [128, 8], F32)
                hsf = lsb("hsf", [128, 512], F32)
                nmx = lsb("nmx", [128, 1], F32)
                mxp = lsb("mxp", [128, 1], F32)
                expv = lsb("expv", [128, 8], F32)
                sme = lsb("sme", [128, 1], F32)
                rcp = lsb("rcp", [128, 1], F32)
                ps4 = lsb("ps4", [128, 1], F32)
                msk = lsb("msk", [128, 8], F32)
                hh2 = lsb("hh2", [128, 512], BF16)
                hmT = lsb("hmT", [128, 4 * T], BF16)
                plh = [psb(f"plh{i}", [128, 520], F32) for i in range(2)]
                pht = psb("pht", [128, 512], BF16)
                pyy = [psb(f"pyy{i}", [128, 512], F32) for i in range(2)]

                def _g(e):
                    e.dma_start(
                        out=WA[:].rearrange("p (k o) -> p k o", o=520),
                        in_=wgA_in.rearrange("(k p) o -> p k o", p=128)
                    ).then_inc(sG.h, 16)
                    e.dma_start(
                        out=BW[:].rearrange("p (k o) -> p k o", o=D),
                        in_=bwT_in.rearrange("(k p) o -> p k o", p=128)
                    ).then_inc(sG.h, 16)
                    sG.n += 32
                sec("gpsimd", _g)
                g_w = sG.n

                evd = [0, 0]
                for t in range(NT):
                    par = t % 2

                    def _t(e, t=t, par=par):
                        if t == 0:
                            e.wait_ge(sG.h, g_w)
                        e.wait_ge(sV.h, evd[par])
                        for kc in range(KC):
                            e.matmul(plh[par][:, 0:512],
                                     xT[:, kc * T + t * 128:
                                        kc * T + t * 128 + 128],
                                     WA[:, kc * 520: kc * 520 + 512],
                                     start=(kc == 0), stop=(kc == KC - 1))
                        for kc in range(KC):
                            e.matmul(plh[par][:, 512:520],
                                     xT[:, kc * T + t * 128:
                                        kc * T + t * 128 + 128],
                                     WA[:, kc * 520 + 512: kc * 520 + 520],
                                     start=(kc == 0), stop=(kc == KC - 1))
                        e.drain()
                        e.nop().then_inc(sT.h, 1)
                        sT.n += 1
                    sec("tensor", _t)
                    t_lh = sT.n

                    def _v(e, par=par, t_lh=t_lh):
                        e.wait_ge(sT.h, t_lh)
                        e.tensor_copy(out=lgs[:], in_=plh[par][:, 512:520])
                        e.tensor_copy(out=hsf[:], in_=plh[par][:, 0:512])
                        e.drain()
                        e.reduce_max(out=nmx[:], in_=lgs[:], axis=AX.X,
                                     negate=True)
                        e.drain()
                        e.tensor_scalar(out=mxp[:], in0=nmx[:], scalar1=-1.0,
                                        scalar2=None, op0=ALU.mult)
                        e.drain()
                        e.tensor_scalar(out=msk[:], in0=lgs[:],
                                        scalar1=mxp[:, 0:1], scalar2=None,
                                        op0=ALU.is_ge)
                        e.drain()
                        e.nop().then_inc(sV.h, 1)
                        sV.n += 1
                    sec("vector", _v)
                    v_sm = sV.n

                    def _s(e, v_sm=v_sm):
                        e.wait_ge(sV.h, v_sm)
                        e.activation(out=expv[:], in_=lgs[:], func=ACT.Exp,
                                     bias=nmx[:, 0:1],
                                     accum_out=sme[:, 0:1]).then_inc(sS.h, 1)
                        e.drain()
                        sS.n += 1
                    sec("scalar", _s)
                    s_e = sS.n

                    def _v2(e, par=par, s_e=s_e):
                        e.wait_ge(sS.h, s_e)
                        e.reciprocal(out=rcp[:], in_=sme[:])
                        e.drain()
                        e.tensor_scalar(out=ps4[:], in0=rcp[:], scalar1=4.0,
                                        scalar2=None, op0=ALU.mult)
                        e.drain()
                        for ex in range(8):
                            e.tensor_scalar(out=hh2[:, ex * 64: ex * 64 + 64],
                                            in0=hsf[:, ex * 64: ex * 64 + 64],
                                            scalar1=msk[:, ex:ex + 1],
                                            scalar2=ps4[:, 0:1],
                                            op0=ALU.mult, op1=ALU.mult)
                        e.drain()
                        e.nop().then_inc(sV.h, 1)
                        sV.n += 1
                        evd[par] = sV.n
                    sec("vector", _v2)
                    v_h = sV.n

                    def _t2(e, v_h=v_h):
                        e.wait_ge(sV.h, v_h)
                        for j in range(4):
                            e.transpose(pht[:, j * 128: j * 128 + 128],
                                        hh2[:, j * 128: j * 128 + 128],
                                        ident_sb[:])
                        e.drain()
                        e.nop().then_inc(sT.h, 1)
                        sT.n += 1
                    sec("tensor", _t2)
                    t_ht = sT.n

                    def _v3(e, t=t, t_ht=t_ht):
                        e.wait_ge(sT.h, t_ht)
                        for j in range(4):
                            e.tensor_copy(out=hmT[:, j * T + t * 128:
                                                  j * T + t * 128 + 128],
                                          in_=pht[:, j * 128: j * 128 + 128])
                        e.drain()
                        e.nop().then_inc(sV.h, 1)
                        sV.n += 1
                    sec("vector", _v3)
                v_hmT = sV.n

                for t in range(NT):
                    for grp in range(4):
                        gpar = grp % 2

                        def _t(e, t=t, grp=grp, gpar=gpar):
                            if t == 0 and grp == 0:
                                e.wait_ge(sV.h, v_hmT)
                            if not (t == 0 and grp < 2):
                                e.wait_ge(sV.h, sV.n)
                            for rc in range(4):
                                e.matmul(pyy[gpar][:],
                                         hmT[:, rc * T + t * 128:
                                             rc * T + t * 128 + 128],
                                         BW[:, rc * D + grp * 512:
                                            rc * D + grp * 512 + 512],
                                         start=(rc == 0), stop=(rc == 3))
                            e.drain()
                            e.nop().then_inc(sT.h, 1)
                            sT.n += 1
                        sec("tensor", _t)
                        t_y = sT.n

                        def _v(e, t=t, grp=grp, gpar=gpar, t_y=t_y):
                            e.wait_ge(sT.h, t_y)
                            e.tensor_tensor(
                                out=x_sb[:, t * D + grp * 512:
                                         t * D + grp * 512 + 512],
                                in0=x_sb[:, t * D + grp * 512:
                                         t * D + grp * 512 + 512],
                                in1=pyy[gpar][:], op=ALU.add)
                            e.drain()
                            e.nop().then_inc(sV.h, 1)
                            sV.n += 1
                        sec("vector", _v)

        _ORDER = ["init", "ln1", "qkv0", "attn0", "res0", "ffn",
                  "ln3", "qkv1", "attn1", "res1", "moe", "all"]

        def want(name):
            return _ORDER.index(name) <= _ORDER.index(stop_after)

        if want("ln1"):
            emit_ln("L1", ag_in_d[0], None)
            emit_collective("AllGather", ALU.bypass, ag_in_d[0], ag_out_d[0])
            barrier()
        if want("qkv0"):
            emit_qkv(0)
            barrier()
        if want("attn0"):
            emit_attn(0)
            barrier()
        if want("res0"):
            emit_residual(0, add_bias=False)
            barrier()
        if want("ffn"):
            emit_ffn()
            barrier()

        ph_l1 = contextlib.ExitStack()
        with ph_l1:
            if want("ln3"):
                xT3 = ph_l1.enter_context(
                    nc.sbuf_tensor("xT3", [128, KC * T], BF16))
                emit_ln("L3", ag_in_d[1], xT3)
                emit_collective("AllGather", ALU.bypass, ag_in_d[1],
                                ag_out_d[1])
                emit_lat(xT3)
                emit_collective("AllGather", ALU.bypass, agl_in, agl_out)
                barrier()
            if want("qkv1"):
                emit_qkv(1)
                barrier()
        if want("attn1"):
            emit_attn(1)
            barrier()
        if want("res1"):
            emit_residual(1, add_bias=True)
            barrier()
        if want("moe"):
            emit_moe()
            barrier()

        def _wout(e):
            e.dma_start(out=out_t.rearrange("(t p) d -> p t d", p=128),
                        in_=x_sb[:].rearrange("p (t d) -> p t d", d=D)
                        ).then_inc(sW.h, 16)
            sW.n += 16
            e.wait_ge(sW.h, sW.n)
        sec("sync", _wout)

    return nc


def _host_constants():
    pos = np.arange(S, dtype=np.float32)
    j64 = np.arange(64, dtype=np.float32)
    inv = (np.float32(10000.0) ** (-j64 / np.float32(64.0))).astype(np.float32)
    theta = (pos[None, :] * inv[:, None]).astype(np.float32)
    cos = np.cos(theta).astype(np.float32)
    sin = np.sin(theta).astype(np.float32)
    c0tmh = np.zeros((128, 2048), dtype=np.float32)
    s0tmh = np.zeros((128, 2048), dtype=np.float32)
    for pb in range(32):
        c0tmh[:, pb * 64:(pb + 1) * 64] = cos[:, pb * 128:(pb + 1) * 128].T
        s0tmh[:, pb * 64:(pb + 1) * 64] = sin[:, pb * 128:(pb + 1) * 128].T
    ident = np.eye(128, dtype=np.float32)
    onesM = np.ones((128, 128), dtype=np.float32)
    onescol = np.ones((128, 64), dtype=np.float32)
    epsv = np.broadcast_to(
        np.array([1e-5, 1e-6, -C_BOUND, 0.0], np.float32)[None, :],
        (128, 4)).copy()
    maskT = np.zeros((128, 4 * 512), dtype=np.float32)
    qpos = np.arange(512)
    for m in range(4):
        maskT[:, m * 512:(m + 1) * 512] = (
            qpos[None, :] >= (m * 128 + np.arange(128))[:, None]
        ).astype(np.float32)
    return c0tmh, s0tmh, ident, onesM, onescol, epsv, maskT


def kernel(L0_attn_x, L0_attn_w_qkv, L0_attn_w_out, L0_attn_q_gain,
           L0_ffn_wq_gate, L0_ffn_s_gate, L0_ffn_wq_up, L0_ffn_s_up,
           L0_ffn_wq_down, L0_ffn_s_down, L1_attn_freqs, L1_attn_wq_w,
           L1_attn_wq_b, L1_attn_wkv_a_w, L1_attn_wkv_b_w, L1_attn_norm_w,
           L1_attn_out_w, L1_attn_out_b, L1_ffn_A, L1_ffn_Bw, L1_ffn_Wg):
    def hnp(a, dt):
        a = np.asarray(a)
        if a.dtype != dt:
            a = a.astype(dt)
        return a

    c0, s0, ident, onesM, onescol, epsv, maskT = _host_constants()
    x_full = np.ascontiguousarray(
        hnp(L0_attn_x, np.float32).reshape(8 * T, D))
    wqkv = hnp(L0_attn_w_qkv, np.float32)
    wout0 = hnp(L0_attn_w_out, np.float32)
    qgain = hnp(L0_attn_q_gain, np.float32)
    wgTh = np.ascontiguousarray(hnp(L0_ffn_wq_gate, np.int32).T.astype(np.uint8))
    wuTh = np.ascontiguousarray(hnp(L0_ffn_wq_up, np.int32).T.astype(np.uint8))
    wdTh = np.ascontiguousarray(hnp(L0_ffn_wq_down, np.int32).T.astype(np.uint8))
    sg = hnp(L0_ffn_s_gate, np.float32)
    su = hnp(L0_ffn_s_up, np.float32)
    sd = hnp(L0_ffn_s_down, np.float32)
    freqs = hnp(L1_attn_freqs, np.float32)
    wq1 = hnp(L1_attn_wq_w, np.float32)
    wqb = hnp(L1_attn_wq_b, np.float32)
    wkva = hnp(L1_attn_wkv_a_w, np.float32)
    wkvb = hnp(L1_attn_wkv_b_w, np.float32)
    normw = hnp(L1_attn_norm_w, np.float32)
    wout1 = hnp(L1_attn_out_w, np.float32)
    outb = hnp(L1_attn_out_b, np.float32)
    A = hnp(L1_ffn_A, np.float32)
    Bw = hnp(L1_ffn_Bw, np.float32)
    Wg = hnp(L1_ffn_Wg, np.float32)

    sg_pp = np.ascontiguousarray(sg.reshape(HC, 128).T)
    su_pp = np.ascontiguousarray(su.reshape(HC, 128).T)
    sd_pp = np.ascontiguousarray(sd.reshape(KC, 128).T)
    normwtm = np.ascontiguousarray(
        np.broadcast_to(normw[None, :], (128, 512)))
    wkvaT = np.ascontiguousarray(wkva.T)
    wgA = np.ascontiguousarray(
        np.concatenate([A.reshape(512, D).T, Wg.T], axis=1))
    bwT = np.ascontiguousarray(Bw.transpose(0, 2, 1).reshape(512, D))
    outb_bc = np.ascontiguousarray(np.broadcast_to(outb[None, :], (128, D)))

    in_maps = []
    for c in range(8):
        h0 = 2 * c
        cols = []
        for t3 in range(3):
            for hh in range(2):
                base = t3 * D + (h0 + hh) * 128
                cols.append(wqkv[:, base:base + 128])
        wqkv_myc = np.ascontiguousarray(np.concatenate(cols, axis=1))
        wout0_myc = np.ascontiguousarray(wout0[256 * c:256 * c + 256, :])
        gain2 = np.ascontiguousarray(
            np.broadcast_to(qgain[h0:h0 + 2][None, :], (128, 2)))
        wq1_myc = np.ascontiguousarray(wq1[256 * c:256 * c + 256, :].T)
        wqbtm = np.zeros((128, 1024), dtype=np.float32)
        for hh in range(2):
            bvec = wqb[(h0 + hh) * 128:(h0 + hh + 1) * 128]
            wqbtm[:, hh * 512: hh * 512 + 512] = np.broadcast_to(
                np.tile(bvec, 4)[None, :], (128, 512))
        krows = wkvb[h0 * 128:(h0 + 2) * 128, :]
        vrows = wkvb[2048 + h0 * 128: 2048 + (h0 + 2) * 128, :]
        wkvb_myc = np.ascontiguousarray(
            np.concatenate([krows, vrows], axis=0).T)
        wout1_myc = np.ascontiguousarray(wout1[:, 256 * c:256 * c + 256].T)
        c2tm = np.zeros((128, 256), dtype=np.float32)
        s2tm = np.zeros((128, 256), dtype=np.float32)
        for hh in range(2):
            h = h0 + hh
            crow = freqs[h, :, 0]
            srow = freqs[h, :, 1]
            c2tm[:, hh * 128: hh * 128 + 128] = np.broadcast_to(
                np.tile(crow, 4)[None, :], (128, 128))
            s2tm[:, hh * 128: hh * 128 + 128] = np.broadcast_to(
                np.tile(srow, 4)[None, :], (128, 128))
        in_maps.append(dict(
            x=np.ascontiguousarray(x_full[c * T:(c + 1) * T]),
            wqkv_my=wqkv_myc, wout0_my=wout0_myc, gain2=gain2,
            c0tm=c0, s0tm=s0, ident=ident, onesM=onesM, onescol=onescol,
            epsv=epsv, maskT=maskT, wgT=wgTh, wuT=wuTh, wdT=wdTh,
            sg_pp=sg_pp, su_pp=su_pp, sd_pp=sd_pp,
            wq1_my=wq1_myc, wqbtm=wqbtm, wkvaT=wkvaT, normwtm=normwtm,
            wkvb_my=wkvb_myc, wout1_my=wout1_myc, outb_bc=outb_bc,
            c2tm=c2tm, s2tm=s2tm, wgA=wgA, bwT=bwT))

    nc = build_nc()
    res = run_bass_kernel_spmd(nc, in_maps, core_ids=list(range(8)))
    out = np.concatenate([np.asarray(res.results[c]["out"])
                          for c in range(8)], axis=0)
    return out.reshape(B, S, D).astype(np.float32)


if __name__ == "__main__":
    build_nc()
    print("build ok")
